# version 01
# baseline (speedup 1.0000x reference)
import contextlib
import numpy as np
import ml_dtypes
import concourse.bass as bass
import concourse.mybir as mybir
from concourse.bass_utils import run_bass_kernel_spmd

F32 = mybir.dt.float32
BF16 = mybir.dt.bfloat16
AF = mybir.ActivationFunctionType
ALU = mybir.AluOpType
AX = mybir.AxisListType.X

T = 1024
TC = 8
H = 2048
HCN = 16
FF = 1408
FCN = 11
NE = 8
BIG = 1.0e4
EPS_LN = 1.0e-5
EPS_AT = 1.0e-15
STAGES = 3
ATTN_UPTO = 9
DEBUG_DUMP = False
DEBUG_SERIAL_OUT = False
DEBUG_FLUSH_ACT = False
PATHS = ["sh"] + list(range(NE))


class Sem:
    def __init__(self, handle):
        self.h = handle
        self.n = 0

    def inc(self, instr, k=1):
        instr.then_inc(self.h, k)
        self.n += k
        return self.n


def build_program():
    nc = bass.Bass("TRN2", target_bir_lowering=False, debug=False)

    x_in = nc.dram_tensor("x", [T, H], F32, kind="ExternalInput").ap()
    out = nc.dram_tensor("out", [T, H], F32, kind="ExternalOutput").ap()
    wqkv_s = nc.dram_tensor("wqkv_s", [48 * H, 128], F32, kind="ExternalInput").ap()
    w_proj = nc.dram_tensor("w_proj", [H, H], F32, kind="ExternalInput").ap()
    b_proj = nc.dram_tensor("b_proj", [H], F32, kind="ExternalInput").ap()
    gate_wt = nc.dram_tensor("gate_wt", [H, NE], F32, kind="ExternalInput").ap()
    g_bias = nc.dram_tensor("g_bias", [NE], F32, kind="ExternalInput").ap()
    ws1_s = nc.dram_tensor("ws1_s", [FCN * H, 128], F32, kind="ExternalInput").ap()
    ws2 = nc.dram_tensor("ws2", [H, FF], F32, kind="ExternalInput").ap()
    ws3_s = nc.dram_tensor("ws3_s", [FCN * H, 128], F32, kind="ExternalInput").ap()
    we1_s = nc.dram_tensor("we1_s", [NE * FCN * H, 128], F32, kind="ExternalInput").ap()
    we2 = nc.dram_tensor("we2", [NE, H, FF], F32, kind="ExternalInput").ap()
    we3_s = nc.dram_tensor("we3_s", [NE * FCN * H, 128], F32, kind="ExternalInput").ap()

    ws1_sw = nc.dram_tensor("ws1_sw", [FCN * H, 128], BF16).ap()
    ws3_sw = nc.dram_tensor("ws3_sw", [FCN * H, 128], BF16).ap()
    we1_sw = nc.dram_tensor("we1_sw", [NE * FCN * H, 128], BF16).ap()
    we3_sw = nc.dram_tensor("we3_sw", [NE * FCN * H, 128], BF16).ap()
    wqkv_sw = nc.dram_tensor("wqkv_sw", [48 * H, 128], BF16).ap()
    ws2_bf = nc.dram_tensor("ws2_bf", [H, FF], BF16).ap()
    we2_bf = nc.dram_tensor("we2_bf", [NE, H, FF], BF16).ap()
    wproj_bf = nc.dram_tensor("wproj_bf", [H, H], BF16).ap()
    xln_dram = nc.dram_tensor("xln_dram", [T, H], BF16).ap()
    kv_dram = nc.dram_tensor("kv_dram", [2 * H, T], BF16).ap()
    cc_in = nc.dram_tensor("cc_in", [128, 528], F32)
    cc_out = nc.dram_tensor("cc_out", [128, 528], F32)
    if DEBUG_DUMP:
        dbg_actK = nc.dram_tensor("dbg_actK", [128, HCN * T], BF16, kind="ExternalOutput").ap()
        dbg_bufV = nc.dram_tensor("dbg_bufV", [128, HCN * T], BF16, kind="ExternalOutput").ap()
        dbg_call = nc.dram_tensor("dbg_call", [128, TC * NE], F32, kind="ExternalOutput").ap()
        dbg_xlnT = nc.dram_tensor("dbg_xlnT", [128, HCN * T], BF16, kind="ExternalOutput").ap()

    ident_f = nc.inline_tensor(np.eye(128, dtype=np.float32), "ident_f").ap()
    oh_np = np.zeros((NE, NE * 128), dtype=np.float32)
    for e in range(NE):
        oh_np[e, e * 128:(e + 1) * 128] = 1.0
    onehot_f = nc.inline_tensor(oh_np, "onehot_f").ap()
    ones_col_bf = nc.inline_tensor(np.ones((128, 1), dtype=ml_dtypes.bfloat16), "c_oc_bf").ap()
    ones_row_bf = nc.inline_tensor(np.ones((1, 128), dtype=ml_dtypes.bfloat16), "c_or_bf").ap()
    ones_row_f = nc.inline_tensor(np.ones((1, 128), dtype=np.float32), "c_or_f").ap()
    iota_np = np.tile(np.arange(NE, dtype=np.float32)[None, :], (128, 1))
    iota_c = nc.inline_tensor(iota_np, "iota_c").ap()
    iopb_c = nc.inline_tensor(iota_np + BIG, "iopb_c").ap()
    epsln_c = nc.inline_tensor(np.full((128, 1), EPS_LN, dtype=np.float32), "epsln_c").ap()

    with contextlib.ExitStack() as st:
        def sbt(name, shape, dtype):
            return st.enter_context(nc.sbuf_tensor(name, shape, dtype))

        xres = sbt("xres", [128, TC * H], F32)
        xlnT = sbt("xlnT", [128, HCN * T], BF16)
        bufK = sbt("bufK", [128, HCN * T], BF16)
        bufV = sbt("bufV", [128, HCN * T], BF16)
        wbufA = sbt("wbufA", [128, 2 * H], BF16)
        wbufB = sbt("wbufB", [128, 2 * H], BF16)
        wbufC = sbt("wbufC", [128, FCN * 1024], BF16)
        gateT = sbt("gateT", [128, HCN * NE], BF16)
        id_sb = sbt("id_sb", [128, 128], F32)
        oh_sb = sbt("oh_sb", [NE, NE * 128], F32)
        oc_bf = sbt("oc_bf", [128, 1], BF16)
        or_bf = sbt("or_bf", [1, 128], BF16)
        or_f = sbt("or_f", [1, 128], F32)
        iota_sb = sbt("iota_sb", [128, NE], F32)
        iopb_sb = sbt("iopb_sb", [128, NE], F32)
        bias_sb = sbt("bias_sb", [1, NE], F32)
        biasb = sbt("biasb", [128, NE], F32)
        stats = sbt("stats", [128, 24], F32)
        mvall = sbt("mvall", [128, 2 * TC], F32)
        negmean = sbt("negmean", [128, TC], F32)
        sdall = sbt("sdall", [128, TC], F32)
        rstd = sbt("rstd", [128, TC], F32)
        scores = sbt("scores", [128, NE], F32)
        routing = sbt("routing", [128, NE], F32)
        eqt = sbt("eqt", [128, NE], F32)
        cand = sbt("cand", [128, NE], F32)
        idx1 = sbt("idx1", [128, 1], F32)
        oh1 = sbt("oh1", [128, NE], F32)
        oh2 = sbt("oh2", [128, NE], F32)
        junk8 = sbt("junk8", [128, NE], F32)
        r2t = sbt("r2t", [128, NE], F32)
        sel1 = sbt("sel1", [128, 1], F32)
        sel2 = sbt("sel2", [128, 1], F32)
        gsum = sbt("gsum", [128, 1], F32)
        ginv = sbt("ginv", [128, 1], F32)
        g1 = sbt("g1", [128, 1], F32)
        g2 = sbt("g2", [128, 1], F32)
        c_all = sbt("c_all", [128, TC * NE], F32)
        dnm = sbt("dnm", [128, NE], F32)
        epsln = sbt("epsln", [128, 1], F32)
        rcp = sbt("rcp", [128, NE], F32)

        xln_tm = [bufV[:, j * H:(j + 1) * H] for j in range(TC)]
        tmp_silu = bufV[:, 2 * H:3 * H].bitcast(F32)
        cT = bufV[0:NE, 3 * H:4 * H].bitcast(F32)
        kv_stage = [wbufC[:, 8192 + i * 1024:8192 + (i + 1) * 1024] for i in range(3)]
        bprojT = wbufB[0:1, 0:H]
        vkT_sb = wbufB[:, H:H + 1056].bitcast(F32)
        vkT_bf = wbufB[:, H + 1056:H + 1584]

        ph1 = st.enter_context(nc.psum_tensor("ph1", [128, 1024], F32))
        ph3 = st.enter_context(nc.psum_tensor("ph3", [128, 1024], F32))
        po = st.enter_context(nc.psum_tensor("po", [128, 1024], F32))
        pu = st.enter_context(nc.psum_tensor("pu", [128, 1024], F32))
        pbank = [ph1[:, 0:512], ph1[:, 512:1024], ph3[:, 0:512], ph3[:, 512:1024]]

        sC = Sem(st.enter_context(nc.semaphore("sC")))
        sB = Sem(st.enter_context(nc.semaphore("sB")))
        sT = Sem(st.enter_context(nc.semaphore("sT")))
        sPE = Sem(st.enter_context(nc.semaphore("sPE")))
        sDV = Sem(st.enter_context(nc.semaphore("sDV")))
        sAC = Sem(st.enter_context(nc.semaphore("sAC")))
        sCC = Sem(st.enter_context(nc.semaphore("sCC")))
        sWa = Sem(st.enter_context(nc.semaphore("sWa")))
        sWb = Sem(st.enter_context(nc.semaphore("sWb")))
        sWc = Sem(st.enter_context(nc.semaphore("sWc")))

        block = st.enter_context(nc.Block())

        M = {}

        def _(sy):
            for dst, src in ((id_sb[:, :], ident_f), (oh_sb[:, :], onehot_f),
                             (oc_bf[:, :], ones_col_bf), (or_bf[:, :], ones_row_bf),
                             (or_f[:, :], ones_row_f), (iota_sb[:, :], iota_c),
                             (iopb_sb[:, :], iopb_c), (epsln[:, :], epsln_c),
                             (bias_sb[:, :], g_bias.rearrange("e -> () e"))):
                sT.inc(sy.dma_start(out=dst, in_=src), 16)
            sT.inc(sy.dma_start(out=xres[:, :].rearrange("p (j h) -> p j h", h=H),
                                in_=x_in.rearrange("(j p) h -> p j h", p=128)), 16)
        block.sync(_)
        M["consts"] = sT.n

        cast = {}

        def _(g):
            sC.inc(g.dma_start(
                out=gateT[:, :].rearrange("p (hc e) -> p hc e", e=NE),
                in_=gate_wt.rearrange("(hc p) e -> p hc e", p=128)), 16)
            cast["gate"] = sC.n
            sC.inc(g.dma_start(out=ws1_sw, in_=ws1_s), 16)
            cast["ws1"] = sC.n
            sC.inc(g.dma_start(out=ws3_sw, in_=ws3_s), 16)
            cast["ws3"] = sC.n
            sC.inc(g.dma_start(out=ws2_bf, in_=ws2), 16)
            cast["ws2"] = sC.n
            for e in range(NE):
                sC.inc(g.dma_start(out=we1_sw[e * FCN * H:(e + 1) * FCN * H, :],
                                   in_=we1_s[e * FCN * H:(e + 1) * FCN * H, :]), 16)
                cast[("we1", e)] = sC.n
                sC.inc(g.dma_start(out=we3_sw[e * FCN * H:(e + 1) * FCN * H, :],
                                   in_=we3_s[e * FCN * H:(e + 1) * FCN * H, :]), 16)
                cast[("we3", e)] = sC.n
                sC.inc(g.dma_start(out=we2_bf[e, :, :], in_=we2[e, :, :]), 16)
                cast[("we2", e)] = sC.n
            if STAGES >= 2:
                for cb in range(6):
                    sC.inc(g.dma_start(out=wqkv_sw[cb * 8 * H:(cb + 1) * 8 * H, :],
                                       in_=wqkv_s[cb * 8 * H:(cb + 1) * 8 * H, :]), 16)
                    cast[("wqkvb", cb)] = sC.n
                sC.inc(g.dma_start(out=wproj_bf, in_=w_proj), 16)
                cast["wproj"] = sC.n
        block.gpsimd(_)


        def _(t):
            t.wait_ge(sT.h, M["consts"])
            sPE.inc(t.matmul(pu[:, 0:NE], or_f[:, :], bias_sb[:, :], start=True, stop=True))
        block.tensor(_)
        M["biasmm"] = sPE.n

        def _(v):
            v.wait_ge(sPE.h, M["biasmm"])
            sDV.inc(v.tensor_copy(biasb[:, :], pu[:, 0:NE]))
        block.vector(_)

        M["pu_free"] = [(sDV, sDV.n)]
        M["po_free"] = []
        M["pb_free"] = [[] for _ in range(4)]
        M["bufK_free"] = []
        M["bufV_free"] = []
        M["xlnT_free"] = []
        M["xres_w"] = []
        M["wA_extra"] = []
        M["wB_extra"] = []
        M["wC_extra"] = []
        wa_idx = [0, 0]
        wb_idx = [0, 0]
        wc_idx = [0, 0]

        def wt(eng, mks):
            for s, n in mks:
                if n > 0:
                    eng.wait_ge(s.h, n)

        def ln_phase():
            ag = []

            def _(v):
                v.wait_ge(sT.h, M["consts"])
                wt(v, M["bufV_free"])
                wt(v, M["xres_w"])
                for j in range(TC):
                    for gq in range(4):
                        sDV.inc(v.bn_stats(stats[:, 6 * gq:6 * gq + 6],
                                           xres[:, j * H + 512 * gq:j * H + 512 * (gq + 1)]))
                    v.wait_ge(sDV.h, sDV.n)
                    sDV.inc(v.bn_aggr(mvall[:, 2 * j:2 * j + 2], stats[:, :]))
                    v.wait_ge(sDV.h, sDV.n)
                    ag.append(sDV.n)
                    sDV.inc(v.tensor_scalar_mul(negmean[:, j:j + 1],
                                                mvall[:, 2 * j:2 * j + 1], -1.0))
            block.vector(_)
            neg_mark = sDV.n

            sq = []

            def _(s):
                s.wait_ge(sT.h, M["consts"])
                for j in range(TC):
                    s.wait_ge(sDV.h, ag[j])
                    sAC.inc(s.activation(sdall[:, j:j + 1], mvall[:, 2 * j + 1:2 * j + 2],
                                         AF.Sqrt, bias=epsln[:, :]))
                    sq.append(sAC.n)
            block.scalar(_)

            xm = []

            def _(v):
                v.wait_ge(sDV.h, neg_mark)
                for j in range(TC):
                    v.wait_ge(sAC.h, sq[j])
                    sDV.inc(v.reciprocal(rstd[:, j:j + 1], sdall[:, j:j + 1]))
                    v.wait_ge(sDV.h, sDV.n)
                for j in range(TC):
                    sDV.inc(v.tensor_scalar(xln_tm[j][:, :], xres[:, j * H:(j + 1) * H],
                                            negmean[:, j:j + 1], rstd[:, j:j + 1],
                                            ALU.add, ALU.mult))
                    xm.append(sDV.n)
            block.vector(_)

            def _(g):
                for j in range(TC):
                    g.wait_ge(sDV.h, xm[j])
                    sB.inc(g.dma_start(out=xln_dram[j * 128:(j + 1) * 128, :],
                                       in_=xln_tm[j][:, :]), 16)
            block.gpsimd(_)
            bdone = sB.n

            def _(sy):
                wt(sy, M["xlnT_free"])
                sy.wait_ge(sB.h, bdone)
                for hc in range(HCN):
                    sT.inc(sy.dma_start_transpose(xlnT[:, hc * T:(hc + 1) * T],
                                                  xln_dram[:, hc * 128:(hc + 1) * 128]), 16)
            block.sync(_)
            M["bufV_free"] = [(sB, bdone)]
            return sT.n

        def ffn_phase(xlnT_mark):
            wA_marks = {}
            wB_marks = {}
            wC_marks = {}

            def _(sy):
                wA_ex, M["wA_extra"] = M["wA_extra"], []
                wB_ex, M["wB_extra"] = M["wB_extra"], []
                wC_ex, M["wC_extra"] = M["wC_extra"], []
                wa_idx[1] = wa_idx[0]
                wb_idx[1] = wb_idx[0]
                wc_idx[1] = wc_idx[0]
                for ip, path in enumerate(PATHS):
                    for f in range(FCN):
                        if path == "sh":
                            src1 = ws1_sw[f * H:(f + 1) * H, :]
                            src3 = ws3_sw[f * H:(f + 1) * H, :]
                            cm = max(cast["ws1"], cast["ws3"])
                        else:
                            base = path * FCN * H
                            src1 = we1_sw[base + f * H:base + (f + 1) * H, :]
                            src3 = we3_sw[base + f * H:base + (f + 1) * H, :]
                            cm = max(cast[("we1", path)], cast[("we3", path)])
                        sy.wait_ge(sC.h, cm)
                        ia = wa_idx[0]
                        if ia >= 2:
                            sy.wait_ge(sWa.h, ia - 1)
                        if wa_idx[0] - wa_idx[1] < 2:
                            wt(sy, wA_ex)
                        sT.inc(sy.dma_start_transpose(wbufA[:, (ia % 2) * H:(ia % 2 + 1) * H],
                                                      src1), 16)
                        wA_marks[(ip, f)] = sT.n
                        wa_idx[0] += 1
                        ib = wb_idx[0]
                        if ib >= 2:
                            sy.wait_ge(sWb.h, ib - 1)
                        if wb_idx[0] - wb_idx[1] < 2:
                            wt(sy, wB_ex)
                        sT.inc(sy.dma_start_transpose(wbufB[:, (ib % 2) * H:(ib % 2 + 1) * H],
                                                      src3), 16)
                        wB_marks[(ip, f)] = sT.n
                        wb_idx[0] += 1
                    w2src = ws2_bf if path == "sh" else we2_bf[path, :, :]
                    cm = cast["ws2"] if path == "sh" else cast[("we2", path)]
                    sy.wait_ge(sC.h, cm)
                    for Q in range(4):
                        ic = wc_idx[0]
                        if ic >= 2:
                            sy.wait_ge(sWc.h, ic - 1)
                        if wc_idx[0] - wc_idx[1] < 2:
                            wt(sy, wC_ex)
                        reg = (ic % 2) * 5632
                        for f in range(FCN):
                            sT.inc(sy.dma_start_transpose(
                                wbufC[:, reg + f * 512:reg + (f + 1) * 512],
                                w2src[Q * 512:(Q + 1) * 512, f * 128:(f + 1) * 128]), 16)
                        wC_marks[(ip, Q)] = sT.n
                        wc_idx[0] += 1
            block.sync(_)

            gate_pe = []
            sig_pred = [sAC.n + j + 1 for j in range(TC)]

            def _(t):
                t.wait_ge(sT.h, xlnT_mark)
                t.wait_ge(sC.h, cast["gate"])
                wt(t, M["pu_free"])
                for j in range(TC):
                    if j >= 2:
                        t.wait_ge(sAC.h, sig_pred[j - 2])
                    dst = pu[:, 0:NE] if j % 2 == 0 else pu[:, 512:512 + NE]
                    for hc in range(HCN):
                        ins = t.matmul(dst, xlnT[:, hc * T + j * 128:hc * T + (j + 1) * 128],
                                       gateT[:, hc * NE:(hc + 1) * NE],
                                       start=(hc == 0), stop=(hc == HCN - 1))
                    sPE.inc(ins)
                    gate_pe.append(sPE.n)
            block.tensor(_)

            sig = []
            gv_pred = [sDV.n + 20 * (j + 1) for j in range(TC)]

            def _(s):
                for j in range(TC):
                    s.wait_ge(sPE.h, gate_pe[j])
                    if j >= 1:
                        s.wait_ge(sDV.h, gv_pred[j - 1])
                    src = pu[:, 0:NE] if j % 2 == 0 else pu[:, 512:512 + NE]
                    sAC.inc(s.activation(scores[:, :], src, AF.Sigmoid))
                    sig.append(sAC.n)
            block.scalar(_)
            assert sig == sig_pred

            gv = []

            def _(v):
                for j in range(TC):
                    v.wait_ge(sAC.h, sig[j])

                    def stp(ins):
                        sDV.inc(ins)
                        v.wait_ge(sDV.h, sDV.n)
                    stp(v.tensor_add(routing[:, :], scores[:, :], biasb[:, :]))
                    stp(v.reduce_max(idx1[:, :], routing[:, :], axis=AX))
                    stp(v.tensor_single_scalar(eqt[:, :], routing[:, :], idx1[:, :], ALU.is_equal))
                    stp(v.scalar_tensor_tensor(cand[:, :], eqt[:, :], -BIG, iopb_sb[:, :],
                                               ALU.mult, ALU.add))
                    stp(v.tensor_reduce(idx1[:, :], cand[:, :], axis=AX, op=ALU.min))
                    stp(v.tensor_single_scalar(oh1[:, :], iota_sb[:, :], idx1[:, :], ALU.is_equal))
                    stp(v.scalar_tensor_tensor(r2t[:, :], oh1[:, :], -BIG, routing[:, :],
                                               ALU.mult, ALU.add))
                    stp(v.reduce_max(idx1[:, :], r2t[:, :], axis=AX))
                    stp(v.tensor_single_scalar(eqt[:, :], r2t[:, :], idx1[:, :], ALU.is_equal))
                    stp(v.scalar_tensor_tensor(cand[:, :], eqt[:, :], -BIG, iopb_sb[:, :],
                                               ALU.mult, ALU.add))
                    stp(v.tensor_reduce(idx1[:, :], cand[:, :], axis=AX, op=ALU.min))
                    stp(v.tensor_single_scalar(oh2[:, :], iota_sb[:, :], idx1[:, :], ALU.is_equal))
                    stp(v.scalar_tensor_tensor(junk8[:, :], scores[:, :], 1.0, oh1[:, :],
                                               ALU.mult, ALU.mult, accum_out=sel1[:, :]))
                    stp(v.scalar_tensor_tensor(junk8[:, :], scores[:, :], 1.0, oh2[:, :],
                                               ALU.mult, ALU.mult, accum_out=sel2[:, :]))
                    stp(v.tensor_add(gsum[:, :], sel1[:, :], sel2[:, :]))
                    stp(v.reciprocal(ginv[:, :], gsum[:, :]))
                    stp(v.tensor_mul(g1[:, :], sel1[:, :], ginv[:, :]))
                    stp(v.tensor_mul(g2[:, :], sel2[:, :], ginv[:, :]))
                    stp(v.tensor_scalar(junk8[:, :], oh1[:, :], g1[:, :], None, ALU.mult))
                    stp(v.scalar_tensor_tensor(c_all[:, j * NE:(j + 1) * NE], oh2[:, :],
                                               g2[:, :], junk8[:, :], ALU.mult, ALU.add))
                    gv.append(sDV.n)
            block.vector(_)
            assert gv == gv_pred

            ctp = []
            ctc_pred = [sDV.n + j + 1 for j in range(TC)]

            def _(t):
                t.wait_ge(sAC.h, sig[-1])
                for j in range(TC):
                    t.wait_ge(sDV.h, gv[j])
                    if j >= 1:
                        t.wait_ge(sDV.h, ctc_pred[j - 1])
                    sPE.inc(t.transpose(pu[0:NE, 0:128], c_all[:, j * NE:(j + 1) * NE],
                                        id_sb[:, :]))
                    ctp.append(sPE.n)
            block.tensor(_)

            ctc = []

            def _(v):
                for j in range(TC):
                    v.wait_ge(sPE.h, ctp[j])
                    sDV.inc(v.tensor_copy(cT[:, j * 128:(j + 1) * 128], pu[0:NE, 0:128]))
                    ctc.append(sDV.n)
            block.vector(_)
            assert ctc == ctc_pred
            cT_mark = sDV.n

            pu_reader = cT_mark
            out_done = 0
            bufK_war = list(M["bufK_free"])
            prev_tail = [None] * 4
            prev_evac_base = [0]
            for ip, path in enumerate(PATHS):
                is_exp = path != "sh"
                if is_exp:
                    def _(t, e=path, pr=pu_reader):
                        t.wait_ge(sDV.h, max(cT_mark, pr))
                        for half in range(2):
                            sPE.inc(t.matmul(pu[:, half * 512:(half + 1) * 512],
                                             oh_sb[:, e * 128:(e + 1) * 128],
                                             cT[:, half * 512:(half + 1) * 512],
                                             start=True, stop=True))
                    block.tensor(_)
                    cbc_mark = sPE.n

                silu_m = {}
                mul2_m = {}
                h1_m = {}
                h3_m = {}
                for f in range(FCN):
                    def _(t, f=f, ip=ip):
                        t.wait_ge(sT.h, wA_marks[(ip, f)])
                        t.wait_ge(sT.h, wB_marks[(ip, f)])
                        slotA = wbufA[:, ((wa_seq[0] + f) % 2) * H:((wa_seq[0] + f) % 2 + 1) * H]
                        slotB = wbufB[:, ((wb_seq[0] + f) % 2) * H:((wb_seq[0] + f) % 2 + 1) * H]
                        for half in range(2):
                            mk = silu_m.get((f - 1, half))
                            if mk is not None:
                                t.wait_ge(sAC.h, mk)
                            else:
                                wt(t, M["pb_free"][half])
                            for hc in range(HCN):
                                ins = t.matmul(ph1[:, half * 512:(half + 1) * 512],
                                               slotA[:, hc * 128:(hc + 1) * 128],
                                               xlnT[:, hc * T + half * 512:hc * T + (half + 1) * 512],
                                               start=(hc == 0), stop=(hc == HCN - 1))
                            sPE.inc(ins)
                            h1_m[(f, half)] = sPE.n
                        t.wait_ge(sPE.h, h1_m[(f, 1)])
                        sWa.inc(t.nop())
                        for half in range(2):
                            mk = mul2_m.get((f - 1, half))
                            if mk is not None:
                                t.wait_ge(sDV.h, mk)
                            else:
                                wt(t, M["pb_free"][2 + half])
                            for hc in range(HCN):
                                ins = t.matmul(ph3[:, half * 512:(half + 1) * 512],
                                               slotB[:, hc * 128:(hc + 1) * 128],
                                               xlnT[:, hc * T + half * 512:hc * T + (half + 1) * 512],
                                               start=(hc == 0), stop=(hc == HCN - 1))
                            sPE.inc(ins)
                            h3_m[(f, half)] = sPE.n
                        t.wait_ge(sPE.h, h3_m[(f, 1)])
                        sWb.inc(t.nop())
                    block.tensor(_)

                    def _(s, f=f):
                        for half in range(2):
                            s.wait_ge(sPE.h, h1_m[(f, half)])
                            mk = mul2_m.get((f - 1, half))
                            if mk is not None:
                                s.wait_ge(sDV.h, mk)
                            sAC.inc(s.activation(tmp_silu[:, half * 512:(half + 1) * 512],
                                                 ph1[:, half * 512:(half + 1) * 512], AF.Silu))
                            silu_m[(f, half)] = sAC.n
                    block.scalar(_)

                    def _(v, f=f, is_exp=is_exp):
                        for half in range(2):
                            v.wait_ge(sAC.h, silu_m[(f, half)])
                            v.wait_ge(sPE.h, h3_m[(f, half)])
                            if f == 0 and half == 0:
                                if out_done:
                                    v.wait_ge(sPE.h, out_done)
                                else:
                                    wt(v, bufK_war)
                                if is_exp:
                                    v.wait_ge(sPE.h, cbc_mark)
                            ts = tmp_silu[:, half * 512:(half + 1) * 512]
                            if is_exp:
                                sDV.inc(v.tensor_mul(ts, ts, pu[:, half * 512:(half + 1) * 512]))
                                v.wait_ge(sDV.h, sDV.n)
                            sDV.inc(v.tensor_mul(
                                bufK[:, f * T + half * 512:f * T + (half + 1) * 512],
                                ts, ph3[:, half * 512:(half + 1) * 512]))
                            mul2_m[(f, half)] = sDV.n
                    block.vector(_)
                if DEBUG_FLUSH_ACT:
                    def _(v):
                        v.wait_ge(sDV.h, sDV.n)
                        for rep in range(3):
                            sDV.inc(v.tensor_copy(junk8[:, :], bufK[0:128, 0:2 * NE].bitcast(F32)))
                            v.wait_ge(sDV.h, sDV.n)
                    block.vector(_)
                wa_seq[0] += FCN
                wb_seq[0] += FCN
                pu_reader = sDV.n
                act_ready = sDV.n
                M["pb_free"] = [[(sAC, silu_m[(FCN - 1, 0)])], [(sAC, silu_m[(FCN - 1, 1)])],
                                [(sDV, mul2_m[(FCN - 1, 0)])], [(sDV, mul2_m[(FCN - 1, 1)])]]

                tile_m = {}
                evac_m = {}
                for Q in range(4):
                    for jg in range(2):
                        def _(t, Q=Q, jg=jg, ip=ip, act_ready=act_ready):
                            t.wait_ge(sT.h, wC_marks[(ip, Q)])
                            if Q == 0 and jg == 0:
                                t.wait_ge(sDV.h, act_ready)
                                if ip == 0:
                                    wt(t, M["po_free"] + M["pu_free"])
                            reg = ((wc_seq[0] + Q) % 2) * 5632
                            for j in range(jg * 4, jg * 4 + 4):
                                tl = Q * 8 + j
                                bank = tl % 4
                                if DEBUG_SERIAL_OUT and (tl > 0 or prev_evac_base[0]):
                                    t.wait_ge(sDV.h, (act_ready + tl) if tl > 0 else prev_evac_base[0])
                                if tl >= 4:
                                    t.wait_ge(sDV.h, evac_m[tl - 4])
                                elif prev_tail[bank] is not None:
                                    t.wait_ge(sDV.h, prev_tail[bank])
                                dst = (po if bank < 2 else pu)[:, (bank % 2) * 512:(bank % 2) * 512 + 512]
                                for f in range(FCN):
                                    ins = t.matmul(dst,
                                                   bufK[:, f * T + j * 128:f * T + (j + 1) * 128],
                                                   wbufC[:, reg + f * 512:reg + (f + 1) * 512],
                                                   start=(f == 0), stop=(f == FCN - 1))
                                sPE.inc(ins)
                                tile_m[tl] = sPE.n
                            if jg == 1:
                                t.wait_ge(sPE.h, tile_m[Q * 8 + 7])
                                sWc.inc(t.nop())
                        block.tensor(_)

                        def _(v, Q=Q, jg=jg):
                            for j in range(jg * 4, jg * 4 + 4):
                                tl = Q * 8 + j
                                bank = tl % 4
                                v.wait_ge(sPE.h, tile_m[tl])
                                src = (po if bank < 2 else pu)[:, (bank % 2) * 512:(bank % 2) * 512 + 512]
                                dst = xres[:, j * H + Q * 512:j * H + (Q + 1) * 512]
                                sDV.inc(v.scalar_tensor_tensor(dst, src, 0.5, dst,
                                                               ALU.mult, ALU.add))
                                evac_m[tl] = sDV.n
                        block.vector(_)
                wc_seq[0] += 4
                out_done = sPE.n
                prev_evac_base[0] = evac_m[31]
                prev_tail = [evac_m[28], evac_m[29], evac_m[30], evac_m[31]]
                pu_reader = sDV.n
                M["po_free"] = [(sDV, sDV.n)]
                M["pu_free"] = [(sDV, sDV.n)]

            M["bufK_free"] = [(sPE, sPE.n)]
            M["xlnT_free"] = [(sPE, sPE.n)]
            M["xres_w"] = [(sDV, sDV.n)]
            M["bufV_free"] = [(sDV, sDV.n), (sPE, sPE.n)]
            return sDV.n

        wa_seq = [0]
        wb_seq = [0]
        wc_seq = [0]

        def attn_phase(xlnT_mark):
            qkv_marks = {}

            def emit_slabs(fb_list):
                def _(sy):
                    wA_ex, M["wA_extra"] = M["wA_extra"], []
                    for Fb in fb_list:
                        sy.wait_ge(sC.h, cast[("wqkvb", Fb // 8)])
                        ia = wa_idx[0]
                        if ia >= 2:
                            sy.wait_ge(sWa.h, ia - 1)
                        if wa_idx[0] - wa_idx[1] < 2:
                            wt(sy, wA_ex)
                        sT.inc(sy.dma_start_transpose(wbufA[:, (ia % 2) * H:(ia % 2 + 1) * H],
                                                      wqkv_sw[Fb * H:(Fb + 1) * H, :]), 16)
                        qkv_marks[Fb] = sT.n
                        wa_idx[0] += 1
                block.sync(_)
            wa_idx[1] = wa_idx[0]
            emit_slabs(list(range(16, 48)))

            def _(g):
                g.wait_ge(sWb.h, wb_seq[0])
                sC.inc(g.dma_start(out=bprojT, in_=b_proj.rearrange("h -> () h")), 16)
                cast["bproj"] = sC.n
            block.gpsimd(_)

            kvb_pred = {}
            for k in range(32):
                kvb_pred[k] = sB.n + (k + 1) * 16
            evac_kv = {}
            tile_kv = {}
            kv_stage_init = [0]

            for k, Fb in enumerate(range(16, 48)):
                def _(t, k=k, Fb=Fb):
                    t.wait_ge(sT.h, qkv_marks[Fb])
                    slot = wbufA[:, ((wa_seq[0] + k) % 2) * H:((wa_seq[0] + k) % 2 + 1) * H]
                    for th in range(2):
                        gt = 2 * k + th
                        if gt >= 4:
                            s_, n_ = evac_kv[gt - 4]
                            t.wait_ge(s_.h, n_)
                        else:
                            wt(t, M["pb_free"][gt])
                        for hc in range(HCN):
                            ins = t.matmul(pbank[gt % 4],
                                           slot[:, hc * 128:(hc + 1) * 128],
                                           xlnT[:, hc * T + th * 512:hc * T + (th + 1) * 512],
                                           start=(hc == 0), stop=(hc == HCN - 1))
                        sPE.inc(ins)
                        tile_kv[gt] = sPE.n
                    t.wait_ge(sPE.h, tile_kv[2 * k + 1])
                    sWa.inc(t.nop())
                block.tensor(_)

                is_k = Fb < 32

                def _(e, k=k, Fb=Fb, is_k=is_k):
                    for th in range(2):
                        gt = 2 * k + th
                        e.wait_ge(sPE.h, tile_kv[gt])
                        if k >= 3:
                            e.wait_ge(sB.h, kvb_pred[k - 3])
                        elif kv_stage_init[0] == 0:
                            e.wait_ge(sWc.h, wc_seq[0])
                        dst = kv_stage[k % 3][:, th * 512:(th + 1) * 512]
                        if is_k:
                            ins = e.activation(dst, pbank[gt % 4], AF.Relu)
                            evac_kv[gt] = (sAC, sAC.inc(ins))
                        else:
                            ins = e.tensor_copy(dst, pbank[gt % 4])
                            evac_kv[gt] = (sDV, sDV.inc(ins))
                    kv_stage_init[0] = 1
                (block.scalar if is_k else block.vector)(_)

                def _(g, k=k, Fb=Fb):
                    for s_, n_ in (evac_kv[2 * k], evac_kv[2 * k + 1]):
                        g.wait_ge(s_.h, n_)
                    sB.inc(g.dma_start(out=kv_dram[(Fb - 16) * 128:(Fb - 15) * 128, :],
                                       in_=kv_stage[k % 3][:, :]), 16)
                block.gpsimd(_)
                assert sB.n == kvb_pred[k]
            wa_seq[0] += 32
            M["pb_free"] = [[evac_kv[60]], [evac_kv[61]], [evac_kv[62]], [evac_kv[63]]]

            if ATTN_UPTO < 2:
                return sDV.n
            kT_marks = []

            def _(sy):
                wt(sy, M["bufK_free"])
                sy.wait_ge(sB.h, kvb_pred[15])
                for j in range(TC):
                    sT.inc(sy.dma_start_transpose(bufK[:, j * H:(j + 1) * H],
                                                  kv_dram[0:H, j * 128:(j + 1) * 128]), 16)
                    kT_marks.append(sT.n)
                wt(sy, M["bufV_free"])
                sy.wait_ge(sB.h, kvb_pred[31])
                for j in range(TC):
                    sT.inc(sy.dma_start_transpose(bufV[:, j * H:(j + 1) * H],
                                                  kv_dram[H:2 * H, j * 128:(j + 1) * 128]), 16)
                    kT_marks.append(sT.n)
            block.sync(_)
            kvT_mark = sT.n
            if ATTN_UPTO >= 4:
                emit_slabs(list(range(16)))

            def _(t):
                t.wait_ge(sT.h, kvT_mark)
                wt(t, M["pu_free"])
                for bank in range(2):
                    first = True
                    for g in range(8):
                        hgrp = bank * 8 + g
                        for q in range(4):
                            hd = 4 * hgrp + q
                            col0 = bank * 512 + g * 33
                            for j in range(TC):
                                t.matmul(pu[32 * q:32 * q + 32, col0:col0 + 32],
                                         bufK[:, j * H + hd * 32:j * H + (hd + 1) * 32],
                                         bufV[:, j * H + hd * 32:j * H + (hd + 1) * 32],
                                         start=first, stop=False,
                                         tile_position=(0, 32 * q))
                                first = False
                                last = (g == 7 and q == 3 and j == TC - 1)
                                ins = t.matmul(pu[32 * q:32 * q + 32, col0 + 32:col0 + 33],
                                               bufK[:, j * H + hd * 32:j * H + (hd + 1) * 32],
                                               oc_bf[:, :],
                                               start=False, stop=last,
                                               tile_position=(0, 32 * q))
                    sPE.inc(ins)
            block.tensor(_)
            vkT_mm_mark = sPE.n

            if ATTN_UPTO < 3:
                return sDV.n
            def _(v):
                v.wait_ge(sPE.h, vkT_mm_mark)
                sDV.inc(v.tensor_copy(vkT_sb[:, 0:264], pu[:, 0:264]))
                sDV.inc(v.tensor_copy(vkT_sb[:, 264:528], pu[:, 512:776]))
            block.vector(_)
            vkT_ev_mark = sDV.n
            M["pu_free"] = [(sDV, sDV.n)]

            def _(g):
                g.wait_ge(sDV.h, vkT_ev_mark)
                sB.inc(g.dma_start(out=cc_in.ap(), in_=vkT_sb[:, :]), 16)
                g.wait_ge(sB.h, sB.n)
                sCC.inc(g.collective_compute(
                    "AllReduce", ALU.add,
                    replica_groups=[[0, 1, 2, 3], [4, 5, 6, 7]],
                    ins=[cc_in.ap().opt()], outs=[cc_out.ap().opt()]))
                g.wait_ge(sCC.h, sCC.n)
                sB.inc(g.dma_start(out=vkT_sb[:, :], in_=cc_out.ap()), 16)
            block.gpsimd(_)
            cc_mark = sB.n

            def _(v):
                v.wait_ge(sB.h, cc_mark)
                sDV.inc(v.tensor_copy(vkT_bf[:, :], vkT_sb[:, :]))
            block.vector(_)
            vkT_bf_mark = sDV.n

            if ATTN_UPTO < 4:
                return sDV.n
            tile_q = {}
            qev = []

            for fb in range(16):
                def _(t, fb=fb):
                    t.wait_ge(sT.h, qkv_marks[fb])
                    slot = wbufA[:, ((wa_seq[0] + fb) % 2) * H:((wa_seq[0] + fb) % 2 + 1) * H]
                    for th in range(2):
                        gt = 2 * fb + th
                        if gt >= 4:
                            s_, n_ = qev[gt - 4]
                            t.wait_ge(s_.h, n_)
                        else:
                            wt(t, M["pb_free"][gt])
                        for hc in range(HCN):
                            ins = t.matmul(pbank[gt % 4],
                                           slot[:, hc * 128:(hc + 1) * 128],
                                           xlnT[:, hc * T + th * 512:hc * T + (th + 1) * 512],
                                           start=(hc == 0), stop=(hc == HCN - 1))
                        sPE.inc(ins)
                        tile_q[gt] = sPE.n
                    t.wait_ge(sPE.h, tile_q[2 * fb + 1])
                    sWa.inc(t.nop())
                block.tensor(_)

                def _(s, fb=fb):
                    for th in range(2):
                        gt = 2 * fb + th
                        s.wait_ge(sPE.h, tile_q[gt])
                        if gt == 0:
                            s.wait_ge(sPE.h, vkT_mm_mark)
                        ins = s.activation(bufK[:, fb * T + th * 512:fb * T + (th + 1) * 512],
                                           pbank[gt % 4], AF.Relu)
                        qev.append((sAC, sAC.inc(ins)))
                block.scalar(_)
            wa_seq[0] += 16
            q_done = sAC.n
            M["pb_free"] = [[qev[28]], [qev[29]], [qev[30]], [qev[31]]]

            if ATTN_UPTO < 5:
                return sDV.n
            ab_pred = [sB.n + (j + 1) * 16 for j in range(TC)]
            epi_m = {}
            otile_m = {}

            for j in range(TC):
                for tp in range(4):
                    def _(t, j=j, tp=tp):
                        if j == 0 and tp == 0:
                            t.wait_ge(sDV.h, vkT_bf_mark)
                            t.wait_ge(sAC.h, q_done)
                            wt(t, M["po_free"])
                        for tt in range(2 * tp, 2 * tp + 2):
                            gi = j * 8 + tt
                            q4, sh = tt % 4, tt // 4
                            if gi >= 2:
                                mk = epi_m.get(gi - 2)
                                if mk is not None:
                                    t.wait_ge(sDV.h, mk)
                            off = (gi % 2) * 512
                            for i in range(8):
                                hgrp = 8 * sh + i
                                ins = t.matmul(
                                    po[:, off + i * 33:off + (i + 1) * 33],
                                    bufK[32 * q4:32 * q4 + 32,
                                         hgrp * T + j * 128:hgrp * T + (j + 1) * 128],
                                    vkT_bf[32 * q4:32 * q4 + 32, hgrp * 33:(hgrp + 1) * 33],
                                    start=(i == 0), stop=(i == 7),
                                    tile_position=(32 * q4, 0))
                            sPE.inc(ins)
                            otile_m[gi] = sPE.n
                    block.tensor(_)

                    def _(v, j=j, tp=tp):
                        if ATTN_UPTO == 41:
                            for tt in range(2 * tp, 2 * tp + 2):
                                gi = j * 8 + tt
                                v.wait_ge(sPE.h, otile_m[gi])
                                sDV.inc(v.tensor_copy(dnm[:, :], po[0:128, 0:NE]))
                                epi_m[gi] = sDV.n
                            return
                        for tt in range(2 * tp, 2 * tp + 2):
                            gi = j * 8 + tt
                            q4, sh = tt % 4, tt // 4
                            v.wait_ge(sPE.h, otile_m[gi])
                            off = (gi % 2) * 512
                            p3 = po[:, off:off + 264].rearrange("p (h e) -> p h e", e=33)
                            sDV.inc(v.tensor_single_scalar(dnm[:, :], p3[:, :, 32:33].rearrange("p h e -> p (h e)"),
                                                           EPS_AT, ALU.add))
                            v.wait_ge(sDV.h, sDV.n)
                            sDV.inc(v.reciprocal(rcp[:, :], dnm[:, :]))
                            v.wait_ge(sDV.h, sDV.n)
                            base = j * H + 1024 * sh
                            ov = bufV[:, base:base + 1024].rearrange("p (i r) -> p i r", r=128)
                            sDV.inc(v.tensor_tensor(
                                ov[:, :, 32 * q4:32 * q4 + 32],
                                p3[:, :, 0:32],
                                rcp[:, :].rearrange("p h -> p h ()").broadcast_to([128, NE, 32]),
                                ALU.mult))
                            epi_m[gi] = sDV.n
                    block.vector(_)

                def _(g, j=j):
                    if ATTN_UPTO in (41, 42):
                        return
                    g.wait_ge(sDV.h, epi_m[j * 8 + 7])
                    sB.inc(g.dma_start(out=xln_dram[j * 128:(j + 1) * 128, :],
                                       in_=bufV[:, j * H:(j + 1) * H]), 16)
                block.gpsimd(_)
                assert ATTN_UPTO <= 42 or sB.n == ab_pred[j]
            M["po_free"] = [(sDV, sDV.n)]
            attn_out_done = sPE.n

            if ATTN_UPTO < 6:
                return sDV.n
            def _(sy):
                sy.wait_ge(sPE.h, attn_out_done)
                sy.wait_ge(sB.h, ab_pred[-1])
                for hc in range(HCN):
                    sT.inc(sy.dma_start_transpose(xlnT[:, hc * T:(hc + 1) * T],
                                                  xln_dram[:, hc * 128:(hc + 1) * 128]), 16)
            block.sync(_)
            afm_mark = sT.n

            pset = {}

            def _(sy):
                sy.wait_ge(sC.h, cast["wproj"])
                wc_idx[1] = wc_idx[0]
                for Q in range(4):
                    for jh in range(2):
                        for ch in range(2):
                            ic = wc_idx[0]
                            if ic >= 2:
                                sy.wait_ge(sWc.h, ic - 1)
                            reg = (ic % 2) * 5632
                            for cc in range(8 * ch, 8 * ch + 8):
                                sT.inc(sy.dma_start_transpose(
                                    wbufC[:, reg + (cc - 8 * ch) * 512:reg + (cc - 8 * ch + 1) * 512],
                                    wproj_bf[Q * 512:(Q + 1) * 512, cc * 128:(cc + 1) * 128]), 16)
                            pset[(Q, jh, ch)] = sT.n
                            wc_idx[0] += 1
            block.sync(_)

            pj_evac = {}
            for Q in range(4):
                for jh in range(2):
                    ptile = {}

                    def _(t, Q=Q, jh=jh):
                        t.wait_ge(sT.h, pset[(Q, jh, 0)])
                        if Q == 0 and jh == 0:
                            t.wait_ge(sT.h, afm_mark)
                            t.wait_ge(sC.h, cast["bproj"])
                            wt(t, M["pb_free"][0] + M["pb_free"][1]
                               + M["pb_free"][2] + M["pb_free"][3])
                        else:
                            prev = (Q, jh - 1) if jh else (Q - 1, 1)
                            for jj in range(4):
                                t.wait_ge(sDV.h, pj_evac[(prev[0], prev[1], jj)])
                        for ch in range(2):
                            if ch == 1:
                                t.wait_ge(sT.h, pset[(Q, jh, 1)])
                            reg = ((wc_seq[0] + 2 * (2 * Q + jh) + ch) % 2) * 5632
                            for jj in range(4):
                                j = jh * 4 + jj
                                for cc in range(8):
                                    c0 = 8 * ch + cc
                                    ins = t.matmul(
                                        pbank[jj],
                                        xlnT[:, c0 * T + j * 128:c0 * T + (j + 1) * 128],
                                        wbufC[:, reg + cc * 512:reg + (cc + 1) * 512],
                                        start=(ch == 0 and cc == 0), stop=False)
                                if ch == 1:
                                    ins = t.matmul(pbank[jj], or_bf[:, :],
                                                   bprojT[0:1, Q * 512:(Q + 1) * 512],
                                                   start=False, stop=(cc == 7))
                                sPE.inc(ins)
                                ptile[jj] = sPE.n
                            if ch == 1:
                                t.wait_ge(sPE.h, ptile[3])
                                sWc.inc(t.nop(), 2)
                    block.tensor(_)

                    def _(v, Q=Q, jh=jh):
                        for jj in range(4):
                            j = jh * 4 + jj
                            v.wait_ge(sPE.h, ptile[jj])
                            dst = xres[:, j * H + Q * 512:j * H + (Q + 1) * 512]
                            sDV.inc(v.tensor_add(dst, dst, pbank[jj]))
                            pj_evac[(Q, jh, jj)] = sDV.n
                    block.vector(_)
            wc_seq[0] += 16
            proj_done_pe = sPE.n
            M["pb_free"] = [[(sDV, pj_evac[(3, 1, jj)])] for jj in range(4)]
            M["bufK_free"] = [(sPE, attn_out_done)]
            M["xlnT_free"] = [(sPE, proj_done_pe)]
            M["xres_w"] = [(sDV, sDV.n)]
            M["bufV_free"] = [(sB, ab_pred[-1])]
            M["wB_extra"] = [(sPE, proj_done_pe)]
            M["wC_extra"] = [(sB, kvb_pred[31])]
            return sDV.n

        m1 = ln_phase()
        ffn_phase(m1)
        if DEBUG_DUMP:
            def _(sy):
                wt(sy, M["xres_w"])
                wt(sy, M["bufK_free"])
                sT.inc(sy.dma_start(out=dbg_actK, in_=bufK[:, :]), 16)
                sT.inc(sy.dma_start(out=dbg_bufV, in_=bufV[:, :]), 16)
                sT.inc(sy.dma_start(out=dbg_call, in_=c_all[:, :]), 16)
                sT.inc(sy.dma_start(out=dbg_xlnT, in_=xlnT[:, :]), 16)
                sy.wait_ge(sT.h, sT.n)
            block.sync(_)
        if STAGES >= 2:
            m2 = ln_phase()
            attn_phase(m2)
        if STAGES >= 3:
            m3 = ln_phase()
            ffn_phase(m3)

        def _(sy):
            wt(sy, M["xres_w"])
            sT.inc(sy.dma_start(out=out.rearrange("(j p) h -> p j h", p=128),
                                in_=xres[:, :].rearrange("p (j h) -> p j h", h=H)), 16)
            sy.wait_ge(sT.h, sT.n)
        block.sync(_)

    return nc


_NC_CACHE = {}


def host_swz(w):
    R = w.shape[0]
    return np.ascontiguousarray(
        w.reshape(R // 128, 128, HCN, 128).transpose(0, 2, 1, 3).reshape(R * HCN, 128))


def kernel(**inputs):
    x = np.asarray(inputs["L0_attn_x"])
    if "prog" not in _NC_CACHE:
        _NC_CACHE["prog"] = build_program()
    nc = _NC_CACHE["prog"]
    common = {
        "wqkv_s": host_swz(np.asarray(inputs["L0_attn_w_qkv"])),
        "w_proj": inputs["L0_attn_w_proj"],
        "b_proj": inputs["L0_attn_b_proj"],
        "g_bias": inputs["L0_ffn_bias"],
        "gate_wt": np.asarray(inputs["L0_ffn_gate_w"]).T,
        "ws1_s": host_swz(np.asarray(inputs["L0_ffn_ws1"])),
        "ws2": inputs["L0_ffn_ws2"],
        "ws3_s": host_swz(np.asarray(inputs["L0_ffn_ws3"])),
        "we1_s": host_swz(np.asarray(inputs["L0_ffn_we1"]).reshape(-1, H)),
        "we2": inputs["L0_ffn_we2"],
        "we3_s": host_swz(np.asarray(inputs["L0_ffn_we3"]).reshape(-1, H)),
    }
    common = {k: np.ascontiguousarray(v) for k, v in common.items()}
    in_maps = []
    for c in range(8):
        m = dict(common)
        m["x"] = np.ascontiguousarray(x[c // 4, (c % 4) * T:(c % 4 + 1) * T, :])
        in_maps.append(m)
    res = run_bass_kernel_spmd(nc, in_maps, core_ids=list(range(8)))
    y = np.empty((2, 4096, H), np.float32)
    for c in range(8):
        y[c // 4, (c % 4) * T:(c % 4 + 1) * T, :] = res.results[c]["out"]
    return y


# version 02
# speedup vs baseline: 1.0116x; 1.0116x over previous
import contextlib
import numpy as np
import ml_dtypes
import concourse.bass as bass
import concourse.mybir as mybir
from concourse.bass_utils import run_bass_kernel_spmd

F32 = mybir.dt.float32
BF16 = mybir.dt.bfloat16
AF = mybir.ActivationFunctionType
ALU = mybir.AluOpType
AX = mybir.AxisListType.X

T = 1024
TC = 8
H = 2048
HCN = 16
FF = 1408
FCN = 11
NE = 8
BIG = 1.0e4
EPS_LN = 1.0e-5
EPS_AT = 1.0e-15
STAGES = 3
ATTN_UPTO = 9
DEBUG_DUMP = False
DEBUG_SERIAL_OUT = False
DEBUG_FLUSH_ACT = False
PATHS = ["sh"] + list(range(NE))


class Sem:
    def __init__(self, handle):
        self.h = handle
        self.n = 0

    def inc(self, instr, k=1):
        instr.then_inc(self.h, k)
        self.n += k
        return self.n


def build_program():
    nc = bass.Bass("TRN2", target_bir_lowering=False, debug=False)

    x_in = nc.dram_tensor("x", [T, H], F32, kind="ExternalInput").ap()
    out = nc.dram_tensor("out", [T, H], F32, kind="ExternalOutput").ap()
    wqkv_s = nc.dram_tensor("wqkv_s", [3 * H, H], F32, kind="ExternalInput").ap()
    w_proj = nc.dram_tensor("w_proj", [8 * 128, 8 * 512], F32, kind="ExternalInput").ap()
    b_proj = nc.dram_tensor("b_proj", [H], F32, kind="ExternalInput").ap()
    gate_wt = nc.dram_tensor("gate_wt", [H, NE], F32, kind="ExternalInput").ap()
    g_bias = nc.dram_tensor("g_bias", [NE], F32, kind="ExternalInput").ap()
    ws1_s = nc.dram_tensor("ws1_s", [FF, H], F32, kind="ExternalInput").ap()
    ws2 = nc.dram_tensor("ws2", [512, FCN * 512], F32, kind="ExternalInput").ap()
    ws3_s = nc.dram_tensor("ws3_s", [FF, H], F32, kind="ExternalInput").ap()
    we1_s = nc.dram_tensor("we1_s", [NE * FF, H], F32, kind="ExternalInput").ap()
    we2 = nc.dram_tensor("we2", [NE * 512, FCN * 512], F32, kind="ExternalInput").ap()
    we3_s = nc.dram_tensor("we3_s", [NE * FF, H], F32, kind="ExternalInput").ap()

    ws1_sw = nc.dram_tensor("ws1_sw", [FF, H], BF16).ap()
    ws3_sw = nc.dram_tensor("ws3_sw", [FF, H], BF16).ap()
    we1_sw = nc.dram_tensor("we1_sw", [NE * FF, H], BF16).ap()
    we3_sw = nc.dram_tensor("we3_sw", [NE * FF, H], BF16).ap()
    wqkv_sw = nc.dram_tensor("wqkv_sw", [3 * H, H], BF16).ap()
    ws2_bf = nc.dram_tensor("ws2_bf", [512, FCN * 512], BF16).ap()
    we2_bf = nc.dram_tensor("we2_bf", [NE * 512, FCN * 512], BF16).ap()
    wproj_bf = nc.dram_tensor("wproj_bf", [8 * 128, 8 * 512], BF16).ap()
    xln_dram = nc.dram_tensor("xln_dram", [T, H], BF16).ap()
    kv_dram = nc.dram_tensor("kv_dram", [2 * H, T], BF16).ap()
    cc_in = nc.dram_tensor("cc_in", [128, 528], F32)
    cc_out = nc.dram_tensor("cc_out", [128, 528], F32)
    if DEBUG_DUMP:
        dbg_actK = nc.dram_tensor("dbg_actK", [128, HCN * T], BF16, kind="ExternalOutput").ap()
        dbg_bufV = nc.dram_tensor("dbg_bufV", [128, HCN * T], BF16, kind="ExternalOutput").ap()
        dbg_call = nc.dram_tensor("dbg_call", [128, TC * NE], F32, kind="ExternalOutput").ap()
        dbg_xlnT = nc.dram_tensor("dbg_xlnT", [128, HCN * T], BF16, kind="ExternalOutput").ap()

    ident_f = nc.inline_tensor(np.eye(128, dtype=np.float32), "ident_f").ap()
    oh_np = np.zeros((NE, NE * 128), dtype=np.float32)
    for e in range(NE):
        oh_np[e, e * 128:(e + 1) * 128] = 1.0
    onehot_f = nc.inline_tensor(oh_np, "onehot_f").ap()
    ones_col_bf = nc.inline_tensor(np.ones((128, 1), dtype=ml_dtypes.bfloat16), "c_oc_bf").ap()
    ones_row_bf = nc.inline_tensor(np.ones((1, 128), dtype=ml_dtypes.bfloat16), "c_or_bf").ap()
    ones_row_f = nc.inline_tensor(np.ones((1, 128), dtype=np.float32), "c_or_f").ap()
    iota_np = np.tile(np.arange(NE, dtype=np.float32)[None, :], (128, 1))
    iota_c = nc.inline_tensor(iota_np, "iota_c").ap()
    iopb_c = nc.inline_tensor(iota_np + BIG, "iopb_c").ap()
    epsln_c = nc.inline_tensor(np.full((128, 1), EPS_LN, dtype=np.float32), "epsln_c").ap()

    with contextlib.ExitStack() as st:
        def sbt(name, shape, dtype):
            return st.enter_context(nc.sbuf_tensor(name, shape, dtype))

        xres = sbt("xres", [128, TC * H], F32)
        xlnT = sbt("xlnT", [128, HCN * T], BF16)
        bufK = sbt("bufK", [128, HCN * T], BF16)
        bufV = sbt("bufV", [128, HCN * T], BF16)
        wbufA = sbt("wbufA", [128, 2 * H], BF16)
        wbufB = sbt("wbufB", [128, 2 * H], BF16)
        wbufC = sbt("wbufC", [128, FCN * 1024], BF16)
        gateT = sbt("gateT", [128, HCN * NE], BF16)
        id_sb = sbt("id_sb", [128, 128], F32)
        oh_sb = sbt("oh_sb", [NE, NE * 128], F32)
        oc_bf = sbt("oc_bf", [128, 1], BF16)
        or_bf = sbt("or_bf", [1, 128], BF16)
        or_f = sbt("or_f", [1, 128], F32)
        iota_sb = sbt("iota_sb", [128, NE], F32)
        iopb_sb = sbt("iopb_sb", [128, NE], F32)
        bias_sb = sbt("bias_sb", [1, NE], F32)
        biasb = sbt("biasb", [128, NE], F32)
        stats = sbt("stats", [128, 24], F32)
        mvall = sbt("mvall", [128, 2 * TC], F32)
        negmean = sbt("negmean", [128, TC], F32)
        sdall = sbt("sdall", [128, TC], F32)
        rstd = sbt("rstd", [128, TC], F32)
        scores = sbt("scores", [128, NE], F32)
        routing = sbt("routing", [128, NE], F32)
        eqt = sbt("eqt", [128, NE], F32)
        cand = sbt("cand", [128, NE], F32)
        idx1 = sbt("idx1", [128, 1], F32)
        oh1 = sbt("oh1", [128, NE], F32)
        oh2 = sbt("oh2", [128, NE], F32)
        junk8 = sbt("junk8", [128, NE], F32)
        r2t = sbt("r2t", [128, NE], F32)
        sel1 = sbt("sel1", [128, 1], F32)
        sel2 = sbt("sel2", [128, 1], F32)
        gsum = sbt("gsum", [128, 1], F32)
        ginv = sbt("ginv", [128, 1], F32)
        g1 = sbt("g1", [128, 1], F32)
        g2 = sbt("g2", [128, 1], F32)
        c_all = sbt("c_all", [128, TC * NE], F32)
        dnm = sbt("dnm", [128, NE], F32)
        epsln = sbt("epsln", [128, 1], F32)
        rcp = sbt("rcp", [128, NE], F32)

        xln_tm = [bufV[:, j * H:(j + 1) * H] for j in range(TC)]
        tmp_silu = bufV[:, 2 * H:3 * H].bitcast(F32)
        cT = bufV[0:NE, 3 * H:4 * H].bitcast(F32)
        kv_stage = [wbufC[:, 8192 + i * 1024:8192 + (i + 1) * 1024] for i in range(3)]
        bprojT = wbufB[0:1, 0:H]
        vkT_sb = wbufB[:, H:H + 1056].bitcast(F32)
        vkT_bf = wbufB[:, H + 1056:H + 1584]

        ph1 = st.enter_context(nc.psum_tensor("ph1", [128, 1024], F32))
        ph3 = st.enter_context(nc.psum_tensor("ph3", [128, 1024], F32))
        po = st.enter_context(nc.psum_tensor("po", [128, 1024], F32))
        pu = st.enter_context(nc.psum_tensor("pu", [128, 1024], F32))
        pbank = [ph1[:, 0:512], ph1[:, 512:1024], ph3[:, 0:512], ph3[:, 512:1024]]

        sC = Sem(st.enter_context(nc.semaphore("sC")))
        sB = Sem(st.enter_context(nc.semaphore("sB")))
        sT = Sem(st.enter_context(nc.semaphore("sT")))
        sPE = Sem(st.enter_context(nc.semaphore("sPE")))
        sDV = Sem(st.enter_context(nc.semaphore("sDV")))
        sAC = Sem(st.enter_context(nc.semaphore("sAC")))
        sCC = Sem(st.enter_context(nc.semaphore("sCC")))
        sWa = Sem(st.enter_context(nc.semaphore("sWa")))
        sWb = Sem(st.enter_context(nc.semaphore("sWb")))
        sWc = Sem(st.enter_context(nc.semaphore("sWc")))

        block = st.enter_context(nc.Block())

        M = {}

        def _(sy):
            for dst, src in ((id_sb[:, :], ident_f), (oh_sb[:, :], onehot_f),
                             (oc_bf[:, :], ones_col_bf), (or_bf[:, :], ones_row_bf),
                             (or_f[:, :], ones_row_f), (iota_sb[:, :], iota_c),
                             (iopb_sb[:, :], iopb_c), (epsln[:, :], epsln_c),
                             (bias_sb[:, :], g_bias.rearrange("e -> () e"))):
                sT.inc(sy.dma_start(out=dst, in_=src), 16)
            sT.inc(sy.dma_start(out=xres[:, :].rearrange("p (j h) -> p j h", h=H),
                                in_=x_in.rearrange("(j p) h -> p j h", p=128)), 16)
        block.sync(_)
        M["consts"] = sT.n

        cast = {}

        def _(g):
            sC.inc(g.dma_start(
                out=gateT[:, :].rearrange("p (hc e) -> p hc e", e=NE),
                in_=gate_wt.rearrange("(hc p) e -> p hc e", p=128)), 16)
            cast["gate"] = sC.n
            sC.inc(g.dma_start(out=ws1_sw, in_=ws1_s), 16)
            cast["ws1"] = sC.n
            sC.inc(g.dma_start(out=ws3_sw, in_=ws3_s), 16)
            cast["ws3"] = sC.n
            sC.inc(g.dma_start(out=ws2_bf, in_=ws2), 16)
            cast["ws2"] = sC.n
            for e in range(NE):
                sC.inc(g.dma_start(out=we1_sw[e * FF:(e + 1) * FF, :],
                                   in_=we1_s[e * FF:(e + 1) * FF, :]), 16)
                cast[("we1", e)] = sC.n
                sC.inc(g.dma_start(out=we3_sw[e * FF:(e + 1) * FF, :],
                                   in_=we3_s[e * FF:(e + 1) * FF, :]), 16)
                cast[("we3", e)] = sC.n
                sC.inc(g.dma_start(out=we2_bf[e * 512:(e + 1) * 512, :],
                                   in_=we2[e * 512:(e + 1) * 512, :]), 16)
                cast[("we2", e)] = sC.n
            if STAGES >= 2:
                for cb in range(6):
                    sC.inc(g.dma_start(out=wqkv_sw[cb * 1024:(cb + 1) * 1024, :],
                                       in_=wqkv_s[cb * 1024:(cb + 1) * 1024, :]), 16)
                    cast[("wqkvb", cb)] = sC.n
                sC.inc(g.dma_start(out=wproj_bf, in_=w_proj), 16)
                cast["wproj"] = sC.n
        block.gpsimd(_)


        def _(t):
            t.wait_ge(sT.h, M["consts"])
            sPE.inc(t.matmul(pu[:, 0:NE], or_f[:, :], bias_sb[:, :], start=True, stop=True))
        block.tensor(_)
        M["biasmm"] = sPE.n

        def _(v):
            v.wait_ge(sPE.h, M["biasmm"])
            sDV.inc(v.tensor_copy(biasb[:, :], pu[:, 0:NE]))
        block.vector(_)

        M["pu_free"] = [(sDV, sDV.n)]
        M["po_free"] = []
        M["pb_free"] = [[] for _ in range(4)]
        M["bufK_free"] = []
        M["bufV_free"] = []
        M["xlnT_free"] = []
        M["xres_w"] = []
        M["wA_extra"] = []
        M["wB_extra"] = []
        M["wC_extra"] = []
        wa_idx = [0, 0]
        wb_idx = [0, 0]
        wc_idx = [0, 0]

        def wt(eng, mks):
            for s, n in mks:
                if n > 0:
                    eng.wait_ge(s.h, n)

        def ln_phase():
            ag = []

            def _(v):
                v.wait_ge(sT.h, M["consts"])
                wt(v, M["bufV_free"])
                wt(v, M["xres_w"])
                for j in range(TC):
                    for gq in range(4):
                        sDV.inc(v.bn_stats(stats[:, 6 * gq:6 * gq + 6],
                                           xres[:, j * H + 512 * gq:j * H + 512 * (gq + 1)]))
                    v.wait_ge(sDV.h, sDV.n)
                    sDV.inc(v.bn_aggr(mvall[:, 2 * j:2 * j + 2], stats[:, :]))
                    v.wait_ge(sDV.h, sDV.n)
                    ag.append(sDV.n)
                    sDV.inc(v.tensor_scalar_mul(negmean[:, j:j + 1],
                                                mvall[:, 2 * j:2 * j + 1], -1.0))
            block.vector(_)
            neg_mark = sDV.n

            sq = []

            def _(s):
                s.wait_ge(sT.h, M["consts"])
                for j in range(TC):
                    s.wait_ge(sDV.h, ag[j])
                    sAC.inc(s.activation(sdall[:, j:j + 1], mvall[:, 2 * j + 1:2 * j + 2],
                                         AF.Sqrt, bias=epsln[:, :]))
                    sq.append(sAC.n)
            block.scalar(_)

            xm = []

            def _(v):
                v.wait_ge(sDV.h, neg_mark)
                for j in range(TC):
                    v.wait_ge(sAC.h, sq[j])
                    sDV.inc(v.reciprocal(rstd[:, j:j + 1], sdall[:, j:j + 1]))
                    v.wait_ge(sDV.h, sDV.n)
                for j in range(TC):
                    sDV.inc(v.tensor_scalar(xln_tm[j][:, :], xres[:, j * H:(j + 1) * H],
                                            negmean[:, j:j + 1], rstd[:, j:j + 1],
                                            ALU.add, ALU.mult))
                    xm.append(sDV.n)
            block.vector(_)

            def _(s):
                for j in range(TC):
                    s.wait_ge(sDV.h, xm[j])
                    sB.inc(s.dma_start(out=xln_dram[j * 128:(j + 1) * 128, :],
                                       in_=xln_tm[j][:, :]), 16)
            block.scalar(_)
            bdone = sB.n

            def _(sy):
                wt(sy, M["xlnT_free"])
                sy.wait_ge(sB.h, bdone)
                for hc in range(HCN):
                    sT.inc(sy.dma_start_transpose(xlnT[:, hc * T:(hc + 1) * T],
                                                  xln_dram[:, hc * 128:(hc + 1) * 128]), 16)
            block.sync(_)
            M["bufV_free"] = [(sB, bdone)]
            return sT.n

        def ffn_phase(xlnT_mark):
            wA_marks = {}
            wB_marks = {}
            wC_marks = {}

            def _(sy):
                wA_ex, M["wA_extra"] = M["wA_extra"], []
                wB_ex, M["wB_extra"] = M["wB_extra"], []
                wC_ex, M["wC_extra"] = M["wC_extra"], []
                wa_idx[1] = wa_idx[0]
                wb_idx[1] = wb_idx[0]
                wc_idx[1] = wc_idx[0]
                for ip, path in enumerate(PATHS):
                    for f in range(FCN):
                        if path == "sh":
                            src1 = ws1_sw[f * 128:(f + 1) * 128, :]
                            src3 = ws3_sw[f * 128:(f + 1) * 128, :]
                            cm = max(cast["ws1"], cast["ws3"])
                        else:
                            base = path * FF
                            src1 = we1_sw[base + f * 128:base + (f + 1) * 128, :]
                            src3 = we3_sw[base + f * 128:base + (f + 1) * 128, :]
                            cm = max(cast[("we1", path)], cast[("we3", path)])
                        sy.wait_ge(sC.h, cm)
                        ia = wa_idx[0]
                        if ia >= 2:
                            sy.wait_ge(sWa.h, ia - 1)
                        if wa_idx[0] - wa_idx[1] < 2:
                            wt(sy, wA_ex)
                        sT.inc(sy.dma_start(out=wbufA[:, (ia % 2) * H:(ia % 2 + 1) * H],
                                            in_=src1), 16)
                        wA_marks[(ip, f)] = sT.n
                        wa_idx[0] += 1
                        ib = wb_idx[0]
                        if ib >= 2:
                            sy.wait_ge(sWb.h, ib - 1)
                        if wb_idx[0] - wb_idx[1] < 2:
                            wt(sy, wB_ex)
                        sT.inc(sy.dma_start(out=wbufB[:, (ib % 2) * H:(ib % 2 + 1) * H],
                                            in_=src3), 16)
                        wB_marks[(ip, f)] = sT.n
                        wb_idx[0] += 1
                    w2src = ws2_bf if path == "sh" else we2_bf[path * 512:(path + 1) * 512, :]
                    cm = cast["ws2"] if path == "sh" else cast[("we2", path)]
                    sy.wait_ge(sC.h, cm)
                    for Q in range(4):
                        ic = wc_idx[0]
                        if ic >= 2:
                            sy.wait_ge(sWc.h, ic - 1)
                        if wc_idx[0] - wc_idx[1] < 2:
                            wt(sy, wC_ex)
                        reg = (ic % 2) * 5632
                        sT.inc(sy.dma_start(
                            out=wbufC[:, reg:reg + 5632],
                            in_=w2src[Q * 128:(Q + 1) * 128, :]), 16)
                        wC_marks[(ip, Q)] = sT.n
                        wc_idx[0] += 1
            block.sync(_)

            gate_pe = []
            sig_pred = [sAC.n + j + 1 for j in range(TC)]

            def _(t):
                t.wait_ge(sT.h, xlnT_mark)
                t.wait_ge(sC.h, cast["gate"])
                wt(t, M["pu_free"])
                for j in range(TC):
                    if j >= 2:
                        t.wait_ge(sAC.h, sig_pred[j - 2])
                    dst = pu[:, 0:NE] if j % 2 == 0 else pu[:, 512:512 + NE]
                    for hc in range(HCN):
                        ins = t.matmul(dst, xlnT[:, hc * T + j * 128:hc * T + (j + 1) * 128],
                                       gateT[:, hc * NE:(hc + 1) * NE],
                                       start=(hc == 0), stop=(hc == HCN - 1))
                    sPE.inc(ins)
                    gate_pe.append(sPE.n)
            block.tensor(_)

            sig = []
            gv_pred = [sDV.n + 20 * (j + 1) for j in range(TC)]

            def _(s):
                for j in range(TC):
                    s.wait_ge(sPE.h, gate_pe[j])
                    if j >= 1:
                        s.wait_ge(sDV.h, gv_pred[j - 1])
                    src = pu[:, 0:NE] if j % 2 == 0 else pu[:, 512:512 + NE]
                    sAC.inc(s.activation(scores[:, :], src, AF.Sigmoid))
                    sig.append(sAC.n)
            block.scalar(_)
            assert sig == sig_pred

            gv = []

            def _(v):
                for j in range(TC):
                    v.wait_ge(sAC.h, sig[j])

                    def stp(ins):
                        sDV.inc(ins)
                        v.wait_ge(sDV.h, sDV.n)
                    stp(v.tensor_add(routing[:, :], scores[:, :], biasb[:, :]))
                    stp(v.reduce_max(idx1[:, :], routing[:, :], axis=AX))
                    stp(v.tensor_single_scalar(eqt[:, :], routing[:, :], idx1[:, :], ALU.is_equal))
                    stp(v.scalar_tensor_tensor(cand[:, :], eqt[:, :], -BIG, iopb_sb[:, :],
                                               ALU.mult, ALU.add))
                    stp(v.tensor_reduce(idx1[:, :], cand[:, :], axis=AX, op=ALU.min))
                    stp(v.tensor_single_scalar(oh1[:, :], iota_sb[:, :], idx1[:, :], ALU.is_equal))
                    stp(v.scalar_tensor_tensor(r2t[:, :], oh1[:, :], -BIG, routing[:, :],
                                               ALU.mult, ALU.add))
                    stp(v.reduce_max(idx1[:, :], r2t[:, :], axis=AX))
                    stp(v.tensor_single_scalar(eqt[:, :], r2t[:, :], idx1[:, :], ALU.is_equal))
                    stp(v.scalar_tensor_tensor(cand[:, :], eqt[:, :], -BIG, iopb_sb[:, :],
                                               ALU.mult, ALU.add))
                    stp(v.tensor_reduce(idx1[:, :], cand[:, :], axis=AX, op=ALU.min))
                    stp(v.tensor_single_scalar(oh2[:, :], iota_sb[:, :], idx1[:, :], ALU.is_equal))
                    stp(v.scalar_tensor_tensor(junk8[:, :], scores[:, :], 1.0, oh1[:, :],
                                               ALU.mult, ALU.mult, accum_out=sel1[:, :]))
                    stp(v.scalar_tensor_tensor(junk8[:, :], scores[:, :], 1.0, oh2[:, :],
                                               ALU.mult, ALU.mult, accum_out=sel2[:, :]))
                    stp(v.tensor_add(gsum[:, :], sel1[:, :], sel2[:, :]))
                    stp(v.reciprocal(ginv[:, :], gsum[:, :]))
                    stp(v.tensor_mul(g1[:, :], sel1[:, :], ginv[:, :]))
                    stp(v.tensor_mul(g2[:, :], sel2[:, :], ginv[:, :]))
                    stp(v.tensor_scalar(junk8[:, :], oh1[:, :], g1[:, :], None, ALU.mult))
                    stp(v.scalar_tensor_tensor(c_all[:, j * NE:(j + 1) * NE], oh2[:, :],
                                               g2[:, :], junk8[:, :], ALU.mult, ALU.add))
                    gv.append(sDV.n)
            block.vector(_)
            assert gv == gv_pred

            ctp = []
            ctc_pred = [sDV.n + j + 1 for j in range(TC)]

            def _(t):
                t.wait_ge(sAC.h, sig[-1])
                for j in range(TC):
                    t.wait_ge(sDV.h, gv[j])
                    if j >= 1:
                        t.wait_ge(sDV.h, ctc_pred[j - 1])
                    sPE.inc(t.transpose(pu[0:NE, 0:128], c_all[:, j * NE:(j + 1) * NE],
                                        id_sb[:, :]))
                    ctp.append(sPE.n)
            block.tensor(_)

            ctc = []

            def _(v):
                for j in range(TC):
                    v.wait_ge(sPE.h, ctp[j])
                    sDV.inc(v.tensor_copy(cT[:, j * 128:(j + 1) * 128], pu[0:NE, 0:128]))
                    ctc.append(sDV.n)
            block.vector(_)
            assert ctc == ctc_pred
            cT_mark = sDV.n

            pu_reader = cT_mark
            out_done = 0
            bufK_war = list(M["bufK_free"])
            prev_tail = [None] * 4
            prev_evac_base = [0]
            for ip, path in enumerate(PATHS):
                is_exp = path != "sh"
                if is_exp:
                    def _(t, e=path, pr=pu_reader):
                        t.wait_ge(sDV.h, max(cT_mark, pr))
                        for half in range(2):
                            sPE.inc(t.matmul(pu[:, half * 512:(half + 1) * 512],
                                             oh_sb[:, e * 128:(e + 1) * 128],
                                             cT[:, half * 512:(half + 1) * 512],
                                             start=True, stop=True))
                    block.tensor(_)
                    cbc_mark = sPE.n

                silu_m = {}
                mul2_m = {}
                h1_m = {}
                h3_m = {}
                for f in range(FCN):
                    def _(t, f=f, ip=ip):
                        t.wait_ge(sT.h, wA_marks[(ip, f)])
                        t.wait_ge(sT.h, wB_marks[(ip, f)])
                        slotA = wbufA[:, ((wa_seq[0] + f) % 2) * H:((wa_seq[0] + f) % 2 + 1) * H]
                        slotB = wbufB[:, ((wb_seq[0] + f) % 2) * H:((wb_seq[0] + f) % 2 + 1) * H]
                        for half in range(2):
                            mk = silu_m.get((f - 1, half))
                            if mk is not None:
                                t.wait_ge(sAC.h, mk)
                            else:
                                wt(t, M["pb_free"][half])
                            for hc in range(HCN):
                                ins = t.matmul(ph1[:, half * 512:(half + 1) * 512],
                                               slotA[:, hc * 128:(hc + 1) * 128],
                                               xlnT[:, hc * T + half * 512:hc * T + (half + 1) * 512],
                                               start=(hc == 0), stop=(hc == HCN - 1))
                            sPE.inc(ins)
                            h1_m[(f, half)] = sPE.n
                        t.wait_ge(sPE.h, h1_m[(f, 1)])
                        sWa.inc(t.nop())
                        for half in range(2):
                            mk = mul2_m.get((f - 1, half))
                            if mk is not None:
                                t.wait_ge(sDV.h, mk)
                            else:
                                wt(t, M["pb_free"][2 + half])
                            for hc in range(HCN):
                                ins = t.matmul(ph3[:, half * 512:(half + 1) * 512],
                                               slotB[:, hc * 128:(hc + 1) * 128],
                                               xlnT[:, hc * T + half * 512:hc * T + (half + 1) * 512],
                                               start=(hc == 0), stop=(hc == HCN - 1))
                            sPE.inc(ins)
                            h3_m[(f, half)] = sPE.n
                        t.wait_ge(sPE.h, h3_m[(f, 1)])
                        sWb.inc(t.nop())
                    block.tensor(_)

                    def _(s, f=f):
                        for half in range(2):
                            s.wait_ge(sPE.h, h1_m[(f, half)])
                            mk = mul2_m.get((f - 1, half))
                            if mk is not None:
                                s.wait_ge(sDV.h, mk)
                            sAC.inc(s.activation(tmp_silu[:, half * 512:(half + 1) * 512],
                                                 ph1[:, half * 512:(half + 1) * 512], AF.Silu))
                            silu_m[(f, half)] = sAC.n
                    block.scalar(_)

                    def _(v, f=f, is_exp=is_exp):
                        for half in range(2):
                            v.wait_ge(sAC.h, silu_m[(f, half)])
                            v.wait_ge(sPE.h, h3_m[(f, half)])
                            if f == 0 and half == 0:
                                if out_done:
                                    v.wait_ge(sPE.h, out_done)
                                else:
                                    wt(v, bufK_war)
                                if is_exp:
                                    v.wait_ge(sPE.h, cbc_mark)
                            ts = tmp_silu[:, half * 512:(half + 1) * 512]
                            if is_exp:
                                sDV.inc(v.tensor_mul(ts, ts, pu[:, half * 512:(half + 1) * 512]))
                                v.wait_ge(sDV.h, sDV.n)
                            sDV.inc(v.tensor_mul(
                                bufK[:, f * T + half * 512:f * T + (half + 1) * 512],
                                ts, ph3[:, half * 512:(half + 1) * 512]))
                            mul2_m[(f, half)] = sDV.n
                    block.vector(_)
                if DEBUG_FLUSH_ACT:
                    def _(v):
                        v.wait_ge(sDV.h, sDV.n)
                        for rep in range(3):
                            sDV.inc(v.tensor_copy(junk8[:, :], bufK[0:128, 0:2 * NE].bitcast(F32)))
                            v.wait_ge(sDV.h, sDV.n)
                    block.vector(_)
                wa_seq[0] += FCN
                wb_seq[0] += FCN
                pu_reader = sDV.n
                act_ready = sDV.n
                M["pb_free"] = [[(sAC, silu_m[(FCN - 1, 0)])], [(sAC, silu_m[(FCN - 1, 1)])],
                                [(sDV, mul2_m[(FCN - 1, 0)])], [(sDV, mul2_m[(FCN - 1, 1)])]]

                tile_m = {}
                evac_m = {}
                for Q in range(4):
                    for jg in range(2):
                        def _(t, Q=Q, jg=jg, ip=ip, act_ready=act_ready):
                            t.wait_ge(sT.h, wC_marks[(ip, Q)])
                            if Q == 0 and jg == 0:
                                t.wait_ge(sDV.h, act_ready)
                                if ip == 0:
                                    wt(t, M["po_free"] + M["pu_free"])
                            reg = ((wc_seq[0] + Q) % 2) * 5632
                            for j in range(jg * 4, jg * 4 + 4):
                                tl = Q * 8 + j
                                bank = tl % 4
                                if DEBUG_SERIAL_OUT and (tl > 0 or prev_evac_base[0]):
                                    t.wait_ge(sDV.h, (act_ready + tl) if tl > 0 else prev_evac_base[0])
                                if tl >= 4:
                                    t.wait_ge(sDV.h, evac_m[tl - 4])
                                elif prev_tail[bank] is not None:
                                    t.wait_ge(sDV.h, prev_tail[bank])
                                dst = (po if bank < 2 else pu)[:, (bank % 2) * 512:(bank % 2) * 512 + 512]
                                for f in range(FCN):
                                    ins = t.matmul(dst,
                                                   bufK[:, f * T + j * 128:f * T + (j + 1) * 128],
                                                   wbufC[:, reg + f * 512:reg + (f + 1) * 512],
                                                   start=(f == 0), stop=(f == FCN - 1))
                                sPE.inc(ins)
                                tile_m[tl] = sPE.n
                            if jg == 1:
                                t.wait_ge(sPE.h, tile_m[Q * 8 + 7])
                                sWc.inc(t.nop())
                        block.tensor(_)

                        def _(v, Q=Q, jg=jg):
                            for j in range(jg * 4, jg * 4 + 4):
                                tl = Q * 8 + j
                                bank = tl % 4
                                v.wait_ge(sPE.h, tile_m[tl])
                                src = (po if bank < 2 else pu)[:, (bank % 2) * 512:(bank % 2) * 512 + 512]
                                dst = xres[:, j * H + Q * 512:j * H + (Q + 1) * 512]
                                sDV.inc(v.scalar_tensor_tensor(dst, src, 0.5, dst,
                                                               ALU.mult, ALU.add))
                                evac_m[tl] = sDV.n
                        block.vector(_)
                wc_seq[0] += 4
                out_done = sPE.n
                prev_evac_base[0] = evac_m[31]
                prev_tail = [evac_m[28], evac_m[29], evac_m[30], evac_m[31]]
                pu_reader = sDV.n
                M["po_free"] = [(sDV, sDV.n)]
                M["pu_free"] = [(sDV, sDV.n)]

            M["bufK_free"] = [(sPE, sPE.n)]
            M["xlnT_free"] = [(sPE, sPE.n)]
            M["xres_w"] = [(sDV, sDV.n)]
            M["bufV_free"] = [(sDV, sDV.n), (sPE, sPE.n)]
            return sDV.n

        wa_seq = [0]
        wb_seq = [0]
        wc_seq = [0]

        def attn_phase(xlnT_mark):
            qkv_marks = {}

            def emit_slabs(fb_list):
                def _(sy):
                    wA_ex, M["wA_extra"] = M["wA_extra"], []
                    for Fb in fb_list:
                        sy.wait_ge(sC.h, cast[("wqkvb", Fb // 8)])
                        ia = wa_idx[0]
                        if ia >= 2:
                            sy.wait_ge(sWa.h, ia - 1)
                        if wa_idx[0] - wa_idx[1] < 2:
                            wt(sy, wA_ex)
                        sT.inc(sy.dma_start(out=wbufA[:, (ia % 2) * H:(ia % 2 + 1) * H],
                                            in_=wqkv_sw[Fb * 128:(Fb + 1) * 128, :]), 16)
                        qkv_marks[Fb] = sT.n
                        wa_idx[0] += 1
                block.sync(_)
            wa_idx[1] = wa_idx[0]
            emit_slabs(list(range(16, 48)))

            def _(g):
                g.wait_ge(sWb.h, wb_seq[0])
                sC.inc(g.dma_start(out=bprojT, in_=b_proj.rearrange("h -> () h")), 16)
                cast["bproj"] = sC.n
            block.gpsimd(_)

            kvb_pred = {}
            for k in range(32):
                kvb_pred[k] = sB.n + (k + 1) * 16
            evac_kv = {}
            tile_kv = {}
            kv_stage_init = [0]

            for k, Fb in enumerate(range(16, 48)):
                def _(t, k=k, Fb=Fb):
                    t.wait_ge(sT.h, qkv_marks[Fb])
                    slot = wbufA[:, ((wa_seq[0] + k) % 2) * H:((wa_seq[0] + k) % 2 + 1) * H]
                    for th in range(2):
                        gt = 2 * k + th
                        if gt >= 4:
                            s_, n_ = evac_kv[gt - 4]
                            t.wait_ge(s_.h, n_)
                        else:
                            wt(t, M["pb_free"][gt])
                        for hc in range(HCN):
                            ins = t.matmul(pbank[gt % 4],
                                           slot[:, hc * 128:(hc + 1) * 128],
                                           xlnT[:, hc * T + th * 512:hc * T + (th + 1) * 512],
                                           start=(hc == 0), stop=(hc == HCN - 1))
                        sPE.inc(ins)
                        tile_kv[gt] = sPE.n
                    t.wait_ge(sPE.h, tile_kv[2 * k + 1])
                    sWa.inc(t.nop())
                block.tensor(_)

                is_k = Fb < 32

                def _(e, k=k, Fb=Fb, is_k=is_k):
                    for th in range(2):
                        gt = 2 * k + th
                        e.wait_ge(sPE.h, tile_kv[gt])
                        if k >= 3:
                            e.wait_ge(sB.h, kvb_pred[k - 3])
                        elif kv_stage_init[0] == 0:
                            e.wait_ge(sWc.h, wc_seq[0])
                        dst = kv_stage[k % 3][:, th * 512:(th + 1) * 512]
                        if is_k:
                            ins = e.activation(dst, pbank[gt % 4], AF.Relu)
                            evac_kv[gt] = (sAC, sAC.inc(ins))
                        else:
                            ins = e.tensor_copy(dst, pbank[gt % 4])
                            evac_kv[gt] = (sDV, sDV.inc(ins))
                    kv_stage_init[0] = 1
                (block.scalar if is_k else block.vector)(_)

                def _(g, k=k, Fb=Fb):
                    for s_, n_ in (evac_kv[2 * k], evac_kv[2 * k + 1]):
                        g.wait_ge(s_.h, n_)
                    sB.inc(g.dma_start(out=kv_dram[(Fb - 16) * 128:(Fb - 15) * 128, :],
                                       in_=kv_stage[k % 3][:, :]), 16)
                block.gpsimd(_)
                assert sB.n == kvb_pred[k]
            wa_seq[0] += 32
            M["pb_free"] = [[evac_kv[60]], [evac_kv[61]], [evac_kv[62]], [evac_kv[63]]]

            if ATTN_UPTO < 2:
                return sDV.n
            kT_marks = []

            def _(sy):
                wt(sy, M["bufK_free"])
                sy.wait_ge(sB.h, kvb_pred[15])
                for j in range(TC):
                    sT.inc(sy.dma_start_transpose(bufK[:, j * H:(j + 1) * H],
                                                  kv_dram[0:H, j * 128:(j + 1) * 128]), 16)
                    kT_marks.append(sT.n)
                wt(sy, M["bufV_free"])
                sy.wait_ge(sB.h, kvb_pred[31])
                for j in range(TC):
                    sT.inc(sy.dma_start_transpose(bufV[:, j * H:(j + 1) * H],
                                                  kv_dram[H:2 * H, j * 128:(j + 1) * 128]), 16)
                    kT_marks.append(sT.n)
            block.sync(_)
            kvT_mark = sT.n
            if ATTN_UPTO >= 4:
                emit_slabs(list(range(16)))

            def _(t):
                t.wait_ge(sT.h, kvT_mark)
                wt(t, M["pu_free"])
                for bank in range(2):
                    first = True
                    for g in range(8):
                        hgrp = bank * 8 + g
                        for q in range(4):
                            hd = 4 * hgrp + q
                            col0 = bank * 512 + g * 33
                            for j in range(TC):
                                t.matmul(pu[32 * q:32 * q + 32, col0:col0 + 32],
                                         bufK[:, j * H + hd * 32:j * H + (hd + 1) * 32],
                                         bufV[:, j * H + hd * 32:j * H + (hd + 1) * 32],
                                         start=first, stop=False,
                                         tile_position=(0, 32 * q))
                                first = False
                                last = (g == 7 and q == 3 and j == TC - 1)
                                ins = t.matmul(pu[32 * q:32 * q + 32, col0 + 32:col0 + 33],
                                               bufK[:, j * H + hd * 32:j * H + (hd + 1) * 32],
                                               oc_bf[:, :],
                                               start=False, stop=last,
                                               tile_position=(0, 32 * q))
                    sPE.inc(ins)
            block.tensor(_)
            vkT_mm_mark = sPE.n

            if ATTN_UPTO < 3:
                return sDV.n
            def _(v):
                v.wait_ge(sPE.h, vkT_mm_mark)
                sDV.inc(v.tensor_copy(vkT_sb[:, 0:264], pu[:, 0:264]))
                sDV.inc(v.tensor_copy(vkT_sb[:, 264:528], pu[:, 512:776]))
            block.vector(_)
            vkT_ev_mark = sDV.n
            M["pu_free"] = [(sDV, sDV.n)]

            def _(g):
                g.wait_ge(sDV.h, vkT_ev_mark)
                sB.inc(g.dma_start(out=cc_in.ap(), in_=vkT_sb[:, :]), 16)
                g.wait_ge(sB.h, sB.n)
                sCC.inc(g.collective_compute(
                    "AllReduce", ALU.add,
                    replica_groups=[[0, 1, 2, 3], [4, 5, 6, 7]],
                    ins=[cc_in.ap().opt()], outs=[cc_out.ap().opt()]))
                g.wait_ge(sCC.h, sCC.n)
                sB.inc(g.dma_start(out=vkT_sb[:, :], in_=cc_out.ap()), 16)
            block.gpsimd(_)
            cc_mark = sB.n

            def _(v):
                v.wait_ge(sB.h, cc_mark)
                sDV.inc(v.tensor_copy(vkT_bf[:, :], vkT_sb[:, :]))
            block.vector(_)
            vkT_bf_mark = sDV.n

            if ATTN_UPTO < 4:
                return sDV.n
            tile_q = {}
            qev = []

            for fb in range(16):
                def _(t, fb=fb):
                    t.wait_ge(sT.h, qkv_marks[fb])
                    slot = wbufA[:, ((wa_seq[0] + fb) % 2) * H:((wa_seq[0] + fb) % 2 + 1) * H]
                    for th in range(2):
                        gt = 2 * fb + th
                        if gt >= 4:
                            s_, n_ = qev[gt - 4]
                            t.wait_ge(s_.h, n_)
                        else:
                            wt(t, M["pb_free"][gt])
                        for hc in range(HCN):
                            ins = t.matmul(pbank[gt % 4],
                                           slot[:, hc * 128:(hc + 1) * 128],
                                           xlnT[:, hc * T + th * 512:hc * T + (th + 1) * 512],
                                           start=(hc == 0), stop=(hc == HCN - 1))
                        sPE.inc(ins)
                        tile_q[gt] = sPE.n
                    t.wait_ge(sPE.h, tile_q[2 * fb + 1])
                    sWa.inc(t.nop())
                block.tensor(_)

                def _(s, fb=fb):
                    for th in range(2):
                        gt = 2 * fb + th
                        s.wait_ge(sPE.h, tile_q[gt])
                        if gt == 0:
                            s.wait_ge(sPE.h, vkT_mm_mark)
                        ins = s.activation(bufK[:, fb * T + th * 512:fb * T + (th + 1) * 512],
                                           pbank[gt % 4], AF.Relu)
                        qev.append((sAC, sAC.inc(ins)))
                block.scalar(_)
            wa_seq[0] += 16
            q_done = sAC.n
            M["pb_free"] = [[qev[28]], [qev[29]], [qev[30]], [qev[31]]]

            if ATTN_UPTO < 5:
                return sDV.n
            ab_pred = [sB.n + (j + 1) * 16 for j in range(TC)]
            epi_m = {}
            otile_m = {}

            for j in range(TC):
                for tp in range(4):
                    def _(t, j=j, tp=tp):
                        if j == 0 and tp == 0:
                            t.wait_ge(sDV.h, vkT_bf_mark)
                            t.wait_ge(sAC.h, q_done)
                            wt(t, M["po_free"])
                        for tt in range(2 * tp, 2 * tp + 2):
                            gi = j * 8 + tt
                            q4, sh = tt % 4, tt // 4
                            if gi >= 2:
                                mk = epi_m.get(gi - 2)
                                if mk is not None:
                                    t.wait_ge(sDV.h, mk)
                            off = (gi % 2) * 512
                            for i in range(8):
                                hgrp = 8 * sh + i
                                ins = t.matmul(
                                    po[:, off + i * 33:off + (i + 1) * 33],
                                    bufK[32 * q4:32 * q4 + 32,
                                         hgrp * T + j * 128:hgrp * T + (j + 1) * 128],
                                    vkT_bf[32 * q4:32 * q4 + 32, hgrp * 33:(hgrp + 1) * 33],
                                    start=(i == 0), stop=(i == 7),
                                    tile_position=(32 * q4, 0))
                            sPE.inc(ins)
                            otile_m[gi] = sPE.n
                    block.tensor(_)

                    def _(v, j=j, tp=tp):
                        if ATTN_UPTO == 41:
                            for tt in range(2 * tp, 2 * tp + 2):
                                gi = j * 8 + tt
                                v.wait_ge(sPE.h, otile_m[gi])
                                sDV.inc(v.tensor_copy(dnm[:, :], po[0:128, 0:NE]))
                                epi_m[gi] = sDV.n
                            return
                        for tt in range(2 * tp, 2 * tp + 2):
                            gi = j * 8 + tt
                            q4, sh = tt % 4, tt // 4
                            v.wait_ge(sPE.h, otile_m[gi])
                            off = (gi % 2) * 512
                            p3 = po[:, off:off + 264].rearrange("p (h e) -> p h e", e=33)
                            sDV.inc(v.tensor_single_scalar(dnm[:, :], p3[:, :, 32:33].rearrange("p h e -> p (h e)"),
                                                           EPS_AT, ALU.add))
                            v.wait_ge(sDV.h, sDV.n)
                            sDV.inc(v.reciprocal(rcp[:, :], dnm[:, :]))
                            v.wait_ge(sDV.h, sDV.n)
                            base = j * H + 1024 * sh
                            ov = bufV[:, base:base + 1024].rearrange("p (i r) -> p i r", r=128)
                            sDV.inc(v.tensor_tensor(
                                ov[:, :, 32 * q4:32 * q4 + 32],
                                p3[:, :, 0:32],
                                rcp[:, :].rearrange("p h -> p h ()").broadcast_to([128, NE, 32]),
                                ALU.mult))
                            epi_m[gi] = sDV.n
                    block.vector(_)

                def _(g, j=j):
                    if ATTN_UPTO in (41, 42):
                        return
                    g.wait_ge(sDV.h, epi_m[j * 8 + 7])
                    sB.inc(g.dma_start(out=xln_dram[j * 128:(j + 1) * 128, :],
                                       in_=bufV[:, j * H:(j + 1) * H]), 16)
                block.gpsimd(_)
                assert ATTN_UPTO <= 42 or sB.n == ab_pred[j]
            M["po_free"] = [(sDV, sDV.n)]
            attn_out_done = sPE.n

            if ATTN_UPTO < 6:
                return sDV.n
            def _(sy):
                sy.wait_ge(sPE.h, attn_out_done)
                sy.wait_ge(sB.h, ab_pred[-1])
                for hc in range(HCN):
                    sT.inc(sy.dma_start_transpose(xlnT[:, hc * T:(hc + 1) * T],
                                                  xln_dram[:, hc * 128:(hc + 1) * 128]), 16)
            block.sync(_)
            afm_mark = sT.n

            pset = {}

            def _(sy):
                sy.wait_ge(sC.h, cast["wproj"])
                wc_idx[1] = wc_idx[0]
                for Q in range(4):
                    for jh in range(2):
                        for ch in range(2):
                            ic = wc_idx[0]
                            if ic >= 2:
                                sy.wait_ge(sWc.h, ic - 1)
                            reg = (ic % 2) * 5632
                            sT.inc(sy.dma_start(
                                out=wbufC[:, reg:reg + 4096],
                                in_=wproj_bf[(Q * 2 + ch) * 128:(Q * 2 + ch + 1) * 128, :]), 16)
                            pset[(Q, jh, ch)] = sT.n
                            wc_idx[0] += 1
            block.sync(_)

            pj_evac = {}
            for Q in range(4):
                for jh in range(2):
                    ptile = {}

                    def _(t, Q=Q, jh=jh):
                        t.wait_ge(sT.h, pset[(Q, jh, 0)])
                        if Q == 0 and jh == 0:
                            t.wait_ge(sT.h, afm_mark)
                            t.wait_ge(sC.h, cast["bproj"])
                            wt(t, M["pb_free"][0] + M["pb_free"][1]
                               + M["pb_free"][2] + M["pb_free"][3])
                        else:
                            prev = (Q, jh - 1) if jh else (Q - 1, 1)
                            for jj in range(4):
                                t.wait_ge(sDV.h, pj_evac[(prev[0], prev[1], jj)])
                        for ch in range(2):
                            if ch == 1:
                                t.wait_ge(sT.h, pset[(Q, jh, 1)])
                            reg = ((wc_seq[0] + 2 * (2 * Q + jh) + ch) % 2) * 5632
                            for jj in range(4):
                                j = jh * 4 + jj
                                for cc in range(8):
                                    c0 = 8 * ch + cc
                                    ins = t.matmul(
                                        pbank[jj],
                                        xlnT[:, c0 * T + j * 128:c0 * T + (j + 1) * 128],
                                        wbufC[:, reg + cc * 512:reg + (cc + 1) * 512],
                                        start=(ch == 0 and cc == 0), stop=False)
                                if ch == 1:
                                    ins = t.matmul(pbank[jj], or_bf[:, :],
                                                   bprojT[0:1, Q * 512:(Q + 1) * 512],
                                                   start=False, stop=(cc == 7))
                                sPE.inc(ins)
                                ptile[jj] = sPE.n
                            if ch == 1:
                                t.wait_ge(sPE.h, ptile[3])
                                sWc.inc(t.nop(), 2)
                    block.tensor(_)

                    def _(v, Q=Q, jh=jh):
                        for jj in range(4):
                            j = jh * 4 + jj
                            v.wait_ge(sPE.h, ptile[jj])
                            dst = xres[:, j * H + Q * 512:j * H + (Q + 1) * 512]
                            sDV.inc(v.tensor_add(dst, dst, pbank[jj]))
                            pj_evac[(Q, jh, jj)] = sDV.n
                    block.vector(_)
            wc_seq[0] += 16
            proj_done_pe = sPE.n
            M["pb_free"] = [[(sDV, pj_evac[(3, 1, jj)])] for jj in range(4)]
            M["bufK_free"] = [(sPE, attn_out_done)]
            M["xlnT_free"] = [(sPE, proj_done_pe)]
            M["xres_w"] = [(sDV, sDV.n)]
            M["bufV_free"] = [(sB, ab_pred[-1])]
            M["wB_extra"] = [(sPE, proj_done_pe)]
            M["wC_extra"] = [(sB, kvb_pred[31])]
            return sDV.n

        m1 = ln_phase()
        ffn_phase(m1)
        if DEBUG_DUMP:
            def _(sy):
                wt(sy, M["xres_w"])
                wt(sy, M["bufK_free"])
                sT.inc(sy.dma_start(out=dbg_actK, in_=bufK[:, :]), 16)
                sT.inc(sy.dma_start(out=dbg_bufV, in_=bufV[:, :]), 16)
                sT.inc(sy.dma_start(out=dbg_call, in_=c_all[:, :]), 16)
                sT.inc(sy.dma_start(out=dbg_xlnT, in_=xlnT[:, :]), 16)
                sy.wait_ge(sT.h, sT.n)
            block.sync(_)
        if STAGES >= 2:
            m2 = ln_phase()
            attn_phase(m2)
        if STAGES >= 3:
            m3 = ln_phase()
            ffn_phase(m3)

        def _(sy):
            wt(sy, M["xres_w"])
            sT.inc(sy.dma_start(out=out.rearrange("(j p) h -> p j h", p=128),
                                in_=xres[:, :].rearrange("p (j h) -> p j h", h=H)), 16)
            sy.wait_ge(sT.h, sT.n)
        block.sync(_)

    return nc


_NC_CACHE = {}


def host_swz(w):
    R = w.shape[0]
    return np.ascontiguousarray(
        w.reshape(R // 128, 128, HCN, 128).transpose(0, 3, 2, 1).reshape(R, HCN * 128))


def host_w2(w):
    return np.ascontiguousarray(
        w.reshape(4, 512, FCN, 128).transpose(0, 3, 2, 1).reshape(512, FCN * 512))


def host_proj(w):
    return np.ascontiguousarray(
        w.reshape(4, 512, 2, 8, 128).transpose(0, 2, 4, 3, 1).reshape(1024, 4096))


def kernel(**inputs):
    x = np.asarray(inputs["L0_attn_x"])
    if "prog" not in _NC_CACHE:
        _NC_CACHE["prog"] = build_program()
    nc = _NC_CACHE["prog"]
    we2_np = np.asarray(inputs["L0_ffn_we2"])
    common = {
        "wqkv_s": host_swz(np.asarray(inputs["L0_attn_w_qkv"])),
        "w_proj": host_proj(np.asarray(inputs["L0_attn_w_proj"])),
        "b_proj": inputs["L0_attn_b_proj"],
        "g_bias": inputs["L0_ffn_bias"],
        "gate_wt": np.asarray(inputs["L0_ffn_gate_w"]).T,
        "ws1_s": host_swz(np.asarray(inputs["L0_ffn_ws1"])),
        "ws2": host_w2(np.asarray(inputs["L0_ffn_ws2"])),
        "ws3_s": host_swz(np.asarray(inputs["L0_ffn_ws3"])),
        "we1_s": host_swz(np.asarray(inputs["L0_ffn_we1"]).reshape(-1, H)),
        "we2": np.concatenate([host_w2(we2_np[e]) for e in range(NE)], axis=0),
        "we3_s": host_swz(np.asarray(inputs["L0_ffn_we3"]).reshape(-1, H)),
    }
    common = {k: np.ascontiguousarray(v) for k, v in common.items()}
    in_maps = []
    for c in range(8):
        m = dict(common)
        m["x"] = np.ascontiguousarray(x[c // 4, (c % 4) * T:(c % 4 + 1) * T, :])
        in_maps.append(m)
    res = run_bass_kernel_spmd(nc, in_maps, core_ids=list(range(8)))
    y = np.empty((2, 4096, H), np.float32)
    for c in range(8):
        y[c // 4, (c % 4) * T:(c % 4 + 1) * T, :] = res.results[c]["out"]
    return y


# version 03
# speedup vs baseline: 1.0772x; 1.0648x over previous
import contextlib
import numpy as np
import ml_dtypes
import concourse.bass as bass
import concourse.mybir as mybir
from concourse.bass_utils import run_bass_kernel_spmd

F32 = mybir.dt.float32
BF16 = mybir.dt.bfloat16
AF = mybir.ActivationFunctionType
ALU = mybir.AluOpType
AX = mybir.AxisListType.X

T = 1024
TC = 8
H = 2048
HCN = 16
FF = 1408
FCN = 11
NE = 8
BIG = 1.0e4
EPS_LN = 1.0e-5
EPS_AT = 1.0e-15
STAGES = 3
ATTN_UPTO = 9
DEBUG_DUMP = False
DEBUG_SERIAL_OUT = False
DEBUG_FLUSH_ACT = False
PATHS = ["sh"] + list(range(NE))


class Sem:
    def __init__(self, handle):
        self.h = handle
        self.n = 0

    def inc(self, instr, k=1):
        instr.then_inc(self.h, k)
        self.n += k
        return self.n


def build_program():
    nc = bass.Bass("TRN2", target_bir_lowering=False, debug=False)

    x_in = nc.dram_tensor("x", [T, H], F32, kind="ExternalInput").ap()
    out = nc.dram_tensor("out", [T, H], F32, kind="ExternalOutput").ap()
    wqkv_s = nc.dram_tensor("wqkv_s", [3 * H, H], F32, kind="ExternalInput").ap()
    w_proj = nc.dram_tensor("w_proj", [8 * 128, 8 * 512], F32, kind="ExternalInput").ap()
    b_proj = nc.dram_tensor("b_proj", [H], F32, kind="ExternalInput").ap()
    gate_wt = nc.dram_tensor("gate_wt", [H, NE], F32, kind="ExternalInput").ap()
    g_bias = nc.dram_tensor("g_bias", [NE], F32, kind="ExternalInput").ap()
    ws1_s = nc.dram_tensor("ws1_s", [FF, H], F32, kind="ExternalInput").ap()
    ws2 = nc.dram_tensor("ws2", [512, FCN * 512], F32, kind="ExternalInput").ap()
    ws3_s = nc.dram_tensor("ws3_s", [FF, H], F32, kind="ExternalInput").ap()
    we1_s = nc.dram_tensor("we1_s", [NE * FF, H], F32, kind="ExternalInput").ap()
    we2 = nc.dram_tensor("we2", [NE * 512, FCN * 512], F32, kind="ExternalInput").ap()
    we3_s = nc.dram_tensor("we3_s", [NE * FF, H], F32, kind="ExternalInput").ap()

    ws1_sw = nc.dram_tensor("ws1_sw", [FF, H], BF16).ap()
    ws3_sw = nc.dram_tensor("ws3_sw", [FF, H], BF16).ap()
    we1_sw = nc.dram_tensor("we1_sw", [NE * FF, H], BF16).ap()
    we3_sw = nc.dram_tensor("we3_sw", [NE * FF, H], BF16).ap()
    wqkv_sw = nc.dram_tensor("wqkv_sw", [3 * H, H], BF16).ap()
    ws2_bf = nc.dram_tensor("ws2_bf", [512, FCN * 512], BF16).ap()
    we2_bf = nc.dram_tensor("we2_bf", [NE * 512, FCN * 512], BF16).ap()
    wproj_bf = nc.dram_tensor("wproj_bf", [8 * 128, 8 * 512], BF16).ap()
    kv_dram = nc.dram_tensor("kv_dram", [2 * H, T], BF16).ap()
    cc_in = nc.dram_tensor("cc_in", [128, 528], F32)
    cc_out = nc.dram_tensor("cc_out", [128, 528], F32)
    if DEBUG_DUMP:
        dbg_actK = nc.dram_tensor("dbg_actK", [128, HCN * T], BF16, kind="ExternalOutput").ap()
        dbg_bufV = nc.dram_tensor("dbg_bufV", [128, HCN * T], BF16, kind="ExternalOutput").ap()
        dbg_call = nc.dram_tensor("dbg_call", [128, TC * NE], F32, kind="ExternalOutput").ap()
        dbg_xlnT = nc.dram_tensor("dbg_xlnT", [128, HCN * T], BF16, kind="ExternalOutput").ap()

    ident_f = nc.inline_tensor(np.eye(128, dtype=np.float32), "ident_f").ap()
    ident_bf = nc.inline_tensor(np.eye(128, dtype=np.float32).astype(ml_dtypes.bfloat16),
                                "ident_bf").ap()
    oh_np = np.zeros((NE, NE * 128), dtype=np.float32)
    for e in range(NE):
        oh_np[e, e * 128:(e + 1) * 128] = 1.0
    onehot_f = nc.inline_tensor(oh_np, "onehot_f").ap()
    ones_col_bf = nc.inline_tensor(np.ones((128, 1), dtype=ml_dtypes.bfloat16), "c_oc_bf").ap()
    ones_row_bf = nc.inline_tensor(np.ones((1, 128), dtype=ml_dtypes.bfloat16), "c_or_bf").ap()
    ones_row_f = nc.inline_tensor(np.ones((1, 128), dtype=np.float32), "c_or_f").ap()
    iota_np = np.tile(np.arange(NE, dtype=np.float32)[None, :], (128, 1))
    iota_c = nc.inline_tensor(iota_np, "iota_c").ap()
    iopb_c = nc.inline_tensor(iota_np + BIG, "iopb_c").ap()
    epsln_c = nc.inline_tensor(np.full((128, 1), EPS_LN, dtype=np.float32), "epsln_c").ap()

    with contextlib.ExitStack() as st:
        def sbt(name, shape, dtype):
            return st.enter_context(nc.sbuf_tensor(name, shape, dtype))

        xres = sbt("xres", [128, TC * H], F32)
        xlnT = sbt("xlnT", [128, HCN * T], BF16)
        bufK = sbt("bufK", [128, HCN * T], BF16)
        bufV = sbt("bufV", [128, HCN * T], BF16)
        wbufA = sbt("wbufA", [128, 2 * H], BF16)
        wbufB = sbt("wbufB", [128, 2 * H], BF16)
        wbufC = sbt("wbufC", [128, FCN * 1024], BF16)
        gateT = sbt("gateT", [128, HCN * NE], BF16)
        id_sb = sbt("id_sb", [128, 128], F32)
        id_bf = sbt("id_bf", [128, 128], BF16)
        oh_sb = sbt("oh_sb", [NE, NE * 128], F32)
        oc_bf = sbt("oc_bf", [128, 1], BF16)
        or_bf = sbt("or_bf", [1, 128], BF16)
        or_f = sbt("or_f", [1, 128], F32)
        iota_sb = sbt("iota_sb", [128, NE], F32)
        iopb_sb = sbt("iopb_sb", [128, NE], F32)
        bias_sb = sbt("bias_sb", [1, NE], F32)
        biasb = sbt("biasb", [128, NE], F32)
        stats = sbt("stats", [128, 24], F32)
        mvall = sbt("mvall", [128, 2 * TC], F32)
        negmean = sbt("negmean", [128, TC], F32)
        sdall = sbt("sdall", [128, TC], F32)
        rstd = sbt("rstd", [128, TC], F32)
        scores = sbt("scores", [128, NE], F32)
        routing = sbt("routing", [128, NE], F32)
        eqt = sbt("eqt", [128, NE], F32)
        cand = sbt("cand", [128, NE], F32)
        idx1 = sbt("idx1", [128, 1], F32)
        oh1 = sbt("oh1", [128, NE], F32)
        oh2 = sbt("oh2", [128, NE], F32)
        junk8 = sbt("junk8", [128, NE], F32)
        r2t = sbt("r2t", [128, NE], F32)
        sel1 = sbt("sel1", [128, 1], F32)
        sel2 = sbt("sel2", [128, 1], F32)
        gsum = sbt("gsum", [128, 1], F32)
        ginv = sbt("ginv", [128, 1], F32)
        g1 = sbt("g1", [128, 1], F32)
        g2 = sbt("g2", [128, 1], F32)
        c_all = sbt("c_all", [128, TC * NE], F32)
        dnm = sbt("dnm", [128, NE], F32)
        epsln = sbt("epsln", [128, 1], F32)
        rcp = sbt("rcp", [128, NE], F32)

        xln_tm = [bufV[:, j * H:(j + 1) * H] for j in range(TC)]
        tmp_silu = bufV[:, 2 * H:3 * H].bitcast(F32)
        cT = bufV[0:NE, 3 * H:4 * H].bitcast(F32)
        kv_stage = [wbufC[:, 8192 + i * 1024:8192 + (i + 1) * 1024] for i in range(3)]
        bprojT = wbufB[0:1, 0:H]
        vkT_sb = wbufB[:, H:H + 1056].bitcast(F32)
        vkT_bf = wbufB[:, H + 1056:H + 1584]

        ph1 = st.enter_context(nc.psum_tensor("ph1", [128, 1024], F32))
        ph3 = st.enter_context(nc.psum_tensor("ph3", [128, 1024], F32))
        po = st.enter_context(nc.psum_tensor("po", [128, 1024], F32))
        pu = st.enter_context(nc.psum_tensor("pu", [128, 1024], F32))
        pbank = [ph1[:, 0:512], ph1[:, 512:1024], ph3[:, 0:512], ph3[:, 512:1024]]

        sC = Sem(st.enter_context(nc.semaphore("sC")))
        sB = Sem(st.enter_context(nc.semaphore("sB")))
        sT = Sem(st.enter_context(nc.semaphore("sT")))
        sPE = Sem(st.enter_context(nc.semaphore("sPE")))
        sDV = Sem(st.enter_context(nc.semaphore("sDV")))
        sAC = Sem(st.enter_context(nc.semaphore("sAC")))
        sCC = Sem(st.enter_context(nc.semaphore("sCC")))
        sWa = Sem(st.enter_context(nc.semaphore("sWa")))
        sWb = Sem(st.enter_context(nc.semaphore("sWb")))
        sWc = Sem(st.enter_context(nc.semaphore("sWc")))

        block = st.enter_context(nc.Block())

        M = {}

        def _(sy):
            for dst, src in ((id_sb[:, :], ident_f), (id_bf[:, :], ident_bf),
                             (oh_sb[:, :], onehot_f),
                             (oc_bf[:, :], ones_col_bf), (or_bf[:, :], ones_row_bf),
                             (or_f[:, :], ones_row_f), (iota_sb[:, :], iota_c),
                             (iopb_sb[:, :], iopb_c), (epsln[:, :], epsln_c),
                             (bias_sb[:, :], g_bias.rearrange("e -> () e"))):
                sT.inc(sy.dma_start(out=dst, in_=src), 16)
            sT.inc(sy.dma_start(out=xres[:, :].rearrange("p (j h) -> p j h", h=H),
                                in_=x_in.rearrange("(j p) h -> p j h", p=128)), 16)
        block.sync(_)
        M["consts"] = sT.n

        cast = {}

        def _(g):
            sC.inc(g.dma_start(
                out=gateT[:, :].rearrange("p (hc e) -> p hc e", e=NE),
                in_=gate_wt.rearrange("(hc p) e -> p hc e", p=128)), 16)
            cast["gate"] = sC.n
            sC.inc(g.dma_start(out=ws1_sw, in_=ws1_s), 16)
            cast["ws1"] = sC.n
            sC.inc(g.dma_start(out=ws3_sw, in_=ws3_s), 16)
            cast["ws3"] = sC.n
            sC.inc(g.dma_start(out=ws2_bf, in_=ws2), 16)
            cast["ws2"] = sC.n
            for e in range(NE):
                sC.inc(g.dma_start(out=we1_sw[e * FF:(e + 1) * FF, :],
                                   in_=we1_s[e * FF:(e + 1) * FF, :]), 16)
                cast[("we1", e)] = sC.n
                sC.inc(g.dma_start(out=we3_sw[e * FF:(e + 1) * FF, :],
                                   in_=we3_s[e * FF:(e + 1) * FF, :]), 16)
                cast[("we3", e)] = sC.n
                sC.inc(g.dma_start(out=we2_bf[e * 512:(e + 1) * 512, :],
                                   in_=we2[e * 512:(e + 1) * 512, :]), 16)
                cast[("we2", e)] = sC.n
            if STAGES >= 2:
                for cb in range(6):
                    sC.inc(g.dma_start(out=wqkv_sw[cb * 1024:(cb + 1) * 1024, :],
                                       in_=wqkv_s[cb * 1024:(cb + 1) * 1024, :]), 16)
                    cast[("wqkvb", cb)] = sC.n
                sC.inc(g.dma_start(out=wproj_bf, in_=w_proj), 16)
                cast["wproj"] = sC.n
        block.gpsimd(_)


        def _(t):
            t.wait_ge(sT.h, M["consts"])
            sPE.inc(t.matmul(pu[:, 0:NE], or_f[:, :], bias_sb[:, :], start=True, stop=True))
        block.tensor(_)
        M["biasmm"] = sPE.n

        def _(v):
            v.wait_ge(sPE.h, M["biasmm"])
            sDV.inc(v.tensor_copy(biasb[:, :], pu[:, 0:NE]))
        block.vector(_)

        M["pu_free"] = [(sDV, sDV.n)]
        M["po_free"] = []
        M["pb_free"] = [[] for _ in range(4)]
        M["bufK_free"] = []
        M["bufV_free"] = []
        M["xlnT_free"] = []
        M["xres_w"] = []
        M["wA_extra"] = []
        M["wB_extra"] = []
        M["wC_extra"] = []
        wa_idx = [0, 0]
        wb_idx = [0, 0]
        wc_idx = [0, 0]

        def wt(eng, mks):
            for s, n in mks:
                if n > 0:
                    eng.wait_ge(s.h, n)

        def ln_phase():
            ag = []

            def _(v):
                v.wait_ge(sT.h, M["consts"])
                wt(v, M["bufV_free"])
                wt(v, M["xres_w"])
                for j in range(TC):
                    for gq in range(4):
                        sDV.inc(v.bn_stats(stats[:, 6 * gq:6 * gq + 6],
                                           xres[:, j * H + 512 * gq:j * H + 512 * (gq + 1)]))
                    v.wait_ge(sDV.h, sDV.n)
                    sDV.inc(v.bn_aggr(mvall[:, 2 * j:2 * j + 2], stats[:, :]))
                    v.wait_ge(sDV.h, sDV.n)
                    ag.append(sDV.n)
                    sDV.inc(v.tensor_scalar_mul(negmean[:, j:j + 1],
                                                mvall[:, 2 * j:2 * j + 1], -1.0))
            block.vector(_)
            neg_mark = sDV.n

            sq = []

            def _(s):
                s.wait_ge(sT.h, M["consts"])
                for j in range(TC):
                    s.wait_ge(sDV.h, ag[j])
                    sAC.inc(s.activation(sdall[:, j:j + 1], mvall[:, 2 * j + 1:2 * j + 2],
                                         AF.Sqrt, bias=epsln[:, :]))
                    sq.append(sAC.n)
            block.scalar(_)

            xm = []

            def _(v):
                v.wait_ge(sDV.h, neg_mark)
                for j in range(TC):
                    v.wait_ge(sAC.h, sq[j])
                    sDV.inc(v.reciprocal(rstd[:, j:j + 1], sdall[:, j:j + 1]))
                    v.wait_ge(sDV.h, sDV.n)
                for j in range(TC):
                    sDV.inc(v.tensor_scalar(xln_tm[j][:, :], xres[:, j * H:(j + 1) * H],
                                            negmean[:, j:j + 1], rstd[:, j:j + 1],
                                            ALU.add, ALU.mult))
                    xm.append(sDV.n)
            block.vector(_)

            mark = pe_transpose_to_xlnT(lambda j, hc: xln_tm[j][:, hc * 128:(hc + 1) * 128],
                                        xm)
            M["bufV_free"] = [(sPE, sPE.n)]
            return mark

        def pe_transpose_to_xlnT(src, jm):
            tr_m = {}
            cp_pred = [sDV.n + s + 1 for s in range(32)]

            def _(t):
                wt(t, M["pu_free"])
                for j in range(TC):
                    t.wait_ge(sDV.h, jm[j])
                    for g in range(4):
                        step = j * 4 + g
                        if step >= 2:
                            t.wait_ge(sDV.h, cp_pred[step - 2])
                        off = (step % 2) * 512
                        for i in range(4):
                            hc = g * 4 + i
                            ins = t.matmul(pu[:, off + i * 128:off + (i + 1) * 128],
                                           src(j, hc), id_bf[:, :],
                                           start=True, stop=True)
                        sPE.inc(ins)
                        tr_m[step] = sPE.n
            block.tensor(_)

            cp = []

            def _(v):
                wt(v, M["xlnT_free"])
                for step in range(32):
                    v.wait_ge(sPE.h, tr_m[step])
                    j, g = divmod(step, 4)
                    off = (step % 2) * 512
                    dst = xlnT[:, :].rearrange("p (hc t) -> p hc t", t=T)[
                        :, g * 4:(g + 1) * 4, j * 128:(j + 1) * 128]
                    sDV.inc(v.tensor_copy(
                        dst, pu[:, off:off + 512].rearrange("p (i t) -> p i t", t=128)))
                    cp.append(sDV.n)
            block.vector(_)
            assert cp == cp_pred
            M["pu_free"] = [(sDV, sDV.n)]
            return sDV.n

        def ffn_phase(xlnT_mark):
            wA_marks = {}
            wB_marks = {}
            wC_marks = {}

            def _(sy):
                wA_ex, M["wA_extra"] = M["wA_extra"], []
                wB_ex, M["wB_extra"] = M["wB_extra"], []
                wC_ex, M["wC_extra"] = M["wC_extra"], []
                wa_idx[1] = wa_idx[0]
                wb_idx[1] = wb_idx[0]
                wc_idx[1] = wc_idx[0]
                for ip, path in enumerate(PATHS):
                    for f in range(FCN):
                        if path == "sh":
                            src1 = ws1_sw[f * 128:(f + 1) * 128, :]
                            src3 = ws3_sw[f * 128:(f + 1) * 128, :]
                            cm = max(cast["ws1"], cast["ws3"])
                        else:
                            base = path * FF
                            src1 = we1_sw[base + f * 128:base + (f + 1) * 128, :]
                            src3 = we3_sw[base + f * 128:base + (f + 1) * 128, :]
                            cm = max(cast[("we1", path)], cast[("we3", path)])
                        sy.wait_ge(sC.h, cm)
                        ia = wa_idx[0]
                        if ia >= 2:
                            sy.wait_ge(sWa.h, ia - 1)
                        if wa_idx[0] - wa_idx[1] < 2:
                            wt(sy, wA_ex)
                        sT.inc(sy.dma_start(out=wbufA[:, (ia % 2) * H:(ia % 2 + 1) * H],
                                            in_=src1), 16)
                        wA_marks[(ip, f)] = sT.n
                        wa_idx[0] += 1
                        ib = wb_idx[0]
                        if ib >= 2:
                            sy.wait_ge(sWb.h, ib - 1)
                        if wb_idx[0] - wb_idx[1] < 2:
                            wt(sy, wB_ex)
                        sT.inc(sy.dma_start(out=wbufB[:, (ib % 2) * H:(ib % 2 + 1) * H],
                                            in_=src3), 16)
                        wB_marks[(ip, f)] = sT.n
                        wb_idx[0] += 1
                    w2src = ws2_bf if path == "sh" else we2_bf[path * 512:(path + 1) * 512, :]
                    cm = cast["ws2"] if path == "sh" else cast[("we2", path)]
                    sy.wait_ge(sC.h, cm)
                    for Q in range(4):
                        ic = wc_idx[0]
                        if ic >= 2:
                            sy.wait_ge(sWc.h, ic - 1)
                        if wc_idx[0] - wc_idx[1] < 2:
                            wt(sy, wC_ex)
                        reg = (ic % 2) * 5632
                        sT.inc(sy.dma_start(
                            out=wbufC[:, reg:reg + 5632],
                            in_=w2src[Q * 128:(Q + 1) * 128, :]), 16)
                        wC_marks[(ip, Q)] = sT.n
                        wc_idx[0] += 1
            block.sync(_)

            gate_pe = []
            sig_pred = [sAC.n + j + 1 for j in range(TC)]

            def _(t):
                t.wait_ge(sDV.h, xlnT_mark)
                t.wait_ge(sC.h, cast["gate"])
                wt(t, M["pu_free"])
                for j in range(TC):
                    if j >= 2:
                        t.wait_ge(sAC.h, sig_pred[j - 2])
                    dst = pu[:, 0:NE] if j % 2 == 0 else pu[:, 512:512 + NE]
                    for hc in range(HCN):
                        ins = t.matmul(dst, xlnT[:, hc * T + j * 128:hc * T + (j + 1) * 128],
                                       gateT[:, hc * NE:(hc + 1) * NE],
                                       start=(hc == 0), stop=(hc == HCN - 1))
                    sPE.inc(ins)
                    gate_pe.append(sPE.n)
            block.tensor(_)

            sig = []
            gv_pred = [sDV.n + 20 * (j + 1) for j in range(TC)]

            def _(s):
                for j in range(TC):
                    s.wait_ge(sPE.h, gate_pe[j])
                    if j >= 1:
                        s.wait_ge(sDV.h, gv_pred[j - 1])
                    src = pu[:, 0:NE] if j % 2 == 0 else pu[:, 512:512 + NE]
                    sAC.inc(s.activation(scores[:, :], src, AF.Sigmoid))
                    sig.append(sAC.n)
            block.scalar(_)
            assert sig == sig_pred

            gv = []

            def _(v):
                for j in range(TC):
                    v.wait_ge(sAC.h, sig[j])

                    def stp(ins):
                        sDV.inc(ins)
                        v.wait_ge(sDV.h, sDV.n)
                    stp(v.tensor_add(routing[:, :], scores[:, :], biasb[:, :]))
                    stp(v.reduce_max(idx1[:, :], routing[:, :], axis=AX))
                    stp(v.tensor_single_scalar(eqt[:, :], routing[:, :], idx1[:, :], ALU.is_equal))
                    stp(v.scalar_tensor_tensor(cand[:, :], eqt[:, :], -BIG, iopb_sb[:, :],
                                               ALU.mult, ALU.add))
                    stp(v.tensor_reduce(idx1[:, :], cand[:, :], axis=AX, op=ALU.min))
                    stp(v.tensor_single_scalar(oh1[:, :], iota_sb[:, :], idx1[:, :], ALU.is_equal))
                    stp(v.scalar_tensor_tensor(r2t[:, :], oh1[:, :], -BIG, routing[:, :],
                                               ALU.mult, ALU.add))
                    stp(v.reduce_max(idx1[:, :], r2t[:, :], axis=AX))
                    stp(v.tensor_single_scalar(eqt[:, :], r2t[:, :], idx1[:, :], ALU.is_equal))
                    stp(v.scalar_tensor_tensor(cand[:, :], eqt[:, :], -BIG, iopb_sb[:, :],
                                               ALU.mult, ALU.add))
                    stp(v.tensor_reduce(idx1[:, :], cand[:, :], axis=AX, op=ALU.min))
                    stp(v.tensor_single_scalar(oh2[:, :], iota_sb[:, :], idx1[:, :], ALU.is_equal))
                    stp(v.scalar_tensor_tensor(junk8[:, :], scores[:, :], 1.0, oh1[:, :],
                                               ALU.mult, ALU.mult, accum_out=sel1[:, :]))
                    stp(v.scalar_tensor_tensor(junk8[:, :], scores[:, :], 1.0, oh2[:, :],
                                               ALU.mult, ALU.mult, accum_out=sel2[:, :]))
                    stp(v.tensor_add(gsum[:, :], sel1[:, :], sel2[:, :]))
                    stp(v.reciprocal(ginv[:, :], gsum[:, :]))
                    stp(v.tensor_mul(g1[:, :], sel1[:, :], ginv[:, :]))
                    stp(v.tensor_mul(g2[:, :], sel2[:, :], ginv[:, :]))
                    stp(v.tensor_scalar(junk8[:, :], oh1[:, :], g1[:, :], None, ALU.mult))
                    stp(v.scalar_tensor_tensor(c_all[:, j * NE:(j + 1) * NE], oh2[:, :],
                                               g2[:, :], junk8[:, :], ALU.mult, ALU.add))
                    gv.append(sDV.n)
            block.vector(_)
            assert gv == gv_pred

            ctp = []
            ctc_pred = [sDV.n + j + 1 for j in range(TC)]

            def _(t):
                t.wait_ge(sAC.h, sig[-1])
                for j in range(TC):
                    t.wait_ge(sDV.h, gv[j])
                    if j >= 1:
                        t.wait_ge(sDV.h, ctc_pred[j - 1])
                    sPE.inc(t.transpose(pu[0:NE, 0:128], c_all[:, j * NE:(j + 1) * NE],
                                        id_sb[:, :]))
                    ctp.append(sPE.n)
            block.tensor(_)

            ctc = []

            def _(v):
                for j in range(TC):
                    v.wait_ge(sPE.h, ctp[j])
                    sDV.inc(v.tensor_copy(cT[:, j * 128:(j + 1) * 128], pu[0:NE, 0:128]))
                    ctc.append(sDV.n)
            block.vector(_)
            assert ctc == ctc_pred
            cT_mark = sDV.n

            pu_reader = cT_mark
            out_done = 0
            bufK_war = list(M["bufK_free"])
            prev_tail = [None] * 4
            prev_evac_base = [0]
            for ip, path in enumerate(PATHS):
                is_exp = path != "sh"
                if is_exp:
                    def _(t, e=path, pr=pu_reader):
                        t.wait_ge(sDV.h, max(cT_mark, pr))
                        for half in range(2):
                            sPE.inc(t.matmul(pu[:, half * 512:(half + 1) * 512],
                                             oh_sb[:, e * 128:(e + 1) * 128],
                                             cT[:, half * 512:(half + 1) * 512],
                                             start=True, stop=True))
                    block.tensor(_)
                    cbc_mark = sPE.n

                silu_m = {}
                mul2_m = {}
                h1_m = {}
                h3_m = {}
                for f in range(FCN):
                    def _(t, f=f, ip=ip):
                        t.wait_ge(sT.h, wA_marks[(ip, f)])
                        t.wait_ge(sT.h, wB_marks[(ip, f)])
                        slotA = wbufA[:, ((wa_seq[0] + f) % 2) * H:((wa_seq[0] + f) % 2 + 1) * H]
                        slotB = wbufB[:, ((wb_seq[0] + f) % 2) * H:((wb_seq[0] + f) % 2 + 1) * H]
                        for half in range(2):
                            mk = silu_m.get((f - 1, half))
                            if mk is not None:
                                t.wait_ge(sAC.h, mk)
                            else:
                                wt(t, M["pb_free"][half])
                            for hc in range(HCN):
                                ins = t.matmul(ph1[:, half * 512:(half + 1) * 512],
                                               slotA[:, hc * 128:(hc + 1) * 128],
                                               xlnT[:, hc * T + half * 512:hc * T + (half + 1) * 512],
                                               start=(hc == 0), stop=(hc == HCN - 1))
                            sPE.inc(ins)
                            h1_m[(f, half)] = sPE.n
                        t.wait_ge(sPE.h, h1_m[(f, 1)])
                        sWa.inc(t.nop())
                        for half in range(2):
                            mk = mul2_m.get((f - 1, half))
                            if mk is not None:
                                t.wait_ge(sDV.h, mk)
                            else:
                                wt(t, M["pb_free"][2 + half])
                            for hc in range(HCN):
                                ins = t.matmul(ph3[:, half * 512:(half + 1) * 512],
                                               slotB[:, hc * 128:(hc + 1) * 128],
                                               xlnT[:, hc * T + half * 512:hc * T + (half + 1) * 512],
                                               start=(hc == 0), stop=(hc == HCN - 1))
                            sPE.inc(ins)
                            h3_m[(f, half)] = sPE.n
                        t.wait_ge(sPE.h, h3_m[(f, 1)])
                        sWb.inc(t.nop())
                    block.tensor(_)

                    def _(s, f=f):
                        for half in range(2):
                            s.wait_ge(sPE.h, h1_m[(f, half)])
                            mk = mul2_m.get((f - 1, half))
                            if mk is not None:
                                s.wait_ge(sDV.h, mk)
                            sAC.inc(s.activation(tmp_silu[:, half * 512:(half + 1) * 512],
                                                 ph1[:, half * 512:(half + 1) * 512], AF.Silu))
                            silu_m[(f, half)] = sAC.n
                    block.scalar(_)

                    def _(v, f=f, is_exp=is_exp):
                        for half in range(2):
                            v.wait_ge(sAC.h, silu_m[(f, half)])
                            v.wait_ge(sPE.h, h3_m[(f, half)])
                            if f == 0 and half == 0:
                                if out_done:
                                    v.wait_ge(sPE.h, out_done)
                                else:
                                    wt(v, bufK_war)
                                if is_exp:
                                    v.wait_ge(sPE.h, cbc_mark)
                            ts = tmp_silu[:, half * 512:(half + 1) * 512]
                            if is_exp:
                                sDV.inc(v.tensor_mul(ts, ts, pu[:, half * 512:(half + 1) * 512]))
                                v.wait_ge(sDV.h, sDV.n)
                            sDV.inc(v.tensor_mul(
                                bufK[:, f * T + half * 512:f * T + (half + 1) * 512],
                                ts, ph3[:, half * 512:(half + 1) * 512]))
                            mul2_m[(f, half)] = sDV.n
                    block.vector(_)
                if DEBUG_FLUSH_ACT:
                    def _(v):
                        v.wait_ge(sDV.h, sDV.n)
                        for rep in range(3):
                            sDV.inc(v.tensor_copy(junk8[:, :], bufK[0:128, 0:2 * NE].bitcast(F32)))
                            v.wait_ge(sDV.h, sDV.n)
                    block.vector(_)
                wa_seq[0] += FCN
                wb_seq[0] += FCN
                pu_reader = sDV.n
                act_ready = sDV.n
                M["pb_free"] = [[(sAC, silu_m[(FCN - 1, 0)])], [(sAC, silu_m[(FCN - 1, 1)])],
                                [(sDV, mul2_m[(FCN - 1, 0)])], [(sDV, mul2_m[(FCN - 1, 1)])]]

                tile_m = {}
                evac_m = {}
                for Q in range(4):
                    for jg in range(2):
                        def _(t, Q=Q, jg=jg, ip=ip, act_ready=act_ready):
                            t.wait_ge(sT.h, wC_marks[(ip, Q)])
                            if Q == 0 and jg == 0:
                                t.wait_ge(sDV.h, act_ready)
                                if ip == 0:
                                    wt(t, M["po_free"] + M["pu_free"])
                            reg = ((wc_seq[0] + Q) % 2) * 5632
                            for j in range(jg * 4, jg * 4 + 4):
                                tl = Q * 8 + j
                                bank = tl % 4
                                if DEBUG_SERIAL_OUT and (tl > 0 or prev_evac_base[0]):
                                    t.wait_ge(sDV.h, (act_ready + tl) if tl > 0 else prev_evac_base[0])
                                if tl >= 4:
                                    t.wait_ge(sDV.h, evac_m[tl - 4])
                                elif prev_tail[bank] is not None:
                                    t.wait_ge(sDV.h, prev_tail[bank])
                                dst = (po if bank < 2 else pu)[:, (bank % 2) * 512:(bank % 2) * 512 + 512]
                                for f in range(FCN):
                                    ins = t.matmul(dst,
                                                   bufK[:, f * T + j * 128:f * T + (j + 1) * 128],
                                                   wbufC[:, reg + f * 512:reg + (f + 1) * 512],
                                                   start=(f == 0), stop=(f == FCN - 1))
                                sPE.inc(ins)
                                tile_m[tl] = sPE.n
                            if jg == 1:
                                t.wait_ge(sPE.h, tile_m[Q * 8 + 7])
                                sWc.inc(t.nop())
                        block.tensor(_)

                        def _(v, Q=Q, jg=jg):
                            for j in range(jg * 4, jg * 4 + 4):
                                tl = Q * 8 + j
                                bank = tl % 4
                                v.wait_ge(sPE.h, tile_m[tl])
                                src = (po if bank < 2 else pu)[:, (bank % 2) * 512:(bank % 2) * 512 + 512]
                                dst = xres[:, j * H + Q * 512:j * H + (Q + 1) * 512]
                                sDV.inc(v.scalar_tensor_tensor(dst, src, 0.5, dst,
                                                               ALU.mult, ALU.add))
                                evac_m[tl] = sDV.n
                        block.vector(_)
                wc_seq[0] += 4
                out_done = sPE.n
                prev_evac_base[0] = evac_m[31]
                prev_tail = [evac_m[28], evac_m[29], evac_m[30], evac_m[31]]
                pu_reader = sDV.n
                M["po_free"] = [(sDV, sDV.n)]
                M["pu_free"] = [(sDV, sDV.n)]

            M["bufK_free"] = [(sPE, sPE.n)]
            M["xlnT_free"] = [(sPE, sPE.n)]
            M["xres_w"] = [(sDV, sDV.n)]
            M["bufV_free"] = [(sDV, sDV.n), (sPE, sPE.n)]
            return sDV.n

        wa_seq = [0]
        wb_seq = [0]
        wc_seq = [0]

        def attn_phase(xlnT_mark):
            qkv_marks = {}

            def emit_slabs(fb_list):
                def _(sy):
                    wA_ex, M["wA_extra"] = M["wA_extra"], []
                    for Fb in fb_list:
                        sy.wait_ge(sC.h, cast[("wqkvb", Fb // 8)])
                        ia = wa_idx[0]
                        if ia >= 2:
                            sy.wait_ge(sWa.h, ia - 1)
                        if wa_idx[0] - wa_idx[1] < 2:
                            wt(sy, wA_ex)
                        sT.inc(sy.dma_start(out=wbufA[:, (ia % 2) * H:(ia % 2 + 1) * H],
                                            in_=wqkv_sw[Fb * 128:(Fb + 1) * 128, :]), 16)
                        qkv_marks[Fb] = sT.n
                        wa_idx[0] += 1
                block.sync(_)
            wa_idx[1] = wa_idx[0]
            emit_slabs(list(range(16, 48)))

            def _(g):
                g.wait_ge(sWb.h, wb_seq[0])
                sC.inc(g.dma_start(out=bprojT, in_=b_proj.rearrange("h -> () h")), 16)
                cast["bproj"] = sC.n
            block.gpsimd(_)

            kvb_pred = {}
            for k in range(32):
                kvb_pred[k] = sB.n + (k + 1) * 16
            evac_kv = {}
            tile_kv = {}
            kv_stage_init = [0]

            for k, Fb in enumerate(range(16, 48)):
                def _(t, k=k, Fb=Fb):
                    if k == 0:
                        t.wait_ge(sDV.h, xlnT_mark)
                    t.wait_ge(sT.h, qkv_marks[Fb])
                    slot = wbufA[:, ((wa_seq[0] + k) % 2) * H:((wa_seq[0] + k) % 2 + 1) * H]
                    for th in range(2):
                        gt = 2 * k + th
                        if gt >= 4:
                            s_, n_ = evac_kv[gt - 4]
                            t.wait_ge(s_.h, n_)
                        else:
                            wt(t, M["pb_free"][gt])
                        for hc in range(HCN):
                            ins = t.matmul(pbank[gt % 4],
                                           slot[:, hc * 128:(hc + 1) * 128],
                                           xlnT[:, hc * T + th * 512:hc * T + (th + 1) * 512],
                                           start=(hc == 0), stop=(hc == HCN - 1))
                        sPE.inc(ins)
                        tile_kv[gt] = sPE.n
                    t.wait_ge(sPE.h, tile_kv[2 * k + 1])
                    sWa.inc(t.nop())
                block.tensor(_)

                is_k = Fb < 32

                def _(e, k=k, Fb=Fb, is_k=is_k):
                    for th in range(2):
                        gt = 2 * k + th
                        e.wait_ge(sPE.h, tile_kv[gt])
                        if k >= 3:
                            e.wait_ge(sB.h, kvb_pred[k - 3])
                        elif kv_stage_init[0] == 0:
                            e.wait_ge(sWc.h, wc_seq[0])
                        dst = kv_stage[k % 3][:, th * 512:(th + 1) * 512]
                        if is_k:
                            ins = e.activation(dst, pbank[gt % 4], AF.Relu)
                            evac_kv[gt] = (sAC, sAC.inc(ins))
                        else:
                            ins = e.tensor_copy(dst, pbank[gt % 4])
                            evac_kv[gt] = (sDV, sDV.inc(ins))
                    kv_stage_init[0] = 1
                (block.scalar if is_k else block.vector)(_)

                def _(g, k=k, Fb=Fb):
                    for s_, n_ in (evac_kv[2 * k], evac_kv[2 * k + 1]):
                        g.wait_ge(s_.h, n_)
                    sB.inc(g.dma_start(out=kv_dram[(Fb - 16) * 128:(Fb - 15) * 128, :],
                                       in_=kv_stage[k % 3][:, :]), 16)
                block.gpsimd(_)
                assert sB.n == kvb_pred[k]
            wa_seq[0] += 32
            M["pb_free"] = [[evac_kv[60]], [evac_kv[61]], [evac_kv[62]], [evac_kv[63]]]

            if ATTN_UPTO < 2:
                return sDV.n
            kT_marks = []

            def _(sy):
                wt(sy, M["bufK_free"])
                sy.wait_ge(sB.h, kvb_pred[15])
                for j in range(TC):
                    sT.inc(sy.dma_start_transpose(bufK[:, j * H:(j + 1) * H],
                                                  kv_dram[0:H, j * 128:(j + 1) * 128]), 16)
                    kT_marks.append(sT.n)
                wt(sy, M["bufV_free"])
                sy.wait_ge(sB.h, kvb_pred[31])
                for j in range(TC):
                    sT.inc(sy.dma_start_transpose(bufV[:, j * H:(j + 1) * H],
                                                  kv_dram[H:2 * H, j * 128:(j + 1) * 128]), 16)
                    kT_marks.append(sT.n)
            block.sync(_)
            kvT_mark = sT.n
            if ATTN_UPTO >= 4:
                emit_slabs(list(range(16)))

            def _(t):
                t.wait_ge(sT.h, kvT_mark)
                wt(t, M["pu_free"])
                for bank in range(2):
                    first = True
                    for g in range(8):
                        hgrp = bank * 8 + g
                        for q in range(4):
                            hd = 4 * hgrp + q
                            col0 = bank * 512 + g * 33
                            for j in range(TC):
                                t.matmul(pu[32 * q:32 * q + 32, col0:col0 + 32],
                                         bufK[:, j * H + hd * 32:j * H + (hd + 1) * 32],
                                         bufV[:, j * H + hd * 32:j * H + (hd + 1) * 32],
                                         start=first, stop=False,
                                         tile_position=(0, 32 * q))
                                first = False
                                last = (g == 7 and q == 3 and j == TC - 1)
                                ins = t.matmul(pu[32 * q:32 * q + 32, col0 + 32:col0 + 33],
                                               bufK[:, j * H + hd * 32:j * H + (hd + 1) * 32],
                                               oc_bf[:, :],
                                               start=False, stop=last,
                                               tile_position=(0, 32 * q))
                    sPE.inc(ins)
            block.tensor(_)
            vkT_mm_mark = sPE.n

            if ATTN_UPTO < 3:
                return sDV.n
            def _(v):
                v.wait_ge(sPE.h, vkT_mm_mark)
                sDV.inc(v.tensor_copy(vkT_sb[:, 0:264], pu[:, 0:264]))
                sDV.inc(v.tensor_copy(vkT_sb[:, 264:528], pu[:, 512:776]))
            block.vector(_)
            vkT_ev_mark = sDV.n
            M["pu_free"] = [(sDV, sDV.n)]

            def _(g):
                g.wait_ge(sDV.h, vkT_ev_mark)
                sB.inc(g.dma_start(out=cc_in.ap(), in_=vkT_sb[:, :]), 16)
                g.wait_ge(sB.h, sB.n)
                sCC.inc(g.collective_compute(
                    "AllReduce", ALU.add,
                    replica_groups=[[0, 1, 2, 3], [4, 5, 6, 7]],
                    ins=[cc_in.ap().opt()], outs=[cc_out.ap().opt()]))
                g.wait_ge(sCC.h, sCC.n)
                sB.inc(g.dma_start(out=vkT_sb[:, :], in_=cc_out.ap()), 16)
            block.gpsimd(_)
            cc_mark = sB.n

            def _(v):
                v.wait_ge(sB.h, cc_mark)
                sDV.inc(v.tensor_copy(vkT_bf[:, :], vkT_sb[:, :]))
            block.vector(_)
            vkT_bf_mark = sDV.n

            if ATTN_UPTO < 4:
                return sDV.n
            tile_q = {}
            qev = []

            for fb in range(16):
                def _(t, fb=fb):
                    t.wait_ge(sT.h, qkv_marks[fb])
                    slot = wbufA[:, ((wa_seq[0] + fb) % 2) * H:((wa_seq[0] + fb) % 2 + 1) * H]
                    for th in range(2):
                        gt = 2 * fb + th
                        if gt >= 4:
                            s_, n_ = qev[gt - 4]
                            t.wait_ge(s_.h, n_)
                        else:
                            wt(t, M["pb_free"][gt])
                        for hc in range(HCN):
                            ins = t.matmul(pbank[gt % 4],
                                           slot[:, hc * 128:(hc + 1) * 128],
                                           xlnT[:, hc * T + th * 512:hc * T + (th + 1) * 512],
                                           start=(hc == 0), stop=(hc == HCN - 1))
                        sPE.inc(ins)
                        tile_q[gt] = sPE.n
                    t.wait_ge(sPE.h, tile_q[2 * fb + 1])
                    sWa.inc(t.nop())
                block.tensor(_)

                def _(s, fb=fb):
                    for th in range(2):
                        gt = 2 * fb + th
                        s.wait_ge(sPE.h, tile_q[gt])
                        if gt == 0:
                            s.wait_ge(sPE.h, vkT_mm_mark)
                        ins = s.activation(bufK[:, fb * T + th * 512:fb * T + (th + 1) * 512],
                                           pbank[gt % 4], AF.Relu)
                        qev.append((sAC, sAC.inc(ins)))
                block.scalar(_)
            wa_seq[0] += 16
            q_done = sAC.n
            M["pb_free"] = [[qev[28]], [qev[29]], [qev[30]], [qev[31]]]

            if ATTN_UPTO < 5:
                return sDV.n
            epi_m = {}
            otile_m = {}

            for j in range(TC):
                for tp in range(4):
                    def _(t, j=j, tp=tp):
                        if j == 0 and tp == 0:
                            t.wait_ge(sDV.h, vkT_bf_mark)
                            t.wait_ge(sAC.h, q_done)
                            wt(t, M["po_free"])
                        for tt in range(2 * tp, 2 * tp + 2):
                            gi = j * 8 + tt
                            q4, sh = tt % 4, tt // 4
                            if gi >= 2:
                                mk = epi_m.get(gi - 2)
                                if mk is not None:
                                    t.wait_ge(sDV.h, mk)
                            off = (gi % 2) * 512
                            for i in range(8):
                                hgrp = 8 * sh + i
                                ins = t.matmul(
                                    po[:, off + i * 33:off + (i + 1) * 33],
                                    bufK[32 * q4:32 * q4 + 32,
                                         hgrp * T + j * 128:hgrp * T + (j + 1) * 128],
                                    vkT_bf[32 * q4:32 * q4 + 32, hgrp * 33:(hgrp + 1) * 33],
                                    start=(i == 0), stop=(i == 7),
                                    tile_position=(32 * q4, 0))
                            sPE.inc(ins)
                            otile_m[gi] = sPE.n
                    block.tensor(_)

                    def _(v, j=j, tp=tp):
                        if ATTN_UPTO == 41:
                            for tt in range(2 * tp, 2 * tp + 2):
                                gi = j * 8 + tt
                                v.wait_ge(sPE.h, otile_m[gi])
                                sDV.inc(v.tensor_copy(dnm[:, :], po[0:128, 0:NE]))
                                epi_m[gi] = sDV.n
                            return
                        for tt in range(2 * tp, 2 * tp + 2):
                            gi = j * 8 + tt
                            q4, sh = tt % 4, tt // 4
                            v.wait_ge(sPE.h, otile_m[gi])
                            off = (gi % 2) * 512
                            p3 = po[:, off:off + 264].rearrange("p (h e) -> p h e", e=33)
                            sDV.inc(v.tensor_single_scalar(dnm[:, :], p3[:, :, 32:33].rearrange("p h e -> p (h e)"),
                                                           EPS_AT, ALU.add))
                            v.wait_ge(sDV.h, sDV.n)
                            sDV.inc(v.reciprocal(rcp[:, :], dnm[:, :]))
                            v.wait_ge(sDV.h, sDV.n)
                            base = j * H + 1024 * sh
                            ov = bufV[:, base:base + 1024].rearrange("p (i r) -> p i r", r=128)
                            sDV.inc(v.tensor_tensor(
                                ov[:, :, 32 * q4:32 * q4 + 32],
                                p3[:, :, 0:32],
                                rcp[:, :].rearrange("p h -> p h ()").broadcast_to([128, NE, 32]),
                                ALU.mult))
                            epi_m[gi] = sDV.n
                    block.vector(_)

            M["po_free"] = [(sDV, sDV.n)]
            attn_out_done = sPE.n

            if ATTN_UPTO < 6:
                return sDV.n
            epi_j = [epi_m[j * 8 + 7] for j in range(TC)]
            afm_mark = pe_transpose_to_xlnT(
                lambda j, hc: bufV[:, j * H + hc * 128:j * H + (hc + 1) * 128], epi_j)

            pset = {}

            def _(sy):
                sy.wait_ge(sC.h, cast["wproj"])
                wc_idx[1] = wc_idx[0]
                for Q in range(4):
                    for jh in range(2):
                        for ch in range(2):
                            ic = wc_idx[0]
                            if ic >= 2:
                                sy.wait_ge(sWc.h, ic - 1)
                            reg = (ic % 2) * 5632
                            sT.inc(sy.dma_start(
                                out=wbufC[:, reg:reg + 4096],
                                in_=wproj_bf[(Q * 2 + ch) * 128:(Q * 2 + ch + 1) * 128, :]), 16)
                            pset[(Q, jh, ch)] = sT.n
                            wc_idx[0] += 1
            block.sync(_)

            pj_evac = {}
            for Q in range(4):
                for jh in range(2):
                    ptile = {}

                    def _(t, Q=Q, jh=jh):
                        t.wait_ge(sT.h, pset[(Q, jh, 0)])
                        if Q == 0 and jh == 0:
                            t.wait_ge(sDV.h, afm_mark)
                            t.wait_ge(sC.h, cast["bproj"])
                            wt(t, M["pb_free"][0] + M["pb_free"][1]
                               + M["pb_free"][2] + M["pb_free"][3])
                        else:
                            prev = (Q, jh - 1) if jh else (Q - 1, 1)
                            for jj in range(4):
                                t.wait_ge(sDV.h, pj_evac[(prev[0], prev[1], jj)])
                        for ch in range(2):
                            if ch == 1:
                                t.wait_ge(sT.h, pset[(Q, jh, 1)])
                            reg = ((wc_seq[0] + 2 * (2 * Q + jh) + ch) % 2) * 5632
                            for jj in range(4):
                                j = jh * 4 + jj
                                for cc in range(8):
                                    c0 = 8 * ch + cc
                                    ins = t.matmul(
                                        pbank[jj],
                                        xlnT[:, c0 * T + j * 128:c0 * T + (j + 1) * 128],
                                        wbufC[:, reg + cc * 512:reg + (cc + 1) * 512],
                                        start=(ch == 0 and cc == 0), stop=False)
                                if ch == 1:
                                    ins = t.matmul(pbank[jj], or_bf[:, :],
                                                   bprojT[0:1, Q * 512:(Q + 1) * 512],
                                                   start=False, stop=(cc == 7))
                                sPE.inc(ins)
                                ptile[jj] = sPE.n
                            if ch == 1:
                                t.wait_ge(sPE.h, ptile[3])
                                sWc.inc(t.nop(), 2)
                    block.tensor(_)

                    def _(v, Q=Q, jh=jh):
                        for jj in range(4):
                            j = jh * 4 + jj
                            v.wait_ge(sPE.h, ptile[jj])
                            dst = xres[:, j * H + Q * 512:j * H + (Q + 1) * 512]
                            sDV.inc(v.tensor_add(dst, dst, pbank[jj]))
                            pj_evac[(Q, jh, jj)] = sDV.n
                    block.vector(_)
            wc_seq[0] += 16
            proj_done_pe = sPE.n
            M["pb_free"] = [[(sDV, pj_evac[(3, 1, jj)])] for jj in range(4)]
            M["bufK_free"] = [(sPE, attn_out_done)]
            M["xlnT_free"] = [(sPE, proj_done_pe)]
            M["xres_w"] = [(sDV, sDV.n)]
            M["bufV_free"] = [(sPE, proj_done_pe)]
            M["wB_extra"] = [(sPE, proj_done_pe)]
            M["wC_extra"] = [(sB, kvb_pred[31])]
            return sDV.n

        m1 = ln_phase()
        ffn_phase(m1)
        if DEBUG_DUMP:
            def _(sy):
                wt(sy, M["xres_w"])
                wt(sy, M["bufK_free"])
                sT.inc(sy.dma_start(out=dbg_actK, in_=bufK[:, :]), 16)
                sT.inc(sy.dma_start(out=dbg_bufV, in_=bufV[:, :]), 16)
                sT.inc(sy.dma_start(out=dbg_call, in_=c_all[:, :]), 16)
                sT.inc(sy.dma_start(out=dbg_xlnT, in_=xlnT[:, :]), 16)
                sy.wait_ge(sT.h, sT.n)
            block.sync(_)
        if STAGES >= 2:
            m2 = ln_phase()
            attn_phase(m2)
        if STAGES >= 3:
            m3 = ln_phase()
            ffn_phase(m3)

        def _(sy):
            wt(sy, M["xres_w"])
            sT.inc(sy.dma_start(out=out.rearrange("(j p) h -> p j h", p=128),
                                in_=xres[:, :].rearrange("p (j h) -> p j h", h=H)), 16)
            sy.wait_ge(sT.h, sT.n)
        block.sync(_)

    return nc


_NC_CACHE = {}


def host_swz(w):
    R = w.shape[0]
    return np.ascontiguousarray(
        w.reshape(R // 128, 128, HCN, 128).transpose(0, 3, 2, 1).reshape(R, HCN * 128))


def host_w2(w):
    return np.ascontiguousarray(
        w.reshape(4, 512, FCN, 128).transpose(0, 3, 2, 1).reshape(512, FCN * 512))


def host_proj(w):
    return np.ascontiguousarray(
        w.reshape(4, 512, 2, 8, 128).transpose(0, 2, 4, 3, 1).reshape(1024, 4096))


def kernel(**inputs):
    x = np.asarray(inputs["L0_attn_x"])
    if "prog" not in _NC_CACHE:
        _NC_CACHE["prog"] = build_program()
    nc = _NC_CACHE["prog"]
    we2_np = np.asarray(inputs["L0_ffn_we2"])
    common = {
        "wqkv_s": host_swz(np.asarray(inputs["L0_attn_w_qkv"])),
        "w_proj": host_proj(np.asarray(inputs["L0_attn_w_proj"])),
        "b_proj": inputs["L0_attn_b_proj"],
        "g_bias": inputs["L0_ffn_bias"],
        "gate_wt": np.asarray(inputs["L0_ffn_gate_w"]).T,
        "ws1_s": host_swz(np.asarray(inputs["L0_ffn_ws1"])),
        "ws2": host_w2(np.asarray(inputs["L0_ffn_ws2"])),
        "ws3_s": host_swz(np.asarray(inputs["L0_ffn_ws3"])),
        "we1_s": host_swz(np.asarray(inputs["L0_ffn_we1"]).reshape(-1, H)),
        "we2": np.concatenate([host_w2(we2_np[e]) for e in range(NE)], axis=0),
        "we3_s": host_swz(np.asarray(inputs["L0_ffn_we3"]).reshape(-1, H)),
    }
    common = {k: np.ascontiguousarray(v) for k, v in common.items()}
    in_maps = []
    for c in range(8):
        m = dict(common)
        m["x"] = np.ascontiguousarray(x[c // 4, (c % 4) * T:(c % 4 + 1) * T, :])
        in_maps.append(m)
    res = run_bass_kernel_spmd(nc, in_maps, core_ids=list(range(8)))
    y = np.empty((2, 4096, H), np.float32)
    for c in range(8):
        y[c // 4, (c % 4) * T:(c % 4 + 1) * T, :] = res.results[c]["out"]
    return y


# version 06
# speedup vs baseline: 1.1630x; 1.0797x over previous
import contextlib
import numpy as np
import ml_dtypes
import concourse.bass as bass
import concourse.mybir as mybir
from concourse.bass_utils import run_bass_kernel_spmd

F32 = mybir.dt.float32
BF16 = mybir.dt.bfloat16
AF = mybir.ActivationFunctionType
ALU = mybir.AluOpType
AX = mybir.AxisListType.X

T = 1024
TC = 8
H = 2048
HCN = 16
FF = 1408
FCN = 11
NE = 8
BIG = 1.0e4
EPS_LN = 1.0e-5
EPS_AT = 1.0e-15
STAGES = 3
ATTN_UPTO = 9
DEBUG_DUMP = False
DEBUG_SERIAL_OUT = False
DEBUG_FLUSH_ACT = False
PATHS = ["sh"] + list(range(NE))


class Sem:
    def __init__(self, handle):
        self.h = handle
        self.n = 0

    def inc(self, instr, k=1):
        instr.then_inc(self.h, k)
        self.n += k
        return self.n


def build_program():
    nc = bass.Bass("TRN2", target_bir_lowering=False, debug=False)

    x_in = nc.dram_tensor("x", [T, H], F32, kind="ExternalInput").ap()
    out = nc.dram_tensor("out", [T, H], F32, kind="ExternalOutput").ap()
    wqkv_s = nc.dram_tensor("wqkv_s", [3 * H, H], F32, kind="ExternalInput").ap()
    w_proj = nc.dram_tensor("w_proj", [8 * 128, 8 * 512], F32, kind="ExternalInput").ap()
    b_proj = nc.dram_tensor("b_proj", [H], F32, kind="ExternalInput").ap()
    gate_wt = nc.dram_tensor("gate_wt", [H, NE], F32, kind="ExternalInput").ap()
    g_bias = nc.dram_tensor("g_bias", [NE], F32, kind="ExternalInput").ap()
    ws1_s = nc.dram_tensor("ws1_s", [FF, H], F32, kind="ExternalInput").ap()
    ws2 = nc.dram_tensor("ws2", [512, FCN * 512], F32, kind="ExternalInput").ap()
    ws3_s = nc.dram_tensor("ws3_s", [FF, H], F32, kind="ExternalInput").ap()
    we1_s = nc.dram_tensor("we1_s", [NE * FF, H], F32, kind="ExternalInput").ap()
    we2 = nc.dram_tensor("we2", [NE * 512, FCN * 512], F32, kind="ExternalInput").ap()
    we3_s = nc.dram_tensor("we3_s", [NE * FF, H], F32, kind="ExternalInput").ap()

    ws1_sw = nc.dram_tensor("ws1_sw", [FF, H], BF16).ap()
    ws3_sw = nc.dram_tensor("ws3_sw", [FF, H], BF16).ap()
    we1_sw = nc.dram_tensor("we1_sw", [NE * FF, H], BF16).ap()
    we3_sw = nc.dram_tensor("we3_sw", [NE * FF, H], BF16).ap()
    wqkv_sw = nc.dram_tensor("wqkv_sw", [3 * H, H], BF16).ap()
    ws2_bf = nc.dram_tensor("ws2_bf", [512, FCN * 512], BF16).ap()
    we2_bf = nc.dram_tensor("we2_bf", [NE * 512, FCN * 512], BF16).ap()
    wproj_bf = nc.dram_tensor("wproj_bf", [8 * 128, 8 * 512], BF16).ap()
    kv_dram = nc.dram_tensor("kv_dram", [2 * H, T], BF16).ap()
    cc_in = nc.dram_tensor("cc_in", [128, 528], F32)
    cc_out = nc.dram_tensor("cc_out", [128, 528], F32)
    if DEBUG_DUMP:
        dbg_actK = nc.dram_tensor("dbg_actK", [128, HCN * T], BF16, kind="ExternalOutput").ap()
        dbg_bufV = nc.dram_tensor("dbg_bufV", [128, HCN * T], BF16, kind="ExternalOutput").ap()
        dbg_call = nc.dram_tensor("dbg_call", [128, TC * NE], F32, kind="ExternalOutput").ap()
        dbg_xlnT = nc.dram_tensor("dbg_xlnT", [128, HCN * T], BF16, kind="ExternalOutput").ap()

    ident_f = nc.inline_tensor(np.eye(128, dtype=np.float32), "ident_f").ap()
    ident_bf = nc.inline_tensor(np.eye(128, dtype=np.float32).astype(ml_dtypes.bfloat16),
                                "ident_bf").ap()
    oh_np = np.zeros((NE, NE * 128), dtype=np.float32)
    for e in range(NE):
        oh_np[e, e * 128:(e + 1) * 128] = 1.0
    onehot_f = nc.inline_tensor(oh_np, "onehot_f").ap()
    ones_col_bf = nc.inline_tensor(np.ones((128, 1), dtype=ml_dtypes.bfloat16), "c_oc_bf").ap()
    ones_row_bf = nc.inline_tensor(np.ones((1, 128), dtype=ml_dtypes.bfloat16), "c_or_bf").ap()
    ones_row_f = nc.inline_tensor(np.ones((1, 128), dtype=np.float32), "c_or_f").ap()
    iota_np = np.tile(np.arange(NE, dtype=np.float32)[None, :], (128, 1))
    iota_c = nc.inline_tensor(iota_np, "iota_c").ap()
    iopb_c = nc.inline_tensor(iota_np + BIG, "iopb_c").ap()
    epsln_c = nc.inline_tensor(np.full((128, 1), EPS_LN, dtype=np.float32), "epsln_c").ap()

    with contextlib.ExitStack() as st:
        def sbt(name, shape, dtype):
            return st.enter_context(nc.sbuf_tensor(name, shape, dtype))

        xres = sbt("xres", [128, TC * H], F32)
        xlnT = sbt("xlnT", [128, HCN * T], BF16)
        bufK = sbt("bufK", [128, HCN * T], BF16)
        bufV = sbt("bufV", [128, HCN * T], BF16)
        wbufA = sbt("wbufA", [128, 2 * H], BF16)
        wbufB = sbt("wbufB", [128, 2 * H], BF16)
        wbufC = sbt("wbufC", [128, FCN * 1024], BF16)
        gateT = sbt("gateT", [128, HCN * NE], BF16)
        id_sb = sbt("id_sb", [128, 128], F32)
        id_bf = sbt("id_bf", [128, 128], BF16)
        oh_sb = sbt("oh_sb", [NE, NE * 128], F32)
        oc_bf = sbt("oc_bf", [128, 1], BF16)
        or_bf = sbt("or_bf", [1, 128], BF16)
        or_f = sbt("or_f", [1, 128], F32)
        iota_sb = sbt("iota_sb", [128, NE], F32)
        iopb_sb = sbt("iopb_sb", [128, NE], F32)
        bias_sb = sbt("bias_sb", [1, NE], F32)
        biasb = sbt("biasb", [128, NE], F32)
        stats = sbt("stats", [128, 24], F32)
        mvall = sbt("mvall", [128, 2 * TC], F32)
        negmean = sbt("negmean", [128, TC], F32)
        sdall = sbt("sdall", [128, TC], F32)
        rstd = sbt("rstd", [128, TC], F32)
        scores = sbt("scores", [128, NE], F32)
        routing = sbt("routing", [128, NE], F32)
        eqt = sbt("eqt", [128, NE], F32)
        cand = sbt("cand", [128, NE], F32)
        idx1 = sbt("idx1", [128, 1], F32)
        oh1 = sbt("oh1", [128, NE], F32)
        oh2 = sbt("oh2", [128, NE], F32)
        junk8 = sbt("junk8", [128, NE], F32)
        r2t = sbt("r2t", [128, NE], F32)
        sel1 = sbt("sel1", [128, 1], F32)
        sel2 = sbt("sel2", [128, 1], F32)
        gsum = sbt("gsum", [128, 1], F32)
        ginv = sbt("ginv", [128, 1], F32)
        g1 = sbt("g1", [128, 1], F32)
        g2 = sbt("g2", [128, 1], F32)
        c_all = sbt("c_all", [128, TC * NE], F32)
        dnm = sbt("dnm", [128, NE], F32)
        epsln = sbt("epsln", [128, 1], F32)
        rcp = sbt("rcp", [128, NE], F32)

        xln_tm = [bufV[:, j * H:(j + 1) * H] for j in range(TC)]
        tmp_silu = bufV[:, 2 * H:3 * H].bitcast(F32)
        cT = bufV[0:NE, 3 * H:4 * H].bitcast(F32)
        kv_stage = [wbufC[:, 8192 + i * 1024:8192 + (i + 1) * 1024] for i in range(3)]
        bprojT = wbufB[0:1, 0:H]
        vkT_sb = wbufB[:, H:H + 1056].bitcast(F32)
        vkT_bf = wbufB[:, H + 1056:H + 1584]

        ph1 = st.enter_context(nc.psum_tensor("ph1", [128, 1024], F32))
        ph3 = st.enter_context(nc.psum_tensor("ph3", [128, 1024], F32))
        po = st.enter_context(nc.psum_tensor("po", [128, 1024], F32))
        pu = st.enter_context(nc.psum_tensor("pu", [128, 1024], F32))
        pbank = [ph1[:, 0:512], ph1[:, 512:1024], ph3[:, 0:512], ph3[:, 512:1024]]

        sC = Sem(st.enter_context(nc.semaphore("sC")))
        sB = Sem(st.enter_context(nc.semaphore("sB")))
        sT = Sem(st.enter_context(nc.semaphore("sT")))
        sPE = Sem(st.enter_context(nc.semaphore("sPE")))
        sDV = Sem(st.enter_context(nc.semaphore("sDV")))
        sAC = Sem(st.enter_context(nc.semaphore("sAC")))
        sCC = Sem(st.enter_context(nc.semaphore("sCC")))
        sWa = Sem(st.enter_context(nc.semaphore("sWa")))
        sWb = Sem(st.enter_context(nc.semaphore("sWb")))
        sWc = Sem(st.enter_context(nc.semaphore("sWc")))

        block = st.enter_context(nc.Block())

        M = {}

        def _(sy):
            for dst, src in ((id_sb[:, :], ident_f), (id_bf[:, :], ident_bf),
                             (oh_sb[:, :], onehot_f),
                             (oc_bf[:, :], ones_col_bf), (or_bf[:, :], ones_row_bf),
                             (or_f[:, :], ones_row_f), (iota_sb[:, :], iota_c),
                             (iopb_sb[:, :], iopb_c), (epsln[:, :], epsln_c),
                             (bias_sb[:, :], g_bias.rearrange("e -> () e"))):
                sT.inc(sy.dma_start(out=dst, in_=src), 16)
            sT.inc(sy.dma_start(out=xres[:, :].rearrange("p (j h) -> p j h", h=H),
                                in_=x_in.rearrange("(j p) h -> p j h", p=128)), 16)
        block.sync(_)
        M["consts"] = sT.n

        cast = {}

        def _(g):
            sC.inc(g.dma_start(
                out=gateT[:, :].rearrange("p (hc e) -> p hc e", e=NE),
                in_=gate_wt.rearrange("(hc p) e -> p hc e", p=128)), 16)
            cast["gate"] = sC.n
            sC.inc(g.dma_start(out=ws1_sw, in_=ws1_s), 16)
            cast["ws1"] = sC.n
            sC.inc(g.dma_start(out=ws3_sw, in_=ws3_s), 16)
            cast["ws3"] = sC.n
            sC.inc(g.dma_start(out=ws2_bf, in_=ws2), 16)
            cast["ws2"] = sC.n
            for e in range(NE):
                g0 = 2 if e == 0 else 11 * e
                for nm, dst, src, rows in (("we1", we1_sw, we1_s, FF),
                                           ("we3", we3_sw, we3_s, FF),
                                           ("we2", we2_bf, we2, 512)):
                    for h in range(2):
                        g.wait_ge(sWa.h, g0)
                        g0 += 1
                        r0 = e * rows + h * (rows // 2)
                        sC.inc(g.dma_start(out=dst[r0:r0 + rows // 2, :],
                                           in_=src[r0:r0 + rows // 2, :]), 16)
                    cast[(nm, e)] = sC.n
            if STAGES >= 2:
                for cb in range(6):
                    for h in range(2):
                        g.wait_ge(sWa.h, 44 + 8 * cb + 4 * h)
                        r0 = cb * 1024 + h * 512
                        sC.inc(g.dma_start(out=wqkv_sw[r0:r0 + 512, :],
                                           in_=wqkv_s[r0:r0 + 512, :]), 16)
                    cast[("wqkvb", cb)] = sC.n
                for h in range(2):
                    g.wait_ge(sWa.h, 92 + 2 * h)
                    sC.inc(g.dma_start(out=wproj_bf[h * 512:(h + 1) * 512, :],
                                       in_=w_proj[h * 512:(h + 1) * 512, :]), 16)
                cast["wproj"] = sC.n
        block.gpsimd(_)


        def _(t):
            t.wait_ge(sT.h, M["consts"])
            sPE.inc(t.matmul(pu[:, 0:NE], or_f[:, :], bias_sb[:, :], start=True, stop=True))
        block.tensor(_)
        M["biasmm"] = sPE.n

        def _(v):
            v.wait_ge(sPE.h, M["biasmm"])
            sDV.inc(v.tensor_copy(biasb[:, :], pu[:, 0:NE]))
        block.vector(_)

        M["pu_free"] = [(sDV, sDV.n)]
        M["po_free"] = []
        M["pb_free"] = [[] for _ in range(4)]
        M["bufK_free"] = []
        M["bufV_free"] = []
        M["xlnT_free"] = []
        M["xres_w"] = []
        M["wA_extra"] = []
        M["wB_extra"] = []
        M["wC_extra"] = []
        wa_idx = [0, 0]
        wb_idx = [0, 0]
        wc_idx = [0, 0]

        def wt(eng, mks):
            for s, n in mks:
                if n > 0:
                    eng.wait_ge(s.h, n)

        def ln_phase():
            ag = []

            def _(v):
                v.wait_ge(sT.h, M["consts"])
                wt(v, M["bufV_free"])
                wt(v, M["xres_w"])
                for j in range(TC):
                    for gq in range(4):
                        sDV.inc(v.bn_stats(stats[:, 6 * gq:6 * gq + 6],
                                           xres[:, j * H + 512 * gq:j * H + 512 * (gq + 1)]))
                    v.wait_ge(sDV.h, sDV.n)
                    sDV.inc(v.bn_aggr(mvall[:, 2 * j:2 * j + 2], stats[:, :]))
                    v.wait_ge(sDV.h, sDV.n)
                    ag.append(sDV.n)
                    sDV.inc(v.tensor_scalar_mul(negmean[:, j:j + 1],
                                                mvall[:, 2 * j:2 * j + 1], -1.0))
            block.vector(_)
            neg_mark = sDV.n

            sq = []

            def _(s):
                s.wait_ge(sT.h, M["consts"])
                for j in range(TC):
                    s.wait_ge(sDV.h, ag[j])
                    sAC.inc(s.activation(sdall[:, j:j + 1], mvall[:, 2 * j + 1:2 * j + 2],
                                         AF.Sqrt, bias=epsln[:, :]))
                    sq.append(sAC.n)
            block.scalar(_)

            xm = []

            def _(v):
                v.wait_ge(sDV.h, neg_mark)
                for j in range(TC):
                    v.wait_ge(sAC.h, sq[j])
                    sDV.inc(v.reciprocal(rstd[:, j:j + 1], sdall[:, j:j + 1]))
                    v.wait_ge(sDV.h, sDV.n)
                for j in range(TC):
                    sDV.inc(v.tensor_scalar(xln_tm[j][:, :], xres[:, j * H:(j + 1) * H],
                                            negmean[:, j:j + 1], rstd[:, j:j + 1],
                                            ALU.add, ALU.mult))
                    xm.append(sDV.n)
            block.vector(_)

            mark = pe_transpose_to_xlnT(lambda j, hc: xln_tm[j][:, hc * 128:(hc + 1) * 128],
                                        xm)
            M["bufV_free"] = [(sPE, sPE.n)]
            return mark

        def pe_transpose_to_xlnT(src, jm):
            tr_m = {}
            cp_pred = [sDV.n + s + 1 for s in range(32)]

            def _(t):
                wt(t, M["pu_free"])
                for j in range(TC):
                    t.wait_ge(sDV.h, jm[j])
                    for g in range(4):
                        step = j * 4 + g
                        if step >= 2:
                            t.wait_ge(sDV.h, cp_pred[step - 2])
                        off = (step % 2) * 512
                        for i in range(4):
                            hc = g * 4 + i
                            ins = t.matmul(pu[:, off + i * 128:off + (i + 1) * 128],
                                           src(j, hc), id_bf[:, :],
                                           start=True, stop=True)
                        sPE.inc(ins)
                        tr_m[step] = sPE.n
            block.tensor(_)

            cp = []

            def _(v):
                wt(v, M["xlnT_free"])
                for step in range(32):
                    v.wait_ge(sPE.h, tr_m[step])
                    j, g = divmod(step, 4)
                    off = (step % 2) * 512
                    dst = xlnT[:, :].rearrange("p (hc t) -> p hc t", t=T)[
                        :, g * 4:(g + 1) * 4, j * 128:(j + 1) * 128]
                    sDV.inc(v.tensor_copy(
                        dst, pu[:, off:off + 512].rearrange("p (i t) -> p i t", t=128)))
                    cp.append(sDV.n)
            block.vector(_)
            assert cp == cp_pred
            M["pu_free"] = [(sDV, sDV.n)]
            return sDV.n

        def ffn_phase(xlnT_mark):
            wA_marks = {}
            wB_marks = {}
            wC_marks = {}

            def _(sy):
                wA_ex, M["wA_extra"] = M["wA_extra"], []
                wB_ex, M["wB_extra"] = M["wB_extra"], []
                wC_ex, M["wC_extra"] = M["wC_extra"], []
                wa_idx[1] = wa_idx[0]
                wb_idx[1] = wb_idx[0]
                wc_idx[1] = wc_idx[0]
                for ip, path in enumerate(PATHS):
                    for f in range(FCN):
                        if path == "sh":
                            src1 = ws1_sw[f * 128:(f + 1) * 128, :]
                            src3 = ws3_sw[f * 128:(f + 1) * 128, :]
                            cm = max(cast["ws1"], cast["ws3"])
                        else:
                            base = path * FF
                            src1 = we1_sw[base + f * 128:base + (f + 1) * 128, :]
                            src3 = we3_sw[base + f * 128:base + (f + 1) * 128, :]
                            cm = max(cast[("we1", path)], cast[("we3", path)])
                        sy.wait_ge(sC.h, cm)
                        ia = wa_idx[0]
                        if ia >= 2:
                            sy.wait_ge(sWa.h, ia - 1)
                        if wa_idx[0] - wa_idx[1] < 2:
                            wt(sy, wA_ex)
                        sT.inc(sy.dma_start(out=wbufA[:, (ia % 2) * H:(ia % 2 + 1) * H],
                                            in_=src1), 16)
                        wA_marks[(ip, f)] = sT.n
                        wa_idx[0] += 1
                        ib = wb_idx[0]
                        if ib >= 2:
                            sy.wait_ge(sWb.h, ib - 1)
                        if wb_idx[0] - wb_idx[1] < 2:
                            wt(sy, wB_ex)
                        sT.inc(sy.dma_start(out=wbufB[:, (ib % 2) * H:(ib % 2 + 1) * H],
                                            in_=src3), 16)
                        wB_marks[(ip, f)] = sT.n
                        wb_idx[0] += 1
                    w2src = ws2_bf if path == "sh" else we2_bf[path * 512:(path + 1) * 512, :]
                    cm = cast["ws2"] if path == "sh" else cast[("we2", path)]
                    sy.wait_ge(sC.h, cm)
                    for Q in range(4):
                        ic = wc_idx[0]
                        if ic >= 2:
                            sy.wait_ge(sWc.h, ic - 1)
                        if wc_idx[0] - wc_idx[1] < 2:
                            wt(sy, wC_ex)
                        reg = (ic % 2) * 5632
                        sT.inc(sy.dma_start(
                            out=wbufC[:, reg:reg + 5632],
                            in_=w2src[Q * 128:(Q + 1) * 128, :]), 16)
                        wC_marks[(ip, Q)] = sT.n
                        wc_idx[0] += 1
            block.sync(_)

            gate_pe = []
            sig_pred = [sAC.n + j + 1 for j in range(TC)]

            def _(t):
                t.wait_ge(sDV.h, xlnT_mark)
                t.wait_ge(sC.h, cast["gate"])
                wt(t, M["pu_free"])
                for j in range(TC):
                    if j >= 2:
                        t.wait_ge(sAC.h, sig_pred[j - 2])
                    dst = pu[:, 0:NE] if j % 2 == 0 else pu[:, 512:512 + NE]
                    for hc in range(HCN):
                        ins = t.matmul(dst, xlnT[:, hc * T + j * 128:hc * T + (j + 1) * 128],
                                       gateT[:, hc * NE:(hc + 1) * NE],
                                       start=(hc == 0), stop=(hc == HCN - 1))
                    sPE.inc(ins)
                    gate_pe.append(sPE.n)
            block.tensor(_)

            sig = []
            gv_pred = [sDV.n + 20 * (j + 1) for j in range(TC)]

            def _(s):
                for j in range(TC):
                    s.wait_ge(sPE.h, gate_pe[j])
                    if j >= 1:
                        s.wait_ge(sDV.h, gv_pred[j - 1])
                    src = pu[:, 0:NE] if j % 2 == 0 else pu[:, 512:512 + NE]
                    sAC.inc(s.activation(scores[:, :], src, AF.Sigmoid))
                    sig.append(sAC.n)
            block.scalar(_)
            assert sig == sig_pred

            gv = []

            def _(v):
                for j in range(TC):
                    v.wait_ge(sAC.h, sig[j])

                    def stp(ins):
                        sDV.inc(ins)
                        v.wait_ge(sDV.h, sDV.n)
                    stp(v.tensor_add(routing[:, :], scores[:, :], biasb[:, :]))
                    stp(v.reduce_max(idx1[:, :], routing[:, :], axis=AX))
                    stp(v.tensor_single_scalar(eqt[:, :], routing[:, :], idx1[:, :], ALU.is_equal))
                    stp(v.scalar_tensor_tensor(cand[:, :], eqt[:, :], -BIG, iopb_sb[:, :],
                                               ALU.mult, ALU.add))
                    stp(v.tensor_reduce(idx1[:, :], cand[:, :], axis=AX, op=ALU.min))
                    stp(v.tensor_single_scalar(oh1[:, :], iota_sb[:, :], idx1[:, :], ALU.is_equal))
                    stp(v.scalar_tensor_tensor(r2t[:, :], oh1[:, :], -BIG, routing[:, :],
                                               ALU.mult, ALU.add))
                    stp(v.reduce_max(idx1[:, :], r2t[:, :], axis=AX))
                    stp(v.tensor_single_scalar(eqt[:, :], r2t[:, :], idx1[:, :], ALU.is_equal))
                    stp(v.scalar_tensor_tensor(cand[:, :], eqt[:, :], -BIG, iopb_sb[:, :],
                                               ALU.mult, ALU.add))
                    stp(v.tensor_reduce(idx1[:, :], cand[:, :], axis=AX, op=ALU.min))
                    stp(v.tensor_single_scalar(oh2[:, :], iota_sb[:, :], idx1[:, :], ALU.is_equal))
                    stp(v.scalar_tensor_tensor(junk8[:, :], scores[:, :], 1.0, oh1[:, :],
                                               ALU.mult, ALU.mult, accum_out=sel1[:, :]))
                    stp(v.scalar_tensor_tensor(junk8[:, :], scores[:, :], 1.0, oh2[:, :],
                                               ALU.mult, ALU.mult, accum_out=sel2[:, :]))
                    stp(v.tensor_add(gsum[:, :], sel1[:, :], sel2[:, :]))
                    stp(v.reciprocal(ginv[:, :], gsum[:, :]))
                    stp(v.tensor_mul(g1[:, :], sel1[:, :], ginv[:, :]))
                    stp(v.tensor_mul(g2[:, :], sel2[:, :], ginv[:, :]))
                    stp(v.tensor_scalar(junk8[:, :], oh1[:, :], g1[:, :], None, ALU.mult))
                    stp(v.scalar_tensor_tensor(c_all[:, j * NE:(j + 1) * NE], oh2[:, :],
                                               g2[:, :], junk8[:, :], ALU.mult, ALU.add))
                    gv.append(sDV.n)
            block.vector(_)
            assert gv == gv_pred

            ctp = []
            ctc_pred = [sDV.n + j + 1 for j in range(TC)]

            def _(t):
                t.wait_ge(sAC.h, sig[-1])
                for j in range(TC):
                    t.wait_ge(sDV.h, gv[j])
                    if j >= 1:
                        t.wait_ge(sDV.h, ctc_pred[j - 1])
                    sPE.inc(t.transpose(pu[0:NE, 0:128], c_all[:, j * NE:(j + 1) * NE],
                                        id_sb[:, :]))
                    ctp.append(sPE.n)
            block.tensor(_)

            ctc = []

            def _(v):
                for j in range(TC):
                    v.wait_ge(sPE.h, ctp[j])
                    sDV.inc(v.tensor_copy(cT[:, j * 128:(j + 1) * 128], pu[0:NE, 0:128]))
                    ctc.append(sDV.n)
            block.vector(_)
            assert ctc == ctc_pred
            cT_mark = sDV.n

            pu_reader = cT_mark
            out_done = 0
            bufK_war = list(M["bufK_free"])
            prev_tail = [None] * 4
            prev_evac_base = [0]
            for ip, path in enumerate(PATHS):
                is_exp = path != "sh"
                if is_exp:
                    def _(t, e=path, pr=pu_reader):
                        t.wait_ge(sDV.h, max(cT_mark, pr))
                        for half in range(2):
                            sPE.inc(t.matmul(pu[:, half * 512:(half + 1) * 512],
                                             oh_sb[:, e * 128:(e + 1) * 128],
                                             cT[:, half * 512:(half + 1) * 512],
                                             start=True, stop=True))
                    block.tensor(_)
                    cbc_mark = sPE.n

                silu_m = {}
                mul2_m = {}
                h1_m = {}
                h3_m = {}
                for f in range(FCN):
                    def _(t, f=f, ip=ip):
                        t.wait_ge(sT.h, wA_marks[(ip, f)])
                        t.wait_ge(sT.h, wB_marks[(ip, f)])
                        slotA = wbufA[:, ((wa_seq[0] + f) % 2) * H:((wa_seq[0] + f) % 2 + 1) * H]
                        slotB = wbufB[:, ((wb_seq[0] + f) % 2) * H:((wb_seq[0] + f) % 2 + 1) * H]
                        for half in range(2):
                            mk = silu_m.get((f - 1, half))
                            if mk is not None:
                                t.wait_ge(sAC.h, mk)
                            else:
                                wt(t, M["pb_free"][half])
                            for hc in range(HCN):
                                ins = t.matmul(ph1[:, half * 512:(half + 1) * 512],
                                               slotA[:, hc * 128:(hc + 1) * 128],
                                               xlnT[:, hc * T + half * 512:hc * T + (half + 1) * 512],
                                               start=(hc == 0), stop=(hc == HCN - 1))
                            sPE.inc(ins)
                            h1_m[(f, half)] = sPE.n
                        t.wait_ge(sPE.h, h1_m[(f, 1)])
                        sWa.inc(t.nop())
                        for half in range(2):
                            mk = mul2_m.get((f - 1, half))
                            if mk is not None:
                                t.wait_ge(sDV.h, mk)
                            else:
                                wt(t, M["pb_free"][2 + half])
                            for hc in range(HCN):
                                ins = t.matmul(ph3[:, half * 512:(half + 1) * 512],
                                               slotB[:, hc * 128:(hc + 1) * 128],
                                               xlnT[:, hc * T + half * 512:hc * T + (half + 1) * 512],
                                               start=(hc == 0), stop=(hc == HCN - 1))
                            sPE.inc(ins)
                            h3_m[(f, half)] = sPE.n
                        t.wait_ge(sPE.h, h3_m[(f, 1)])
                        sWb.inc(t.nop())
                    block.tensor(_)

                    def _(s, f=f):
                        for half in range(2):
                            s.wait_ge(sPE.h, h1_m[(f, half)])
                            mk = mul2_m.get((f - 1, half))
                            if mk is not None:
                                s.wait_ge(sDV.h, mk)
                            sAC.inc(s.activation(tmp_silu[:, half * 512:(half + 1) * 512],
                                                 ph1[:, half * 512:(half + 1) * 512], AF.Silu))
                            silu_m[(f, half)] = sAC.n
                    block.scalar(_)

                    def _(v, f=f, is_exp=is_exp):
                        for half in range(2):
                            v.wait_ge(sAC.h, silu_m[(f, half)])
                            v.wait_ge(sPE.h, h3_m[(f, half)])
                            if f == 0 and half == 0:
                                if out_done:
                                    v.wait_ge(sPE.h, out_done)
                                else:
                                    wt(v, bufK_war)
                                if is_exp:
                                    v.wait_ge(sPE.h, cbc_mark)
                            ts = tmp_silu[:, half * 512:(half + 1) * 512]
                            if is_exp:
                                sDV.inc(v.tensor_mul(ts, ts, pu[:, half * 512:(half + 1) * 512]))
                                v.wait_ge(sDV.h, sDV.n)
                            sDV.inc(v.tensor_mul(
                                bufK[:, f * T + half * 512:f * T + (half + 1) * 512],
                                ts, ph3[:, half * 512:(half + 1) * 512]))
                            mul2_m[(f, half)] = sDV.n
                    block.vector(_)
                if DEBUG_FLUSH_ACT:
                    def _(v):
                        v.wait_ge(sDV.h, sDV.n)
                        for rep in range(3):
                            sDV.inc(v.tensor_copy(junk8[:, :], bufK[0:128, 0:2 * NE].bitcast(F32)))
                            v.wait_ge(sDV.h, sDV.n)
                    block.vector(_)
                wa_seq[0] += FCN
                wb_seq[0] += FCN
                pu_reader = sDV.n
                act_ready = sDV.n
                M["pb_free"] = [[(sAC, silu_m[(FCN - 1, 0)])], [(sAC, silu_m[(FCN - 1, 1)])],
                                [(sDV, mul2_m[(FCN - 1, 0)])], [(sDV, mul2_m[(FCN - 1, 1)])]]

                tile_m = {}
                evac_m = {}
                for Q in range(4):
                    for jg in range(2):
                        def _(t, Q=Q, jg=jg, ip=ip, act_ready=act_ready):
                            t.wait_ge(sT.h, wC_marks[(ip, Q)])
                            if Q == 0 and jg == 0:
                                t.wait_ge(sDV.h, act_ready)
                                if ip == 0:
                                    wt(t, M["po_free"] + M["pu_free"])
                            reg = ((wc_seq[0] + Q) % 2) * 5632
                            for j in range(jg * 4, jg * 4 + 4):
                                tl = Q * 8 + j
                                bank = tl % 4
                                if DEBUG_SERIAL_OUT and (tl > 0 or prev_evac_base[0]):
                                    t.wait_ge(sDV.h, (act_ready + tl) if tl > 0 else prev_evac_base[0])
                                if tl >= 4:
                                    t.wait_ge(sDV.h, evac_m[tl - 4])
                                elif prev_tail[bank] is not None:
                                    t.wait_ge(sDV.h, prev_tail[bank])
                                dst = (po if bank < 2 else pu)[:, (bank % 2) * 512:(bank % 2) * 512 + 512]
                                for f in range(FCN):
                                    ins = t.matmul(dst,
                                                   bufK[:, f * T + j * 128:f * T + (j + 1) * 128],
                                                   wbufC[:, reg + f * 512:reg + (f + 1) * 512],
                                                   start=(f == 0), stop=(f == FCN - 1))
                                sPE.inc(ins)
                                tile_m[tl] = sPE.n
                            if jg == 1:
                                t.wait_ge(sPE.h, tile_m[Q * 8 + 7])
                                sWc.inc(t.nop())
                        block.tensor(_)

                        def _(v, Q=Q, jg=jg):
                            for j in range(jg * 4, jg * 4 + 4):
                                tl = Q * 8 + j
                                bank = tl % 4
                                v.wait_ge(sPE.h, tile_m[tl])
                                src = (po if bank < 2 else pu)[:, (bank % 2) * 512:(bank % 2) * 512 + 512]
                                dst = xres[:, j * H + Q * 512:j * H + (Q + 1) * 512]
                                sDV.inc(v.scalar_tensor_tensor(dst, src, 0.5, dst,
                                                               ALU.mult, ALU.add))
                                evac_m[tl] = sDV.n
                        block.vector(_)
                wc_seq[0] += 4
                out_done = sPE.n
                prev_evac_base[0] = evac_m[31]
                prev_tail = [evac_m[28], evac_m[29], evac_m[30], evac_m[31]]
                pu_reader = sDV.n
                M["po_free"] = [(sDV, sDV.n)]
                M["pu_free"] = [(sDV, sDV.n)]

            M["bufK_free"] = [(sPE, sPE.n)]
            M["xlnT_free"] = [(sPE, sPE.n)]
            M["xres_w"] = [(sDV, sDV.n)]
            M["bufV_free"] = [(sDV, sDV.n), (sPE, sPE.n)]
            return sDV.n

        wa_seq = [0]
        wb_seq = [0]
        wc_seq = [0]

        def attn_phase(xlnT_mark):
            qkv_marks = {}

            def emit_slabs(fb_list):
                def _(sy):
                    wA_ex, M["wA_extra"] = M["wA_extra"], []
                    for Fb in fb_list:
                        sy.wait_ge(sC.h, cast[("wqkvb", Fb // 8)])
                        ia = wa_idx[0]
                        if ia >= 2:
                            sy.wait_ge(sWa.h, ia - 1)
                        if wa_idx[0] - wa_idx[1] < 2:
                            wt(sy, wA_ex)
                        sT.inc(sy.dma_start(out=wbufA[:, (ia % 2) * H:(ia % 2 + 1) * H],
                                            in_=wqkv_sw[Fb * 128:(Fb + 1) * 128, :]), 16)
                        qkv_marks[Fb] = sT.n
                        wa_idx[0] += 1
                block.sync(_)
            wa_idx[1] = wa_idx[0]
            emit_slabs(list(range(16, 48)))

            def _(g):
                g.wait_ge(sWb.h, wb_seq[0])
                sC.inc(g.dma_start(out=bprojT, in_=b_proj.rearrange("h -> () h")), 16)
                cast["bproj"] = sC.n
            block.gpsimd(_)

            kvb_pred = {}
            for k in range(32):
                kvb_pred[k] = sB.n + (k + 1) * 16
            evac_kv = {}
            tile_kv = {}
            kv_stage_init = [0]

            for k, Fb in enumerate(range(16, 48)):
                def _(t, k=k, Fb=Fb):
                    if k == 0:
                        t.wait_ge(sDV.h, xlnT_mark)
                    t.wait_ge(sT.h, qkv_marks[Fb])
                    slot = wbufA[:, ((wa_seq[0] + k) % 2) * H:((wa_seq[0] + k) % 2 + 1) * H]
                    for th in range(2):
                        gt = 2 * k + th
                        if gt >= 4:
                            s_, n_ = evac_kv[gt - 4]
                            t.wait_ge(s_.h, n_)
                        else:
                            wt(t, M["pb_free"][gt])
                        for hc in range(HCN):
                            ins = t.matmul(pbank[gt % 4],
                                           slot[:, hc * 128:(hc + 1) * 128],
                                           xlnT[:, hc * T + th * 512:hc * T + (th + 1) * 512],
                                           start=(hc == 0), stop=(hc == HCN - 1))
                        sPE.inc(ins)
                        tile_kv[gt] = sPE.n
                    t.wait_ge(sPE.h, tile_kv[2 * k + 1])
                    sWa.inc(t.nop())
                block.tensor(_)

                is_k = Fb < 32

                def _(e, k=k, Fb=Fb, is_k=is_k):
                    for th in range(2):
                        gt = 2 * k + th
                        e.wait_ge(sPE.h, tile_kv[gt])
                        if k >= 3:
                            e.wait_ge(sB.h, kvb_pred[k - 3])
                        elif kv_stage_init[0] == 0:
                            e.wait_ge(sWc.h, wc_seq[0])
                        dst = kv_stage[k % 3][:, th * 512:(th + 1) * 512]
                        if is_k:
                            ins = e.activation(dst, pbank[gt % 4], AF.Relu)
                            evac_kv[gt] = (sAC, sAC.inc(ins))
                        else:
                            ins = e.tensor_copy(dst, pbank[gt % 4])
                            evac_kv[gt] = (sDV, sDV.inc(ins))
                    kv_stage_init[0] = 1
                (block.scalar if is_k else block.vector)(_)

                def _(g, k=k, Fb=Fb):
                    for s_, n_ in (evac_kv[2 * k], evac_kv[2 * k + 1]):
                        g.wait_ge(s_.h, n_)
                    sB.inc(g.dma_start(out=kv_dram[(Fb - 16) * 128:(Fb - 15) * 128, :],
                                       in_=kv_stage[k % 3][:, :]), 16)
                block.gpsimd(_)
                assert sB.n == kvb_pred[k]
            wa_seq[0] += 32
            M["pb_free"] = [[evac_kv[60]], [evac_kv[61]], [evac_kv[62]], [evac_kv[63]]]

            if ATTN_UPTO < 2:
                return sDV.n
            kT_marks = []

            def _(sy):
                wt(sy, M["bufK_free"])
                sy.wait_ge(sB.h, kvb_pred[15])
                for j in range(TC):
                    sT.inc(sy.dma_start_transpose(bufK[:, j * H:(j + 1) * H],
                                                  kv_dram[0:H, j * 128:(j + 1) * 128]), 16)
                    kT_marks.append(sT.n)
                wt(sy, M["bufV_free"])
                sy.wait_ge(sB.h, kvb_pred[31])
                for j in range(TC):
                    sT.inc(sy.dma_start_transpose(bufV[:, j * H:(j + 1) * H],
                                                  kv_dram[H:2 * H, j * 128:(j + 1) * 128]), 16)
                    kT_marks.append(sT.n)
            block.sync(_)
            kvT_mark = sT.n
            if ATTN_UPTO >= 4:
                emit_slabs(list(range(16)))

            def _(t):
                t.wait_ge(sT.h, kvT_mark)
                wt(t, M["pu_free"])
                for bank in range(2):
                    first = True
                    for g in range(8):
                        hgrp = bank * 8 + g
                        for q in range(4):
                            hd = 4 * hgrp + q
                            col0 = bank * 512 + g * 33
                            for j in range(TC):
                                t.matmul(pu[32 * q:32 * q + 32, col0:col0 + 32],
                                         bufK[:, j * H + hd * 32:j * H + (hd + 1) * 32],
                                         bufV[:, j * H + hd * 32:j * H + (hd + 1) * 32],
                                         start=first, stop=False,
                                         tile_position=(0, 32 * q))
                                first = False
                                last = (g == 7 and q == 3 and j == TC - 1)
                                ins = t.matmul(pu[32 * q:32 * q + 32, col0 + 32:col0 + 33],
                                               bufK[:, j * H + hd * 32:j * H + (hd + 1) * 32],
                                               oc_bf[:, :],
                                               start=False, stop=last,
                                               tile_position=(0, 32 * q))
                    sPE.inc(ins)
                for col in (500, 1020):
                    sPE.inc(t.matmul(pu[:, col:col + 1], id_bf[:, :], oc_bf[:, :],
                                     start=True, stop=True))
            block.tensor(_)
            vkT_mm_mark = sPE.n

            if ATTN_UPTO < 3:
                return sDV.n
            def _(v):
                v.wait_ge(sPE.h, vkT_mm_mark)
                sDV.inc(v.tensor_copy(vkT_sb[:, 0:264], pu[:, 0:264]))
                sDV.inc(v.tensor_copy(vkT_sb[:, 264:528], pu[:, 512:776]))
            block.vector(_)
            vkT_ev_mark = sDV.n
            M["pu_free"] = [(sDV, sDV.n)]

            def _(g):
                g.wait_ge(sDV.h, vkT_ev_mark)
                sB.inc(g.dma_start(out=cc_in.ap(), in_=vkT_sb[:, :]), 16)
                g.wait_ge(sB.h, sB.n)
                sCC.inc(g.collective_compute(
                    "AllReduce", ALU.add,
                    replica_groups=[[0, 1, 2, 3], [4, 5, 6, 7]],
                    ins=[cc_in.ap().opt()], outs=[cc_out.ap().opt()]))
                g.wait_ge(sCC.h, sCC.n)
                sB.inc(g.dma_start(out=vkT_sb[:, :], in_=cc_out.ap()), 16)
            block.gpsimd(_)
            cc_mark = sB.n

            def _(v):
                v.wait_ge(sB.h, cc_mark)
                sDV.inc(v.tensor_copy(vkT_bf[:, :], vkT_sb[:, :]))
            block.vector(_)
            vkT_bf_mark = sDV.n

            if ATTN_UPTO < 4:
                return sDV.n
            tile_q = {}
            qev = []

            for fb in range(16):
                def _(t, fb=fb):
                    t.wait_ge(sT.h, qkv_marks[fb])
                    slot = wbufA[:, ((wa_seq[0] + fb) % 2) * H:((wa_seq[0] + fb) % 2 + 1) * H]
                    for th in range(2):
                        gt = 2 * fb + th
                        if gt >= 4:
                            s_, n_ = qev[gt - 4]
                            t.wait_ge(s_.h, n_)
                        else:
                            wt(t, M["pb_free"][gt])
                        for hc in range(HCN):
                            ins = t.matmul(pbank[gt % 4],
                                           slot[:, hc * 128:(hc + 1) * 128],
                                           xlnT[:, hc * T + th * 512:hc * T + (th + 1) * 512],
                                           start=(hc == 0), stop=(hc == HCN - 1))
                        sPE.inc(ins)
                        tile_q[gt] = sPE.n
                    t.wait_ge(sPE.h, tile_q[2 * fb + 1])
                    sWa.inc(t.nop())
                block.tensor(_)

                def _(s, fb=fb):
                    for th in range(2):
                        gt = 2 * fb + th
                        s.wait_ge(sPE.h, tile_q[gt])
                        if gt == 0:
                            s.wait_ge(sPE.h, vkT_mm_mark)
                        ins = s.activation(bufK[:, fb * T + th * 512:fb * T + (th + 1) * 512],
                                           pbank[gt % 4], AF.Relu)
                        qev.append((sAC, sAC.inc(ins)))
                block.scalar(_)
            wa_seq[0] += 16
            q_done = sAC.n
            M["pb_free"] = [[qev[28]], [qev[29]], [qev[30]], [qev[31]]]

            if ATTN_UPTO < 5:
                return sDV.n
            epi_m = {}
            otile_m = {}

            for j in range(TC):
                for tp in range(4):
                    def _(t, j=j, tp=tp):
                        if j == 0 and tp == 0:
                            t.wait_ge(sDV.h, vkT_bf_mark)
                            t.wait_ge(sAC.h, q_done)
                            wt(t, M["po_free"])
                        for tt in range(2 * tp, 2 * tp + 2):
                            gi = j * 8 + tt
                            q4, sh = tt % 4, tt // 4
                            if gi >= 2:
                                mk = epi_m.get(gi - 2)
                                if mk is not None:
                                    t.wait_ge(sDV.h, mk)
                            off = (gi % 2) * 512
                            for i in range(8):
                                hgrp = 8 * sh + i
                                ins = t.matmul(
                                    po[:, off + i * 33:off + (i + 1) * 33],
                                    bufK[32 * q4:32 * q4 + 32,
                                         hgrp * T + j * 128:hgrp * T + (j + 1) * 128],
                                    vkT_bf[32 * q4:32 * q4 + 32, hgrp * 33:(hgrp + 1) * 33],
                                    start=(i == 0), stop=(i == 7),
                                    tile_position=(32 * q4, 0))
                            ins = t.matmul(po[:, off + 490:off + 491],
                                           id_bf[:, :], oc_bf[:, :],
                                           start=True, stop=True)
                            sPE.inc(ins)
                            otile_m[gi] = sPE.n
                    block.tensor(_)

                    def _(v, j=j, tp=tp):
                        if ATTN_UPTO == 41:
                            for tt in range(2 * tp, 2 * tp + 2):
                                gi = j * 8 + tt
                                v.wait_ge(sPE.h, otile_m[gi])
                                sDV.inc(v.tensor_copy(dnm[:, :], po[0:128, 0:NE]))
                                epi_m[gi] = sDV.n
                            return
                        for tt in range(2 * tp, 2 * tp + 2):
                            gi = j * 8 + tt
                            q4, sh = tt % 4, tt // 4
                            v.wait_ge(sPE.h, otile_m[gi])
                            off = (gi % 2) * 512
                            p3 = po[:, off:off + 264].rearrange("p (h e) -> p h e", e=33)
                            sDV.inc(v.tensor_single_scalar(dnm[:, :], p3[:, :, 32:33].rearrange("p h e -> p (h e)"),
                                                           EPS_AT, ALU.add))
                            v.wait_ge(sDV.h, sDV.n)
                            sDV.inc(v.reciprocal(rcp[:, :], dnm[:, :]))
                            v.wait_ge(sDV.h, sDV.n)
                            base = j * H + 1024 * sh
                            ov = bufV[:, base:base + 1024].rearrange("p (i r) -> p i r", r=128)
                            sDV.inc(v.tensor_tensor(
                                ov[:, :, 32 * q4:32 * q4 + 32],
                                p3[:, :, 0:32],
                                rcp[:, :].rearrange("p h -> p h ()").broadcast_to([128, NE, 32]),
                                ALU.mult))
                            epi_m[gi] = sDV.n
                    block.vector(_)

            M["po_free"] = [(sDV, sDV.n)]
            attn_out_done = sPE.n

            if ATTN_UPTO < 6:
                return sDV.n
            epi_j = [epi_m[j * 8 + 7] for j in range(TC)]
            afm_mark = pe_transpose_to_xlnT(
                lambda j, hc: bufV[:, j * H + hc * 128:j * H + (hc + 1) * 128], epi_j)

            pset = {}

            def _(sy):
                sy.wait_ge(sC.h, cast["wproj"])
                wc_idx[1] = wc_idx[0]
                for Q in range(4):
                    for jh in range(2):
                        for ch in range(2):
                            ic = wc_idx[0]
                            if ic >= 2:
                                sy.wait_ge(sWc.h, ic - 1)
                            reg = (ic % 2) * 5632
                            sT.inc(sy.dma_start(
                                out=wbufC[:, reg:reg + 4096],
                                in_=wproj_bf[(Q * 2 + ch) * 128:(Q * 2 + ch + 1) * 128, :]), 16)
                            pset[(Q, jh, ch)] = sT.n
                            wc_idx[0] += 1
            block.sync(_)

            pj_evac = {}
            for Q in range(4):
                for jh in range(2):
                    ptile = {}

                    def _(t, Q=Q, jh=jh):
                        t.wait_ge(sT.h, pset[(Q, jh, 0)])
                        if Q == 0 and jh == 0:
                            t.wait_ge(sDV.h, afm_mark)
                            t.wait_ge(sC.h, cast["bproj"])
                            wt(t, M["pb_free"][0] + M["pb_free"][1]
                               + M["pb_free"][2] + M["pb_free"][3])
                        else:
                            prev = (Q, jh - 1) if jh else (Q - 1, 1)
                            for jj in range(4):
                                t.wait_ge(sDV.h, pj_evac[(prev[0], prev[1], jj)])
                        for ch in range(2):
                            if ch == 1:
                                t.wait_ge(sT.h, pset[(Q, jh, 1)])
                            reg = ((wc_seq[0] + 2 * (2 * Q + jh) + ch) % 2) * 5632
                            for jj in range(4):
                                j = jh * 4 + jj
                                for cc in range(8):
                                    c0 = 8 * ch + cc
                                    ins = t.matmul(
                                        pbank[jj],
                                        xlnT[:, c0 * T + j * 128:c0 * T + (j + 1) * 128],
                                        wbufC[:, reg + cc * 512:reg + (cc + 1) * 512],
                                        start=(ch == 0 and cc == 0), stop=False)
                                if ch == 1:
                                    ins = t.matmul(pbank[jj], or_bf[:, :],
                                                   bprojT[0:1, Q * 512:(Q + 1) * 512],
                                                   start=False, stop=(cc == 7))
                                sPE.inc(ins)
                                ptile[jj] = sPE.n
                            if ch == 1:
                                t.wait_ge(sPE.h, ptile[3])
                                sWc.inc(t.nop(), 2)
                    block.tensor(_)

                    def _(v, Q=Q, jh=jh):
                        for jj in range(4):
                            j = jh * 4 + jj
                            v.wait_ge(sPE.h, ptile[jj])
                            dst = xres[:, j * H + Q * 512:j * H + (Q + 1) * 512]
                            sDV.inc(v.tensor_add(dst, dst, pbank[jj]))
                            pj_evac[(Q, jh, jj)] = sDV.n
                    block.vector(_)
            wc_seq[0] += 16
            proj_done_pe = sPE.n
            M["pb_free"] = [[(sDV, pj_evac[(3, 1, jj)])] for jj in range(4)]
            M["bufK_free"] = [(sPE, attn_out_done)]
            M["xlnT_free"] = [(sPE, proj_done_pe)]
            M["xres_w"] = [(sDV, sDV.n)]
            M["bufV_free"] = [(sPE, proj_done_pe)]
            M["wB_extra"] = [(sPE, proj_done_pe)]
            M["wC_extra"] = [(sB, kvb_pred[31])]
            return sDV.n

        m1 = ln_phase()
        ffn_phase(m1)
        if DEBUG_DUMP:
            def _(sy):
                wt(sy, M["xres_w"])
                wt(sy, M["bufK_free"])
                sT.inc(sy.dma_start(out=dbg_actK, in_=bufK[:, :]), 16)
                sT.inc(sy.dma_start(out=dbg_bufV, in_=bufV[:, :]), 16)
                sT.inc(sy.dma_start(out=dbg_call, in_=c_all[:, :]), 16)
                sT.inc(sy.dma_start(out=dbg_xlnT, in_=xlnT[:, :]), 16)
                sy.wait_ge(sT.h, sT.n)
            block.sync(_)
        if STAGES >= 2:
            m2 = ln_phase()
            attn_phase(m2)
        if STAGES >= 3:
            m3 = ln_phase()
            ffn_phase(m3)

        def _(sy):
            wt(sy, M["xres_w"])
            sT.inc(sy.dma_start(out=out.rearrange("(j p) h -> p j h", p=128),
                                in_=xres[:, :].rearrange("p (j h) -> p j h", h=H)), 16)
            sy.wait_ge(sT.h, sT.n)
        block.sync(_)

    return nc


_NC_CACHE = {}


def host_swz(w):
    R = w.shape[0]
    return np.ascontiguousarray(
        w.reshape(R // 128, 128, HCN, 128).transpose(0, 3, 2, 1).reshape(R, HCN * 128))


def host_w2(w):
    return np.ascontiguousarray(
        w.reshape(4, 512, FCN, 128).transpose(0, 3, 2, 1).reshape(512, FCN * 512))


def host_proj(w):
    return np.ascontiguousarray(
        w.reshape(4, 512, 2, 8, 128).transpose(0, 2, 4, 3, 1).reshape(1024, 4096))


def kernel(**inputs):
    x = np.asarray(inputs["L0_attn_x"])
    if "prog" not in _NC_CACHE:
        _NC_CACHE["prog"] = build_program()
    nc = _NC_CACHE["prog"]
    we2_np = np.asarray(inputs["L0_ffn_we2"])
    common = {
        "wqkv_s": host_swz(np.asarray(inputs["L0_attn_w_qkv"])),
        "w_proj": host_proj(np.asarray(inputs["L0_attn_w_proj"])),
        "b_proj": inputs["L0_attn_b_proj"],
        "g_bias": inputs["L0_ffn_bias"],
        "gate_wt": np.asarray(inputs["L0_ffn_gate_w"]).T,
        "ws1_s": host_swz(np.asarray(inputs["L0_ffn_ws1"])),
        "ws2": host_w2(np.asarray(inputs["L0_ffn_ws2"])),
        "ws3_s": host_swz(np.asarray(inputs["L0_ffn_ws3"])),
        "we1_s": host_swz(np.asarray(inputs["L0_ffn_we1"]).reshape(-1, H)),
        "we2": np.concatenate([host_w2(we2_np[e]) for e in range(NE)], axis=0),
        "we3_s": host_swz(np.asarray(inputs["L0_ffn_we3"]).reshape(-1, H)),
    }
    common = {k: np.ascontiguousarray(v) for k, v in common.items()}
    in_maps = []
    for c in range(8):
        m = dict(common)
        m["x"] = np.ascontiguousarray(x[c // 4, (c % 4) * T:(c % 4 + 1) * T, :])
        in_maps.append(m)
    res = run_bass_kernel_spmd(nc, in_maps, core_ids=list(range(8)))
    y = np.empty((2, 4096, H), np.float32)
    for c in range(8):
        y[c // 4, (c % 4) * T:(c % 4 + 1) * T, :] = res.results[c]["out"]
    return y


# version 07
# speedup vs baseline: 1.1651x; 1.0018x over previous
import contextlib
import numpy as np
import ml_dtypes
import concourse.bass as bass
import concourse.mybir as mybir
from concourse.bass_utils import run_bass_kernel_spmd

F32 = mybir.dt.float32
BF16 = mybir.dt.bfloat16
AF = mybir.ActivationFunctionType
ALU = mybir.AluOpType
AX = mybir.AxisListType.X

T = 1024
TC = 8
H = 2048
HCN = 16
FF = 1408
FCN = 11
NE = 8
BIG = 1.0e4
EPS_LN = 1.0e-5
EPS_AT = 1.0e-15
STAGES = 3
ATTN_UPTO = 9
DEBUG_DUMP = False
DEBUG_SERIAL_OUT = False
DEBUG_FLUSH_ACT = False
PATHS = ["sh"] + list(range(NE))


class Sem:
    def __init__(self, handle):
        self.h = handle
        self.n = 0

    def inc(self, instr, k=1):
        instr.then_inc(self.h, k)
        self.n += k
        return self.n


def build_program():
    nc = bass.Bass("TRN2", target_bir_lowering=False, debug=False)

    x_in = nc.dram_tensor("x", [T, H], F32, kind="ExternalInput").ap()
    out = nc.dram_tensor("out", [T, H], F32, kind="ExternalOutput").ap()
    wqkv_s = nc.dram_tensor("wqkv_s", [3 * H, H], F32, kind="ExternalInput").ap()
    w_proj = nc.dram_tensor("w_proj", [8 * 128, 8 * 512], F32, kind="ExternalInput").ap()
    b_proj = nc.dram_tensor("b_proj", [H], F32, kind="ExternalInput").ap()
    gate_wt = nc.dram_tensor("gate_wt", [H, NE], F32, kind="ExternalInput").ap()
    g_bias = nc.dram_tensor("g_bias", [NE], F32, kind="ExternalInput").ap()
    ws1_s = nc.dram_tensor("ws1_s", [FF, H], F32, kind="ExternalInput").ap()
    ws2 = nc.dram_tensor("ws2", [512, FCN * 512], F32, kind="ExternalInput").ap()
    ws3_s = nc.dram_tensor("ws3_s", [FF, H], F32, kind="ExternalInput").ap()
    we1_s = nc.dram_tensor("we1_s", [NE * FF, H], F32, kind="ExternalInput").ap()
    we2 = nc.dram_tensor("we2", [NE * 512, FCN * 512], F32, kind="ExternalInput").ap()
    we3_s = nc.dram_tensor("we3_s", [NE * FF, H], F32, kind="ExternalInput").ap()

    ws1_sw = nc.dram_tensor("ws1_sw", [FF, H], BF16).ap()
    ws3_sw = nc.dram_tensor("ws3_sw", [FF, H], BF16).ap()
    we1_sw = nc.dram_tensor("we1_sw", [NE * FF, H], BF16).ap()
    we3_sw = nc.dram_tensor("we3_sw", [NE * FF, H], BF16).ap()
    wqkv_sw = nc.dram_tensor("wqkv_sw", [3 * H, H], BF16).ap()
    ws2_bf = nc.dram_tensor("ws2_bf", [512, FCN * 512], BF16).ap()
    we2_bf = nc.dram_tensor("we2_bf", [NE * 512, FCN * 512], BF16).ap()
    wproj_bf = nc.dram_tensor("wproj_bf", [8 * 128, 8 * 512], BF16).ap()
    cc_in = nc.dram_tensor("cc_in", [128, 528], F32)
    cc_out = nc.dram_tensor("cc_out", [128, 528], F32)
    if DEBUG_DUMP:
        dbg_actK = nc.dram_tensor("dbg_actK", [128, HCN * T], BF16, kind="ExternalOutput").ap()
        dbg_bufV = nc.dram_tensor("dbg_bufV", [128, HCN * T], BF16, kind="ExternalOutput").ap()
        dbg_call = nc.dram_tensor("dbg_call", [128, TC * NE], F32, kind="ExternalOutput").ap()
        dbg_xlnT = nc.dram_tensor("dbg_xlnT", [128, HCN * T], BF16, kind="ExternalOutput").ap()

    ident_f = nc.inline_tensor(np.eye(128, dtype=np.float32), "ident_f").ap()
    ident_bf = nc.inline_tensor(np.eye(128, dtype=np.float32).astype(ml_dtypes.bfloat16),
                                "ident_bf").ap()
    oh_np = np.zeros((NE, NE * 128), dtype=np.float32)
    for e in range(NE):
        oh_np[e, e * 128:(e + 1) * 128] = 1.0
    onehot_f = nc.inline_tensor(oh_np, "onehot_f").ap()
    ones_col_bf = nc.inline_tensor(np.ones((128, 1), dtype=ml_dtypes.bfloat16), "c_oc_bf").ap()
    ones_row_bf = nc.inline_tensor(np.ones((1, 128), dtype=ml_dtypes.bfloat16), "c_or_bf").ap()
    ones_row_f = nc.inline_tensor(np.ones((1, 128), dtype=np.float32), "c_or_f").ap()
    iota_np = np.tile(np.arange(NE, dtype=np.float32)[None, :], (128, 1))
    iota_c = nc.inline_tensor(iota_np, "iota_c").ap()
    iopb_c = nc.inline_tensor(iota_np + BIG, "iopb_c").ap()
    epsln_c = nc.inline_tensor(np.full((128, 1), EPS_LN, dtype=np.float32), "epsln_c").ap()

    with contextlib.ExitStack() as st:
        def sbt(name, shape, dtype):
            return st.enter_context(nc.sbuf_tensor(name, shape, dtype))

        xres = sbt("xres", [128, TC * H], F32)
        xlnT = sbt("xlnT", [128, HCN * T], BF16)
        bufK = sbt("bufK", [128, HCN * T], BF16)
        bufV = sbt("bufV", [128, HCN * T], BF16)
        wbufA = sbt("wbufA", [128, 2 * H], BF16)
        wbufB = sbt("wbufB", [128, 2 * H], BF16)
        wbufC = sbt("wbufC", [128, FCN * 1024], BF16)
        gateT = sbt("gateT", [128, HCN * NE], BF16)
        id_sb = sbt("id_sb", [128, 128], F32)
        id_bf = sbt("id_bf", [128, 128], BF16)
        oh_sb = sbt("oh_sb", [NE, NE * 128], F32)
        oc_bf = sbt("oc_bf", [128, 1], BF16)
        or_bf = sbt("or_bf", [1, 128], BF16)
        or_f = sbt("or_f", [1, 128], F32)
        iota_sb = sbt("iota_sb", [128, NE], F32)
        iopb_sb = sbt("iopb_sb", [128, NE], F32)
        bias_sb = sbt("bias_sb", [1, NE], F32)
        biasb = sbt("biasb", [128, NE], F32)
        stats = sbt("stats", [128, 24], F32)
        mvall = sbt("mvall", [128, 2 * TC], F32)
        negmean = sbt("negmean", [128, TC], F32)
        sdall = sbt("sdall", [128, TC], F32)
        rstd = sbt("rstd", [128, TC], F32)
        scores = sbt("scores", [128, NE], F32)
        routing = sbt("routing", [128, NE], F32)
        eqt = sbt("eqt", [128, NE], F32)
        cand = sbt("cand", [128, NE], F32)
        idx1 = sbt("idx1", [128, 1], F32)
        oh1 = sbt("oh1", [128, NE], F32)
        oh2 = sbt("oh2", [128, NE], F32)
        junk8 = sbt("junk8", [128, NE], F32)
        r2t = sbt("r2t", [128, NE], F32)
        sel1 = sbt("sel1", [128, 1], F32)
        sel2 = sbt("sel2", [128, 1], F32)
        gsum = sbt("gsum", [128, 1], F32)
        ginv = sbt("ginv", [128, 1], F32)
        g1 = sbt("g1", [128, 1], F32)
        g2 = sbt("g2", [128, 1], F32)
        c_all = sbt("c_all", [128, TC * NE], F32)
        dnm = sbt("dnm", [128, NE], F32)
        epsln = sbt("epsln", [128, 1], F32)
        rcp = sbt("rcp", [128, NE], F32)

        xln_tm = [bufV[:, j * H:(j + 1) * H] for j in range(TC)]
        tmp_silu = bufV[:, 2 * H:3 * H].bitcast(F32)
        cT = bufV[0:NE, 3 * H:4 * H].bitcast(F32)
        kv_stage = [wbufC[:, 8192 + i * 1024:8192 + (i + 1) * 1024] for i in range(3)]
        bprojT = wbufB[0:1, 0:H]
        vkT_sb = wbufB[:, H:H + 1056].bitcast(F32)
        vkT_bf = wbufB[:, H + 1056:H + 1584]

        ph1 = st.enter_context(nc.psum_tensor("ph1", [128, 1024], F32))
        ph3 = st.enter_context(nc.psum_tensor("ph3", [128, 1024], F32))
        po = st.enter_context(nc.psum_tensor("po", [128, 1024], F32))
        pu = st.enter_context(nc.psum_tensor("pu", [128, 1024], F32))
        pbank = [ph1[:, 0:512], ph1[:, 512:1024], ph3[:, 0:512], ph3[:, 512:1024]]

        sC = Sem(st.enter_context(nc.semaphore("sC")))
        sB = Sem(st.enter_context(nc.semaphore("sB")))
        sT = Sem(st.enter_context(nc.semaphore("sT")))
        sPE = Sem(st.enter_context(nc.semaphore("sPE")))
        sDV = Sem(st.enter_context(nc.semaphore("sDV")))
        sAC = Sem(st.enter_context(nc.semaphore("sAC")))
        sCC = Sem(st.enter_context(nc.semaphore("sCC")))
        sWa = Sem(st.enter_context(nc.semaphore("sWa")))
        sWb = Sem(st.enter_context(nc.semaphore("sWb")))
        sWc = Sem(st.enter_context(nc.semaphore("sWc")))

        block = st.enter_context(nc.Block())

        M = {}

        def _(sy):
            for dst, src in ((id_sb[:, :], ident_f), (id_bf[:, :], ident_bf),
                             (oh_sb[:, :], onehot_f),
                             (oc_bf[:, :], ones_col_bf), (or_bf[:, :], ones_row_bf),
                             (or_f[:, :], ones_row_f), (iota_sb[:, :], iota_c),
                             (iopb_sb[:, :], iopb_c), (epsln[:, :], epsln_c),
                             (bias_sb[:, :], g_bias.rearrange("e -> () e"))):
                sT.inc(sy.dma_start(out=dst, in_=src), 16)
            sT.inc(sy.dma_start(out=xres[:, :].rearrange("p (j h) -> p j h", h=H),
                                in_=x_in.rearrange("(j p) h -> p j h", p=128)), 16)
        block.sync(_)
        M["consts"] = sT.n

        cast = {}

        def _(g):
            sC.inc(g.dma_start(
                out=gateT[:, :].rearrange("p (hc e) -> p hc e", e=NE),
                in_=gate_wt.rearrange("(hc p) e -> p hc e", p=128)), 16)
            cast["gate"] = sC.n
            sC.inc(g.dma_start(out=ws1_sw, in_=ws1_s), 16)
            cast["ws1"] = sC.n
            sC.inc(g.dma_start(out=ws3_sw, in_=ws3_s), 16)
            cast["ws3"] = sC.n
            sC.inc(g.dma_start(out=ws2_bf, in_=ws2), 16)
            cast["ws2"] = sC.n
            for e in range(NE):
                g0 = 2 if e == 0 else 11 * e
                for nm, dst, src, rows in (("we1", we1_sw, we1_s, FF),
                                           ("we3", we3_sw, we3_s, FF),
                                           ("we2", we2_bf, we2, 512)):
                    for h in range(2):
                        g.wait_ge(sWa.h, g0)
                        g0 += 1
                        r0 = e * rows + h * (rows // 2)
                        sC.inc(g.dma_start(out=dst[r0:r0 + rows // 2, :],
                                           in_=src[r0:r0 + rows // 2, :]), 16)
                    cast[(nm, e)] = sC.n
            if STAGES >= 2:
                for cb in range(6):
                    for h in range(2):
                        g.wait_ge(sWa.h, 44 + 8 * cb + 4 * h)
                        r0 = cb * 1024 + h * 512
                        sC.inc(g.dma_start(out=wqkv_sw[r0:r0 + 512, :],
                                           in_=wqkv_s[r0:r0 + 512, :]), 16)
                    cast[("wqkvb", cb)] = sC.n
                for h in range(2):
                    g.wait_ge(sWa.h, 92 + 2 * h)
                    sC.inc(g.dma_start(out=wproj_bf[h * 512:(h + 1) * 512, :],
                                       in_=w_proj[h * 512:(h + 1) * 512, :]), 16)
                cast["wproj"] = sC.n
        block.gpsimd(_)


        def _(t):
            t.wait_ge(sT.h, M["consts"])
            sPE.inc(t.matmul(pu[:, 0:NE], or_f[:, :], bias_sb[:, :], start=True, stop=True))
        block.tensor(_)
        M["biasmm"] = sPE.n

        def _(v):
            v.wait_ge(sPE.h, M["biasmm"])
            sDV.inc(v.tensor_copy(biasb[:, :], pu[:, 0:NE]))
        block.vector(_)

        M["pu_free"] = [(sDV, sDV.n)]
        M["po_free"] = []
        M["pb_free"] = [[] for _ in range(4)]
        M["bufK_free"] = []
        M["bufV_free"] = []
        M["xlnT_free"] = []
        M["xres_w"] = []
        M["wA_extra"] = []
        M["wB_extra"] = []
        M["wC_extra"] = []
        wa_idx = [0, 0]
        wb_idx = [0, 0]
        wc_idx = [0, 0]

        def wt(eng, mks):
            for s, n in mks:
                if n > 0:
                    eng.wait_ge(s.h, n)

        def ln_phase():
            ag = []

            def _(v):
                v.wait_ge(sT.h, M["consts"])
                wt(v, M["bufV_free"])
                wt(v, M["xres_w"])
                for j in range(TC):
                    for gq in range(4):
                        sDV.inc(v.bn_stats(stats[:, 6 * gq:6 * gq + 6],
                                           xres[:, j * H + 512 * gq:j * H + 512 * (gq + 1)]))
                    v.wait_ge(sDV.h, sDV.n)
                    sDV.inc(v.bn_aggr(mvall[:, 2 * j:2 * j + 2], stats[:, :]))
                    v.wait_ge(sDV.h, sDV.n)
                    ag.append(sDV.n)
                    sDV.inc(v.tensor_scalar_mul(negmean[:, j:j + 1],
                                                mvall[:, 2 * j:2 * j + 1], -1.0))
            block.vector(_)
            neg_mark = sDV.n

            sq = []

            def _(s):
                s.wait_ge(sT.h, M["consts"])
                for j in range(TC):
                    s.wait_ge(sDV.h, ag[j])
                    sAC.inc(s.activation(sdall[:, j:j + 1], mvall[:, 2 * j + 1:2 * j + 2],
                                         AF.Sqrt, bias=epsln[:, :]))
                    sq.append(sAC.n)
            block.scalar(_)

            xm = []

            def _(v):
                v.wait_ge(sDV.h, neg_mark)
                for j in range(TC):
                    v.wait_ge(sAC.h, sq[j])
                    sDV.inc(v.reciprocal(rstd[:, j:j + 1], sdall[:, j:j + 1]))
                    v.wait_ge(sDV.h, sDV.n)
                for j in range(TC):
                    sDV.inc(v.tensor_scalar(xln_tm[j][:, :], xres[:, j * H:(j + 1) * H],
                                            negmean[:, j:j + 1], rstd[:, j:j + 1],
                                            ALU.add, ALU.mult))
                    xm.append(sDV.n)
            block.vector(_)

            mark = pe_transpose_to_xlnT(lambda j, hc: xln_tm[j][:, hc * 128:(hc + 1) * 128],
                                        xm)
            M["bufV_free"] = [(sPE, sPE.n)]
            return mark

        def pe_transpose_to_xlnT(src, jm):
            tr_m = {}
            cp_pred = [sDV.n + s + 1 for s in range(32)]

            def _(t):
                wt(t, M["pu_free"])
                for j in range(TC):
                    t.wait_ge(sDV.h, jm[j])
                    for g in range(4):
                        step = j * 4 + g
                        if step >= 2:
                            t.wait_ge(sDV.h, cp_pred[step - 2])
                        off = (step % 2) * 512
                        for i in range(4):
                            hc = g * 4 + i
                            ins = t.matmul(pu[:, off + i * 128:off + (i + 1) * 128],
                                           src(j, hc), id_bf[:, :],
                                           start=True, stop=True)
                        sPE.inc(ins)
                        tr_m[step] = sPE.n
            block.tensor(_)

            cp = []

            def _(v):
                wt(v, M["xlnT_free"])
                for step in range(32):
                    v.wait_ge(sPE.h, tr_m[step])
                    j, g = divmod(step, 4)
                    off = (step % 2) * 512
                    dst = xlnT[:, :].rearrange("p (hc t) -> p hc t", t=T)[
                        :, g * 4:(g + 1) * 4, j * 128:(j + 1) * 128]
                    sDV.inc(v.tensor_copy(
                        dst, pu[:, off:off + 512].rearrange("p (i t) -> p i t", t=128)))
                    cp.append(sDV.n)
            block.vector(_)
            assert cp == cp_pred
            M["pu_free"] = [(sDV, sDV.n)]
            return sDV.n

        def ffn_phase(xlnT_mark):
            wA_marks = {}
            wB_marks = {}
            wC_marks = {}

            def _(sy):
                wA_ex, M["wA_extra"] = M["wA_extra"], []
                wB_ex, M["wB_extra"] = M["wB_extra"], []
                wC_ex, M["wC_extra"] = M["wC_extra"], []
                wa_idx[1] = wa_idx[0]
                wb_idx[1] = wb_idx[0]
                wc_idx[1] = wc_idx[0]
                for ip, path in enumerate(PATHS):
                    for f in range(FCN):
                        if path == "sh":
                            src1 = ws1_sw[f * 128:(f + 1) * 128, :]
                            src3 = ws3_sw[f * 128:(f + 1) * 128, :]
                            cm = max(cast["ws1"], cast["ws3"])
                        else:
                            base = path * FF
                            src1 = we1_sw[base + f * 128:base + (f + 1) * 128, :]
                            src3 = we3_sw[base + f * 128:base + (f + 1) * 128, :]
                            cm = max(cast[("we1", path)], cast[("we3", path)])
                        sy.wait_ge(sC.h, cm)
                        ia = wa_idx[0]
                        if ia >= 2:
                            sy.wait_ge(sWa.h, ia - 1)
                        if wa_idx[0] - wa_idx[1] < 2:
                            wt(sy, wA_ex)
                        sT.inc(sy.dma_start(out=wbufA[:, (ia % 2) * H:(ia % 2 + 1) * H],
                                            in_=src1), 16)
                        wA_marks[(ip, f)] = sT.n
                        wa_idx[0] += 1
                        ib = wb_idx[0]
                        if ib >= 2:
                            sy.wait_ge(sWb.h, ib - 1)
                        if wb_idx[0] - wb_idx[1] < 2:
                            wt(sy, wB_ex)
                        sT.inc(sy.dma_start(out=wbufB[:, (ib % 2) * H:(ib % 2 + 1) * H],
                                            in_=src3), 16)
                        wB_marks[(ip, f)] = sT.n
                        wb_idx[0] += 1
                    w2src = ws2_bf if path == "sh" else we2_bf[path * 512:(path + 1) * 512, :]
                    cm = cast["ws2"] if path == "sh" else cast[("we2", path)]
                    sy.wait_ge(sC.h, cm)
                    for Q in range(4):
                        ic = wc_idx[0]
                        if ic >= 2:
                            sy.wait_ge(sWc.h, ic - 1)
                        if wc_idx[0] - wc_idx[1] < 2:
                            wt(sy, wC_ex)
                        reg = (ic % 2) * 5632
                        sT.inc(sy.dma_start(
                            out=wbufC[:, reg:reg + 5632],
                            in_=w2src[Q * 128:(Q + 1) * 128, :]), 16)
                        wC_marks[(ip, Q)] = sT.n
                        wc_idx[0] += 1
            block.sync(_)

            gate_pe = []
            sig_pred = [sAC.n + j + 1 for j in range(TC)]

            def _(t):
                t.wait_ge(sDV.h, xlnT_mark)
                t.wait_ge(sC.h, cast["gate"])
                wt(t, M["pu_free"])
                for j in range(TC):
                    if j >= 2:
                        t.wait_ge(sAC.h, sig_pred[j - 2])
                    dst = pu[:, 0:NE] if j % 2 == 0 else pu[:, 512:512 + NE]
                    for hc in range(HCN):
                        ins = t.matmul(dst, xlnT[:, hc * T + j * 128:hc * T + (j + 1) * 128],
                                       gateT[:, hc * NE:(hc + 1) * NE],
                                       start=(hc == 0), stop=(hc == HCN - 1))
                    sPE.inc(ins)
                    gate_pe.append(sPE.n)
            block.tensor(_)

            sig = []
            gv_pred = [sDV.n + 20 * (j + 1) for j in range(TC)]

            def _(s):
                for j in range(TC):
                    s.wait_ge(sPE.h, gate_pe[j])
                    if j >= 1:
                        s.wait_ge(sDV.h, gv_pred[j - 1])
                    src = pu[:, 0:NE] if j % 2 == 0 else pu[:, 512:512 + NE]
                    sAC.inc(s.activation(scores[:, :], src, AF.Sigmoid))
                    sig.append(sAC.n)
            block.scalar(_)
            assert sig == sig_pred

            gv = []

            def _(v):
                for j in range(TC):
                    v.wait_ge(sAC.h, sig[j])

                    def stp(ins):
                        sDV.inc(ins)
                        v.wait_ge(sDV.h, sDV.n)
                    stp(v.tensor_add(routing[:, :], scores[:, :], biasb[:, :]))
                    stp(v.reduce_max(idx1[:, :], routing[:, :], axis=AX))
                    stp(v.tensor_single_scalar(eqt[:, :], routing[:, :], idx1[:, :], ALU.is_equal))
                    stp(v.scalar_tensor_tensor(cand[:, :], eqt[:, :], -BIG, iopb_sb[:, :],
                                               ALU.mult, ALU.add))
                    stp(v.tensor_reduce(idx1[:, :], cand[:, :], axis=AX, op=ALU.min))
                    stp(v.tensor_single_scalar(oh1[:, :], iota_sb[:, :], idx1[:, :], ALU.is_equal))
                    stp(v.scalar_tensor_tensor(r2t[:, :], oh1[:, :], -BIG, routing[:, :],
                                               ALU.mult, ALU.add))
                    stp(v.reduce_max(idx1[:, :], r2t[:, :], axis=AX))
                    stp(v.tensor_single_scalar(eqt[:, :], r2t[:, :], idx1[:, :], ALU.is_equal))
                    stp(v.scalar_tensor_tensor(cand[:, :], eqt[:, :], -BIG, iopb_sb[:, :],
                                               ALU.mult, ALU.add))
                    stp(v.tensor_reduce(idx1[:, :], cand[:, :], axis=AX, op=ALU.min))
                    stp(v.tensor_single_scalar(oh2[:, :], iota_sb[:, :], idx1[:, :], ALU.is_equal))
                    stp(v.scalar_tensor_tensor(junk8[:, :], scores[:, :], 1.0, oh1[:, :],
                                               ALU.mult, ALU.mult, accum_out=sel1[:, :]))
                    stp(v.scalar_tensor_tensor(junk8[:, :], scores[:, :], 1.0, oh2[:, :],
                                               ALU.mult, ALU.mult, accum_out=sel2[:, :]))
                    stp(v.tensor_add(gsum[:, :], sel1[:, :], sel2[:, :]))
                    stp(v.reciprocal(ginv[:, :], gsum[:, :]))
                    stp(v.tensor_mul(g1[:, :], sel1[:, :], ginv[:, :]))
                    stp(v.tensor_mul(g2[:, :], sel2[:, :], ginv[:, :]))
                    stp(v.tensor_scalar(junk8[:, :], oh1[:, :], g1[:, :], None, ALU.mult))
                    stp(v.scalar_tensor_tensor(c_all[:, j * NE:(j + 1) * NE], oh2[:, :],
                                               g2[:, :], junk8[:, :], ALU.mult, ALU.add))
                    gv.append(sDV.n)
            block.vector(_)
            assert gv == gv_pred

            ctp = []
            ctc_pred = [sDV.n + j + 1 for j in range(TC)]

            def _(t):
                t.wait_ge(sAC.h, sig[-1])
                for j in range(TC):
                    t.wait_ge(sDV.h, gv[j])
                    if j >= 1:
                        t.wait_ge(sDV.h, ctc_pred[j - 1])
                    sPE.inc(t.transpose(pu[0:NE, 0:128], c_all[:, j * NE:(j + 1) * NE],
                                        id_sb[:, :]))
                    ctp.append(sPE.n)
            block.tensor(_)

            ctc = []

            def _(v):
                for j in range(TC):
                    v.wait_ge(sPE.h, ctp[j])
                    sDV.inc(v.tensor_copy(cT[:, j * 128:(j + 1) * 128], pu[0:NE, 0:128]))
                    ctc.append(sDV.n)
            block.vector(_)
            assert ctc == ctc_pred
            cT_mark = sDV.n

            pu_reader = cT_mark
            out_done = 0
            bufK_war = list(M["bufK_free"])
            prev_tail = [None] * 4
            prev_evac_base = [0]
            for ip, path in enumerate(PATHS):
                is_exp = path != "sh"
                if is_exp:
                    def _(t, e=path, pr=pu_reader):
                        t.wait_ge(sDV.h, max(cT_mark, pr))
                        for half in range(2):
                            sPE.inc(t.matmul(pu[:, half * 512:(half + 1) * 512],
                                             oh_sb[:, e * 128:(e + 1) * 128],
                                             cT[:, half * 512:(half + 1) * 512],
                                             start=True, stop=True))
                    block.tensor(_)
                    cbc_mark = sPE.n

                silu_m = {}
                mul2_m = {}
                h1_m = {}
                h3_m = {}
                for f in range(FCN):
                    def _(t, f=f, ip=ip):
                        t.wait_ge(sT.h, wA_marks[(ip, f)])
                        t.wait_ge(sT.h, wB_marks[(ip, f)])
                        slotA = wbufA[:, ((wa_seq[0] + f) % 2) * H:((wa_seq[0] + f) % 2 + 1) * H]
                        slotB = wbufB[:, ((wb_seq[0] + f) % 2) * H:((wb_seq[0] + f) % 2 + 1) * H]
                        for half in range(2):
                            mk = silu_m.get((f - 1, half))
                            if mk is not None:
                                t.wait_ge(sAC.h, mk)
                            else:
                                wt(t, M["pb_free"][half])
                            for hc in range(HCN):
                                ins = t.matmul(ph1[:, half * 512:(half + 1) * 512],
                                               slotA[:, hc * 128:(hc + 1) * 128],
                                               xlnT[:, hc * T + half * 512:hc * T + (half + 1) * 512],
                                               start=(hc == 0), stop=(hc == HCN - 1))
                            sPE.inc(ins)
                            h1_m[(f, half)] = sPE.n
                        t.wait_ge(sPE.h, h1_m[(f, 1)])
                        sWa.inc(t.nop())
                        for half in range(2):
                            mk = mul2_m.get((f - 1, half))
                            if mk is not None:
                                t.wait_ge(sDV.h, mk)
                            else:
                                wt(t, M["pb_free"][2 + half])
                            for hc in range(HCN):
                                ins = t.matmul(ph3[:, half * 512:(half + 1) * 512],
                                               slotB[:, hc * 128:(hc + 1) * 128],
                                               xlnT[:, hc * T + half * 512:hc * T + (half + 1) * 512],
                                               start=(hc == 0), stop=(hc == HCN - 1))
                            sPE.inc(ins)
                            h3_m[(f, half)] = sPE.n
                        t.wait_ge(sPE.h, h3_m[(f, 1)])
                        sWb.inc(t.nop())
                    block.tensor(_)

                    def _(s, f=f):
                        for half in range(2):
                            s.wait_ge(sPE.h, h1_m[(f, half)])
                            mk = mul2_m.get((f - 1, half))
                            if mk is not None:
                                s.wait_ge(sDV.h, mk)
                            sAC.inc(s.activation(tmp_silu[:, half * 512:(half + 1) * 512],
                                                 ph1[:, half * 512:(half + 1) * 512], AF.Silu))
                            silu_m[(f, half)] = sAC.n
                    block.scalar(_)

                    def _(v, f=f, is_exp=is_exp):
                        for half in range(2):
                            v.wait_ge(sAC.h, silu_m[(f, half)])
                            v.wait_ge(sPE.h, h3_m[(f, half)])
                            if f == 0 and half == 0:
                                if out_done:
                                    v.wait_ge(sPE.h, out_done)
                                else:
                                    wt(v, bufK_war)
                                if is_exp:
                                    v.wait_ge(sPE.h, cbc_mark)
                            ts = tmp_silu[:, half * 512:(half + 1) * 512]
                            if is_exp:
                                sDV.inc(v.tensor_mul(ts, ts, pu[:, half * 512:(half + 1) * 512]))
                                v.wait_ge(sDV.h, sDV.n)
                            sDV.inc(v.tensor_mul(
                                bufK[:, f * T + half * 512:f * T + (half + 1) * 512],
                                ts, ph3[:, half * 512:(half + 1) * 512]))
                            mul2_m[(f, half)] = sDV.n
                    block.vector(_)
                if DEBUG_FLUSH_ACT:
                    def _(v):
                        v.wait_ge(sDV.h, sDV.n)
                        for rep in range(3):
                            sDV.inc(v.tensor_copy(junk8[:, :], bufK[0:128, 0:2 * NE].bitcast(F32)))
                            v.wait_ge(sDV.h, sDV.n)
                    block.vector(_)
                wa_seq[0] += FCN
                wb_seq[0] += FCN
                pu_reader = sDV.n
                act_ready = sDV.n
                M["pb_free"] = [[(sAC, silu_m[(FCN - 1, 0)])], [(sAC, silu_m[(FCN - 1, 1)])],
                                [(sDV, mul2_m[(FCN - 1, 0)])], [(sDV, mul2_m[(FCN - 1, 1)])]]

                tile_m = {}
                evac_m = {}
                for Q in range(4):
                    for jg in range(2):
                        def _(t, Q=Q, jg=jg, ip=ip, act_ready=act_ready):
                            t.wait_ge(sT.h, wC_marks[(ip, Q)])
                            if Q == 0 and jg == 0:
                                t.wait_ge(sDV.h, act_ready)
                                if ip == 0:
                                    wt(t, M["po_free"] + M["pu_free"])
                            reg = ((wc_seq[0] + Q) % 2) * 5632
                            for j in range(jg * 4, jg * 4 + 4):
                                tl = Q * 8 + j
                                bank = tl % 4
                                if DEBUG_SERIAL_OUT and (tl > 0 or prev_evac_base[0]):
                                    t.wait_ge(sDV.h, (act_ready + tl) if tl > 0 else prev_evac_base[0])
                                if tl >= 4:
                                    t.wait_ge(sDV.h, evac_m[tl - 4])
                                elif prev_tail[bank] is not None:
                                    t.wait_ge(sDV.h, prev_tail[bank])
                                dst = (po if bank < 2 else pu)[:, (bank % 2) * 512:(bank % 2) * 512 + 512]
                                for f in range(FCN):
                                    ins = t.matmul(dst,
                                                   bufK[:, f * T + j * 128:f * T + (j + 1) * 128],
                                                   wbufC[:, reg + f * 512:reg + (f + 1) * 512],
                                                   start=(f == 0), stop=(f == FCN - 1))
                                sPE.inc(ins)
                                tile_m[tl] = sPE.n
                            if jg == 1:
                                t.wait_ge(sPE.h, tile_m[Q * 8 + 7])
                                sWc.inc(t.nop())
                        block.tensor(_)

                        def _(v, Q=Q, jg=jg):
                            for j in range(jg * 4, jg * 4 + 4):
                                tl = Q * 8 + j
                                bank = tl % 4
                                v.wait_ge(sPE.h, tile_m[tl])
                                src = (po if bank < 2 else pu)[:, (bank % 2) * 512:(bank % 2) * 512 + 512]
                                dst = xres[:, j * H + Q * 512:j * H + (Q + 1) * 512]
                                sDV.inc(v.scalar_tensor_tensor(dst, src, 0.5, dst,
                                                               ALU.mult, ALU.add))
                                evac_m[tl] = sDV.n
                        block.vector(_)
                wc_seq[0] += 4
                out_done = sPE.n
                prev_evac_base[0] = evac_m[31]
                prev_tail = [evac_m[28], evac_m[29], evac_m[30], evac_m[31]]
                pu_reader = sDV.n
                M["po_free"] = [(sDV, sDV.n)]
                M["pu_free"] = [(sDV, sDV.n)]

            M["bufK_free"] = [(sPE, sPE.n)]
            M["xlnT_free"] = [(sPE, sPE.n)]
            M["xres_w"] = [(sDV, sDV.n)]
            M["bufV_free"] = [(sDV, sDV.n), (sPE, sPE.n)]
            return sDV.n

        wa_seq = [0]
        wb_seq = [0]
        wc_seq = [0]

        def attn_phase(xlnT_mark):
            qkv_marks = {}

            def emit_slabs(fb_list):
                def _(sy):
                    wA_ex, M["wA_extra"] = M["wA_extra"], []
                    for Fb in fb_list:
                        sy.wait_ge(sC.h, cast[("wqkvb", Fb // 8)])
                        ia = wa_idx[0]
                        if ia >= 2:
                            sy.wait_ge(sWa.h, ia - 1)
                        if wa_idx[0] - wa_idx[1] < 2:
                            wt(sy, wA_ex)
                        sT.inc(sy.dma_start(out=wbufA[:, (ia % 2) * H:(ia % 2 + 1) * H],
                                            in_=wqkv_sw[Fb * 128:(Fb + 1) * 128, :]), 16)
                        qkv_marks[Fb] = sT.n
                        wa_idx[0] += 1
                block.sync(_)
            wa_idx[1] = wa_idx[0]
            emit_slabs(list(range(16, 48)))

            def _(g):
                g.wait_ge(sWb.h, wb_seq[0])
                sC.inc(g.dma_start(out=bprojT, in_=b_proj.rearrange("h -> () h")), 16)
                cast["bproj"] = sC.n
            block.gpsimd(_)

            evac_kv = {}
            tile_kv = {}
            kv_tr = {}
            kv_cp = {}
            kv_stage_init = [0]

            for k, Fb in enumerate(range(16, 48)):
                is_k = Fb < 32

                def _(t, k=k, Fb=Fb):
                    if k == 0:
                        t.wait_ge(sDV.h, xlnT_mark)
                    t.wait_ge(sT.h, qkv_marks[Fb])
                    slot = wbufA[:, ((wa_seq[0] + k) % 2) * H:((wa_seq[0] + k) % 2 + 1) * H]
                    for th in range(2):
                        gt = 2 * k + th
                        if gt >= 4:
                            s_, n_ = evac_kv[gt - 4]
                            t.wait_ge(s_.h, n_)
                        else:
                            wt(t, M["pb_free"][gt])
                        for hc in range(HCN):
                            ins = t.matmul(pbank[gt % 4],
                                           slot[:, hc * 128:(hc + 1) * 128],
                                           xlnT[:, hc * T + th * 512:hc * T + (th + 1) * 512],
                                           start=(hc == 0), stop=(hc == HCN - 1))
                        sPE.inc(ins)
                        tile_kv[gt] = sPE.n
                    t.wait_ge(sPE.h, tile_kv[2 * k + 1])
                    sWa.inc(t.nop())
                block.tensor(_)

                def _(e, k=k, Fb=Fb, is_k=is_k):
                    for th in range(2):
                        gt = 2 * k + th
                        e.wait_ge(sPE.h, tile_kv[gt])
                        if k >= 3:
                            e.wait_ge(sPE.h, kv_tr[2 * (k - 3) + 1])
                        elif kv_stage_init[0] == 0:
                            e.wait_ge(sWc.h, wc_seq[0])
                        dst = kv_stage[k % 3][:, th * 512:(th + 1) * 512]
                        if is_k:
                            ins = e.activation(dst, pbank[gt % 4], AF.Relu)
                            evac_kv[gt] = (sAC, sAC.inc(ins))
                        else:
                            ins = e.tensor_copy(dst, pbank[gt % 4])
                            evac_kv[gt] = (sDV, sDV.inc(ins))
                    kv_stage_init[0] = 1
                (block.scalar if is_k else block.vector)(_)

                def _(t, k=k, Fb=Fb):
                    for th in range(2):
                        step = 2 * k + th
                        s_, n_ = evac_kv[step]
                        t.wait_ge(s_.h, n_)
                        if step >= 2:
                            t.wait_ge(sDV.h, kv_cp[step - 2])
                        else:
                            wt(t, M["po_free"])
                        off = (step % 2) * 512
                        for i in range(4):
                            ins = t.matmul(
                                po[:, off + i * 128:off + (i + 1) * 128],
                                kv_stage[k % 3][:, th * 512 + i * 128:th * 512 + (i + 1) * 128],
                                id_bf[:, :], start=True, stop=True)
                        sPE.inc(ins)
                        kv_tr[step] = sPE.n
                block.tensor(_)

                def _(v, k=k, Fb=Fb, is_k=is_k):
                    dstbuf = bufK if is_k else bufV
                    fc = (Fb - 16) if is_k else (Fb - 32)
                    for th in range(2):
                        step = 2 * k + th
                        v.wait_ge(sPE.h, kv_tr[step])
                        if is_k and step == 0:
                            wt(v, M["bufK_free"])
                        if (not is_k) and k == 16 and th == 0:
                            wt(v, M["bufV_free"])
                        off = (step % 2) * 512
                        dst = dstbuf[:, :].rearrange("p (j h) -> p j h", h=H)[
                            :, th * 4:(th + 1) * 4, fc * 128:(fc + 1) * 128]
                        sDV.inc(v.tensor_copy(
                            dst, po[:, off:off + 512].rearrange("p (i t) -> p i t", t=128)))
                        kv_cp[step] = sDV.n
                block.vector(_)
            wa_seq[0] += 32
            M["pb_free"] = [[evac_kv[60]], [evac_kv[61]], [evac_kv[62]], [evac_kv[63]]]
            M["po_free"] = [(sDV, kv_cp[63])]

            if ATTN_UPTO < 2:
                return sDV.n
            if ATTN_UPTO >= 4:
                emit_slabs(list(range(16)))

            def _(t):
                t.wait_ge(sDV.h, kv_cp[63])
                wt(t, M["pu_free"])
                for bank in range(2):
                    first = True
                    for g in range(8):
                        hgrp = bank * 8 + g
                        for q in range(4):
                            hd = 4 * hgrp + q
                            col0 = bank * 512 + g * 33
                            for j in range(TC):
                                t.matmul(pu[32 * q:32 * q + 32, col0:col0 + 32],
                                         bufK[:, j * H + hd * 32:j * H + (hd + 1) * 32],
                                         bufV[:, j * H + hd * 32:j * H + (hd + 1) * 32],
                                         start=first, stop=False,
                                         tile_position=(0, 32 * q))
                                first = False
                                last = (g == 7 and q == 3 and j == TC - 1)
                                ins = t.matmul(pu[32 * q:32 * q + 32, col0 + 32:col0 + 33],
                                               bufK[:, j * H + hd * 32:j * H + (hd + 1) * 32],
                                               oc_bf[:, :],
                                               start=False, stop=last,
                                               tile_position=(0, 32 * q))
                    sPE.inc(ins)
                for col in (500, 1020):
                    sPE.inc(t.matmul(pu[:, col:col + 1], id_bf[:, :], oc_bf[:, :],
                                     start=True, stop=True))
            block.tensor(_)
            vkT_mm_mark = sPE.n

            if ATTN_UPTO < 3:
                return sDV.n
            def _(v):
                v.wait_ge(sPE.h, vkT_mm_mark)
                sDV.inc(v.tensor_copy(vkT_sb[:, 0:264], pu[:, 0:264]))
                sDV.inc(v.tensor_copy(vkT_sb[:, 264:528], pu[:, 512:776]))
            block.vector(_)
            vkT_ev_mark = sDV.n
            M["pu_free"] = [(sDV, sDV.n)]

            def _(g):
                g.wait_ge(sDV.h, vkT_ev_mark)
                sB.inc(g.dma_start(out=cc_in.ap(), in_=vkT_sb[:, :]), 16)
                g.wait_ge(sB.h, sB.n)
                sCC.inc(g.collective_compute(
                    "AllReduce", ALU.add,
                    replica_groups=[[0, 1, 2, 3], [4, 5, 6, 7]],
                    ins=[cc_in.ap().opt()], outs=[cc_out.ap().opt()]))
                g.wait_ge(sCC.h, sCC.n)
                sB.inc(g.dma_start(out=vkT_sb[:, :], in_=cc_out.ap()), 16)
            block.gpsimd(_)
            cc_mark = sB.n

            def _(v):
                v.wait_ge(sB.h, cc_mark)
                sDV.inc(v.tensor_copy(vkT_bf[:, :], vkT_sb[:, :]))
            block.vector(_)
            vkT_bf_mark = sDV.n

            if ATTN_UPTO < 4:
                return sDV.n
            tile_q = {}
            qev = []

            for fb in range(16):
                def _(t, fb=fb):
                    t.wait_ge(sT.h, qkv_marks[fb])
                    slot = wbufA[:, ((wa_seq[0] + fb) % 2) * H:((wa_seq[0] + fb) % 2 + 1) * H]
                    for th in range(2):
                        gt = 2 * fb + th
                        if gt >= 4:
                            s_, n_ = qev[gt - 4]
                            t.wait_ge(s_.h, n_)
                        else:
                            wt(t, M["pb_free"][gt])
                        for hc in range(HCN):
                            ins = t.matmul(pbank[gt % 4],
                                           slot[:, hc * 128:(hc + 1) * 128],
                                           xlnT[:, hc * T + th * 512:hc * T + (th + 1) * 512],
                                           start=(hc == 0), stop=(hc == HCN - 1))
                        sPE.inc(ins)
                        tile_q[gt] = sPE.n
                    t.wait_ge(sPE.h, tile_q[2 * fb + 1])
                    sWa.inc(t.nop())
                block.tensor(_)

                def _(s, fb=fb):
                    for th in range(2):
                        gt = 2 * fb + th
                        s.wait_ge(sPE.h, tile_q[gt])
                        if gt == 0:
                            s.wait_ge(sPE.h, vkT_mm_mark)
                        ins = s.activation(bufK[:, fb * T + th * 512:fb * T + (th + 1) * 512],
                                           pbank[gt % 4], AF.Relu)
                        qev.append((sAC, sAC.inc(ins)))
                block.scalar(_)
            wa_seq[0] += 16
            q_done = sAC.n
            M["pb_free"] = [[qev[28]], [qev[29]], [qev[30]], [qev[31]]]

            if ATTN_UPTO < 5:
                return sDV.n
            epi_m = {}
            otile_m = {}

            for j in range(TC):
                for tp in range(4):
                    def _(t, j=j, tp=tp):
                        if j == 0 and tp == 0:
                            t.wait_ge(sDV.h, vkT_bf_mark)
                            t.wait_ge(sAC.h, q_done)
                            wt(t, M["po_free"])
                        for tt in range(2 * tp, 2 * tp + 2):
                            gi = j * 8 + tt
                            q4, sh = tt % 4, tt // 4
                            if gi >= 2:
                                mk = epi_m.get(gi - 2)
                                if mk is not None:
                                    t.wait_ge(sDV.h, mk)
                            off = (gi % 2) * 512
                            for i in range(8):
                                hgrp = 8 * sh + i
                                ins = t.matmul(
                                    po[:, off + i * 33:off + (i + 1) * 33],
                                    bufK[32 * q4:32 * q4 + 32,
                                         hgrp * T + j * 128:hgrp * T + (j + 1) * 128],
                                    vkT_bf[32 * q4:32 * q4 + 32, hgrp * 33:(hgrp + 1) * 33],
                                    start=(i == 0), stop=(i == 7),
                                    tile_position=(32 * q4, 0))
                            ins = t.matmul(po[:, off + 490:off + 491],
                                           id_bf[:, :], oc_bf[:, :],
                                           start=True, stop=True)
                            sPE.inc(ins)
                            otile_m[gi] = sPE.n
                    block.tensor(_)

                    def _(v, j=j, tp=tp):
                        if ATTN_UPTO == 41:
                            for tt in range(2 * tp, 2 * tp + 2):
                                gi = j * 8 + tt
                                v.wait_ge(sPE.h, otile_m[gi])
                                sDV.inc(v.tensor_copy(dnm[:, :], po[0:128, 0:NE]))
                                epi_m[gi] = sDV.n
                            return
                        for tt in range(2 * tp, 2 * tp + 2):
                            gi = j * 8 + tt
                            q4, sh = tt % 4, tt // 4
                            v.wait_ge(sPE.h, otile_m[gi])
                            off = (gi % 2) * 512
                            p3 = po[:, off:off + 264].rearrange("p (h e) -> p h e", e=33)
                            sDV.inc(v.tensor_single_scalar(dnm[:, :], p3[:, :, 32:33].rearrange("p h e -> p (h e)"),
                                                           EPS_AT, ALU.add))
                            v.wait_ge(sDV.h, sDV.n)
                            sDV.inc(v.reciprocal(rcp[:, :], dnm[:, :]))
                            v.wait_ge(sDV.h, sDV.n)
                            base = j * H + 1024 * sh
                            ov = bufV[:, base:base + 1024].rearrange("p (i r) -> p i r", r=128)
                            sDV.inc(v.tensor_tensor(
                                ov[:, :, 32 * q4:32 * q4 + 32],
                                p3[:, :, 0:32],
                                rcp[:, :].rearrange("p h -> p h ()").broadcast_to([128, NE, 32]),
                                ALU.mult))
                            epi_m[gi] = sDV.n
                    block.vector(_)

            M["po_free"] = [(sDV, sDV.n)]
            attn_out_done = sPE.n

            if ATTN_UPTO < 6:
                return sDV.n
            epi_j = [epi_m[j * 8 + 7] for j in range(TC)]
            afm_mark = pe_transpose_to_xlnT(
                lambda j, hc: bufV[:, j * H + hc * 128:j * H + (hc + 1) * 128], epi_j)

            pset = {}

            def _(sy):
                sy.wait_ge(sC.h, cast["wproj"])
                wc_idx[1] = wc_idx[0]
                for Q in range(4):
                    for jh in range(2):
                        for ch in range(2):
                            ic = wc_idx[0]
                            if ic >= 2:
                                sy.wait_ge(sWc.h, ic - 1)
                            reg = (ic % 2) * 5632
                            sT.inc(sy.dma_start(
                                out=wbufC[:, reg:reg + 4096],
                                in_=wproj_bf[(Q * 2 + ch) * 128:(Q * 2 + ch + 1) * 128, :]), 16)
                            pset[(Q, jh, ch)] = sT.n
                            wc_idx[0] += 1
            block.sync(_)

            pj_evac = {}
            for Q in range(4):
                for jh in range(2):
                    ptile = {}

                    def _(t, Q=Q, jh=jh):
                        t.wait_ge(sT.h, pset[(Q, jh, 0)])
                        if Q == 0 and jh == 0:
                            t.wait_ge(sDV.h, afm_mark)
                            t.wait_ge(sC.h, cast["bproj"])
                            wt(t, M["pb_free"][0] + M["pb_free"][1]
                               + M["pb_free"][2] + M["pb_free"][3])
                        else:
                            prev = (Q, jh - 1) if jh else (Q - 1, 1)
                            for jj in range(4):
                                t.wait_ge(sDV.h, pj_evac[(prev[0], prev[1], jj)])
                        for ch in range(2):
                            if ch == 1:
                                t.wait_ge(sT.h, pset[(Q, jh, 1)])
                            reg = ((wc_seq[0] + 2 * (2 * Q + jh) + ch) % 2) * 5632
                            for jj in range(4):
                                j = jh * 4 + jj
                                for cc in range(8):
                                    c0 = 8 * ch + cc
                                    ins = t.matmul(
                                        pbank[jj],
                                        xlnT[:, c0 * T + j * 128:c0 * T + (j + 1) * 128],
                                        wbufC[:, reg + cc * 512:reg + (cc + 1) * 512],
                                        start=(ch == 0 and cc == 0), stop=False)
                                if ch == 1:
                                    ins = t.matmul(pbank[jj], or_bf[:, :],
                                                   bprojT[0:1, Q * 512:(Q + 1) * 512],
                                                   start=False, stop=(cc == 7))
                                sPE.inc(ins)
                                ptile[jj] = sPE.n
                            if ch == 1:
                                t.wait_ge(sPE.h, ptile[3])
                                sWc.inc(t.nop(), 2)
                    block.tensor(_)

                    def _(v, Q=Q, jh=jh):
                        for jj in range(4):
                            j = jh * 4 + jj
                            v.wait_ge(sPE.h, ptile[jj])
                            dst = xres[:, j * H + Q * 512:j * H + (Q + 1) * 512]
                            sDV.inc(v.tensor_add(dst, dst, pbank[jj]))
                            pj_evac[(Q, jh, jj)] = sDV.n
                    block.vector(_)
            wc_seq[0] += 16
            proj_done_pe = sPE.n
            M["pb_free"] = [[(sDV, pj_evac[(3, 1, jj)])] for jj in range(4)]
            M["bufK_free"] = [(sPE, attn_out_done)]
            M["xlnT_free"] = [(sPE, proj_done_pe)]
            M["xres_w"] = [(sDV, sDV.n)]
            M["bufV_free"] = [(sPE, proj_done_pe)]
            M["wB_extra"] = [(sPE, proj_done_pe)]
            M["wC_extra"] = [(sPE, kv_tr[63])]
            return sDV.n

        m1 = ln_phase()
        ffn_phase(m1)
        if DEBUG_DUMP:
            def _(sy):
                wt(sy, M["xres_w"])
                wt(sy, M["bufK_free"])
                sT.inc(sy.dma_start(out=dbg_actK, in_=bufK[:, :]), 16)
                sT.inc(sy.dma_start(out=dbg_bufV, in_=bufV[:, :]), 16)
                sT.inc(sy.dma_start(out=dbg_call, in_=c_all[:, :]), 16)
                sT.inc(sy.dma_start(out=dbg_xlnT, in_=xlnT[:, :]), 16)
                sy.wait_ge(sT.h, sT.n)
            block.sync(_)
        if STAGES >= 2:
            m2 = ln_phase()
            attn_phase(m2)
        if STAGES >= 3:
            m3 = ln_phase()
            ffn_phase(m3)

        def _(sy):
            wt(sy, M["xres_w"])
            sT.inc(sy.dma_start(out=out.rearrange("(j p) h -> p j h", p=128),
                                in_=xres[:, :].rearrange("p (j h) -> p j h", h=H)), 16)
            sy.wait_ge(sT.h, sT.n)
        block.sync(_)

    return nc


_NC_CACHE = {}


def host_swz(w):
    R = w.shape[0]
    return np.ascontiguousarray(
        w.reshape(R // 128, 128, HCN, 128).transpose(0, 3, 2, 1).reshape(R, HCN * 128))


def host_w2(w):
    return np.ascontiguousarray(
        w.reshape(4, 512, FCN, 128).transpose(0, 3, 2, 1).reshape(512, FCN * 512))


def host_proj(w):
    return np.ascontiguousarray(
        w.reshape(4, 512, 2, 8, 128).transpose(0, 2, 4, 3, 1).reshape(1024, 4096))


def kernel(**inputs):
    x = np.asarray(inputs["L0_attn_x"])
    if "prog" not in _NC_CACHE:
        _NC_CACHE["prog"] = build_program()
    nc = _NC_CACHE["prog"]
    we2_np = np.asarray(inputs["L0_ffn_we2"])
    common = {
        "wqkv_s": host_swz(np.asarray(inputs["L0_attn_w_qkv"])),
        "w_proj": host_proj(np.asarray(inputs["L0_attn_w_proj"])),
        "b_proj": inputs["L0_attn_b_proj"],
        "g_bias": inputs["L0_ffn_bias"],
        "gate_wt": np.asarray(inputs["L0_ffn_gate_w"]).T,
        "ws1_s": host_swz(np.asarray(inputs["L0_ffn_ws1"])),
        "ws2": host_w2(np.asarray(inputs["L0_ffn_ws2"])),
        "ws3_s": host_swz(np.asarray(inputs["L0_ffn_ws3"])),
        "we1_s": host_swz(np.asarray(inputs["L0_ffn_we1"]).reshape(-1, H)),
        "we2": np.concatenate([host_w2(we2_np[e]) for e in range(NE)], axis=0),
        "we3_s": host_swz(np.asarray(inputs["L0_ffn_we3"]).reshape(-1, H)),
    }
    common = {k: np.ascontiguousarray(v) for k, v in common.items()}
    in_maps = []
    for c in range(8):
        m = dict(common)
        m["x"] = np.ascontiguousarray(x[c // 4, (c % 4) * T:(c % 4 + 1) * T, :])
        in_maps.append(m)
    res = run_bass_kernel_spmd(nc, in_maps, core_ids=list(range(8)))
    y = np.empty((2, 4096, H), np.float32)
    for c in range(8):
        y[c // 4, (c % 4) * T:(c % 4 + 1) * T, :] = res.results[c]["out"]
    return y


# version 08
# speedup vs baseline: 1.2250x; 1.0514x over previous
import contextlib
import numpy as np
import ml_dtypes
import concourse.bass as bass
import concourse.mybir as mybir
from concourse.bass_utils import run_bass_kernel_spmd

F32 = mybir.dt.float32
BF16 = mybir.dt.bfloat16
AF = mybir.ActivationFunctionType
ALU = mybir.AluOpType
AX = mybir.AxisListType.X

T = 1024
TC = 8
H = 2048
HCN = 16
FF = 1408
FCN = 11
NE = 8
BIG = 1.0e4
EPS_LN = 1.0e-5
EPS_AT = 1.0e-15
STAGES = 3
ATTN_UPTO = 9
DEBUG_DUMP = False
DEBUG_SERIAL_OUT = False
DEBUG_FLUSH_ACT = False
PATHS = ["sh"] + list(range(NE))


class Sem:
    def __init__(self, handle):
        self.h = handle
        self.n = 0

    def inc(self, instr, k=1):
        instr.then_inc(self.h, k)
        self.n += k
        return self.n


def build_program():
    nc = bass.Bass("TRN2", target_bir_lowering=False, debug=False)

    x_in = nc.dram_tensor("x", [T, H], F32, kind="ExternalInput").ap()
    out = nc.dram_tensor("out", [T, H], F32, kind="ExternalOutput").ap()
    wqkv_s = nc.dram_tensor("wqkv_s", [3 * H, H], F32, kind="ExternalInput").ap()
    w_proj = nc.dram_tensor("w_proj", [8 * 128, 8 * 512], F32, kind="ExternalInput").ap()
    b_proj = nc.dram_tensor("b_proj", [H], F32, kind="ExternalInput").ap()
    gate_wt = nc.dram_tensor("gate_wt", [H, NE], F32, kind="ExternalInput").ap()
    g_bias = nc.dram_tensor("g_bias", [NE], F32, kind="ExternalInput").ap()
    ws1_s = nc.dram_tensor("ws1_s", [FF, H], F32, kind="ExternalInput").ap()
    ws2 = nc.dram_tensor("ws2", [512, FCN * 512], F32, kind="ExternalInput").ap()
    ws3_s = nc.dram_tensor("ws3_s", [FF, H], F32, kind="ExternalInput").ap()
    we1_s = nc.dram_tensor("we1_s", [NE * FF, H], F32, kind="ExternalInput").ap()
    we2 = nc.dram_tensor("we2", [NE * 512, FCN * 512], F32, kind="ExternalInput").ap()
    we3_s = nc.dram_tensor("we3_s", [NE * FF, H], F32, kind="ExternalInput").ap()

    ws1_sw = nc.dram_tensor("ws1_sw", [FF, H], BF16).ap()
    ws3_sw = nc.dram_tensor("ws3_sw", [FF, H], BF16).ap()
    we1_sw = nc.dram_tensor("we1_sw", [NE * FF, H], BF16).ap()
    we3_sw = nc.dram_tensor("we3_sw", [NE * FF, H], BF16).ap()
    wqkv_sw = nc.dram_tensor("wqkv_sw", [3 * H, H], BF16).ap()
    ws2_bf = nc.dram_tensor("ws2_bf", [512, FCN * 512], BF16).ap()
    we2_bf = nc.dram_tensor("we2_bf", [NE * 512, FCN * 512], BF16).ap()
    wproj_bf = nc.dram_tensor("wproj_bf", [8 * 128, 8 * 512], BF16).ap()
    cc_in = nc.dram_tensor("cc_in", [128, 528], F32)
    cc_out = nc.dram_tensor("cc_out", [128, 528], F32)
    if DEBUG_DUMP:
        dbg_actK = nc.dram_tensor("dbg_actK", [128, HCN * T], BF16, kind="ExternalOutput").ap()
        dbg_bufV = nc.dram_tensor("dbg_bufV", [128, HCN * T], BF16, kind="ExternalOutput").ap()
        dbg_call = nc.dram_tensor("dbg_call", [128, TC * NE], F32, kind="ExternalOutput").ap()
        dbg_xlnT = nc.dram_tensor("dbg_xlnT", [128, HCN * T], BF16, kind="ExternalOutput").ap()

    ident_f = nc.inline_tensor(np.eye(128, dtype=np.float32), "ident_f").ap()
    ident_bf = nc.inline_tensor(np.eye(128, dtype=np.float32).astype(ml_dtypes.bfloat16),
                                "ident_bf").ap()
    oh_np = np.zeros((NE, NE * 128), dtype=np.float32)
    for e in range(NE):
        oh_np[e, e * 128:(e + 1) * 128] = 1.0
    onehot_f = nc.inline_tensor(oh_np, "onehot_f").ap()
    ones_col_bf = nc.inline_tensor(np.ones((128, 1), dtype=ml_dtypes.bfloat16), "c_oc_bf").ap()
    ones_row_bf = nc.inline_tensor(np.ones((1, 128), dtype=ml_dtypes.bfloat16), "c_or_bf").ap()
    ones_row_f = nc.inline_tensor(np.ones((1, 128), dtype=np.float32), "c_or_f").ap()
    iota_np = np.tile(np.arange(NE, dtype=np.float32)[None, :], (128, 1))
    iota_c = nc.inline_tensor(iota_np, "iota_c").ap()
    iopb_c = nc.inline_tensor(iota_np + BIG, "iopb_c").ap()
    epsln_c = nc.inline_tensor(np.full((128, 1), EPS_LN, dtype=np.float32), "epsln_c").ap()

    with contextlib.ExitStack() as st:
        def sbt(name, shape, dtype):
            return st.enter_context(nc.sbuf_tensor(name, shape, dtype))

        xres = sbt("xres", [128, TC * H], F32)
        xlnT = sbt("xlnT", [128, HCN * T], BF16)
        bufK = sbt("bufK", [128, HCN * T], BF16)
        bufV = sbt("bufV", [128, HCN * T], BF16)
        wbufA = sbt("wbufA", [128, 2 * H], BF16)
        wbufB = sbt("wbufB", [128, 2 * H], BF16)
        wbufC = sbt("wbufC", [128, FCN * 1024], BF16)
        gateT = sbt("gateT", [128, HCN * NE], BF16)
        id_sb = sbt("id_sb", [128, 128], F32)
        id_bf = sbt("id_bf", [128, 128], BF16)
        oh_sb = sbt("oh_sb", [NE, NE * 128], F32)
        oc_bf = sbt("oc_bf", [128, 1], BF16)
        or_bf = sbt("or_bf", [1, 128], BF16)
        or_f = sbt("or_f", [1, 128], F32)
        iota_sb = sbt("iota_sb", [128, NE], F32)
        iopb_sb = sbt("iopb_sb", [128, NE], F32)
        bias_sb = sbt("bias_sb", [1, NE], F32)
        biasb = sbt("biasb", [128, NE], F32)
        stats = sbt("stats", [128, 24], F32)
        mvall = sbt("mvall", [128, 2 * TC], F32)
        negmean = sbt("negmean", [128, TC], F32)
        sdall = sbt("sdall", [128, TC], F32)
        rstd = sbt("rstd", [128, TC], F32)
        scores = sbt("scores", [128, NE], F32)
        routing = sbt("routing", [128, NE], F32)
        eqt = sbt("eqt", [128, NE], F32)
        cand = sbt("cand", [128, NE], F32)
        idx1 = sbt("idx1", [128, 1], F32)
        oh1 = sbt("oh1", [128, NE], F32)
        oh2 = sbt("oh2", [128, NE], F32)
        junk8 = sbt("junk8", [128, NE], F32)
        r2t = sbt("r2t", [128, NE], F32)
        sel1 = sbt("sel1", [128, 1], F32)
        sel2 = sbt("sel2", [128, 1], F32)
        gsum = sbt("gsum", [128, 1], F32)
        ginv = sbt("ginv", [128, 1], F32)
        g1 = sbt("g1", [128, 1], F32)
        g2 = sbt("g2", [128, 1], F32)
        c_all = sbt("c_all", [128, TC * NE], F32)
        dnm = sbt("dnm", [128, NE], F32)
        epsln = sbt("epsln", [128, 1], F32)
        rcp = sbt("rcp", [128, NE], F32)

        xln_tm = [bufV[:, j * H:(j + 1) * H] for j in range(TC)]
        tmp_silu = bufV[:, 2 * H:3 * H].bitcast(F32)
        cT = bufV[0:NE, 3 * H:4 * H].bitcast(F32)
        kv_stage = [wbufC[:, 8192 + i * 1024:8192 + (i + 1) * 1024] for i in range(3)]
        bprojT = wbufB[0:1, 0:H]
        vkT_sb = wbufB[:, H:H + 1056].bitcast(F32)
        vkT_bf = wbufB[:, H + 1056:H + 1584]

        ph1 = st.enter_context(nc.psum_tensor("ph1", [128, 1024], F32))
        ph3 = st.enter_context(nc.psum_tensor("ph3", [128, 1024], F32))
        po = st.enter_context(nc.psum_tensor("po", [128, 1024], F32))
        pu = st.enter_context(nc.psum_tensor("pu", [128, 1024], F32))
        pbank = [ph1[:, 0:512], ph1[:, 512:1024], ph3[:, 0:512], ph3[:, 512:1024]]

        sC = Sem(st.enter_context(nc.semaphore("sC")))
        sB = Sem(st.enter_context(nc.semaphore("sB")))
        sT = Sem(st.enter_context(nc.semaphore("sT")))
        sIn = Sem(st.enter_context(nc.semaphore("sIn")))
        sTa = [Sem(st.enter_context(nc.semaphore("sTa%d" % i))) for i in range(2)]
        sTb = [Sem(st.enter_context(nc.semaphore("sTb%d" % i))) for i in range(2)]
        sTc = [Sem(st.enter_context(nc.semaphore("sTc%d" % i))) for i in range(2)]
        sPE = Sem(st.enter_context(nc.semaphore("sPE")))
        sDV = Sem(st.enter_context(nc.semaphore("sDV")))
        sAC = Sem(st.enter_context(nc.semaphore("sAC")))
        sCC = Sem(st.enter_context(nc.semaphore("sCC")))
        sWa = Sem(st.enter_context(nc.semaphore("sWa")))
        sWb = Sem(st.enter_context(nc.semaphore("sWb")))
        sWc = Sem(st.enter_context(nc.semaphore("sWc")))

        block = st.enter_context(nc.Block())

        M = {}

        def _(sy):
            for dst, src in ((id_sb[:, :], ident_f), (id_bf[:, :], ident_bf),
                             (oh_sb[:, :], onehot_f),
                             (oc_bf[:, :], ones_col_bf), (or_bf[:, :], ones_row_bf),
                             (or_f[:, :], ones_row_f), (iota_sb[:, :], iota_c),
                             (iopb_sb[:, :], iopb_c), (epsln[:, :], epsln_c),
                             (bias_sb[:, :], g_bias.rearrange("e -> () e"))):
                sIn.inc(sy.dma_start(out=dst, in_=src), 16)
            sIn.inc(sy.dma_start(out=xres[:, :].rearrange("p (j h) -> p j h", h=H),
                                 in_=x_in.rearrange("(j p) h -> p j h", p=128)), 16)
        block.sync(_)
        M["consts"] = sIn.n

        cast = {}

        def _(g):
            sC.inc(g.dma_start(
                out=gateT[:, :].rearrange("p (hc e) -> p hc e", e=NE),
                in_=gate_wt.rearrange("(hc p) e -> p hc e", p=128)), 16)
            cast["gate"] = sC.n
            g.wait_ge(sC.h, sC.n)
            sC.inc(g.dma_start(out=ws1_sw, in_=ws1_s), 16)
            cast["ws1"] = sC.n
            g.wait_ge(sC.h, sC.n)
            sC.inc(g.dma_start(out=ws3_sw, in_=ws3_s), 16)
            cast["ws3"] = sC.n
            g.wait_ge(sC.h, sC.n)
            sC.inc(g.dma_start(out=ws2_bf, in_=ws2), 16)
            cast["ws2"] = sC.n
            g.wait_ge(sC.h, sC.n)
            for e in range(NE):
                g0 = 2 if e == 0 else 11 * e
                for nm, dst, src, rows in (("we1", we1_sw, we1_s, FF),
                                           ("we3", we3_sw, we3_s, FF),
                                           ("we2", we2_bf, we2, 512)):
                    for h in range(2):
                        g.wait_ge(sWa.h, g0)
                        g0 += 1
                        r0 = e * rows + h * (rows // 2)
                        sC.inc(g.dma_start(out=dst[r0:r0 + rows // 2, :],
                                           in_=src[r0:r0 + rows // 2, :]), 16)
                        g.wait_ge(sC.h, sC.n)
                    cast[(nm, e)] = sC.n
            if STAGES >= 2:
                for cb in range(6):
                    for h in range(2):
                        g.wait_ge(sWa.h, 44 + 8 * cb + 4 * h)
                        r0 = cb * 1024 + h * 512
                        sC.inc(g.dma_start(out=wqkv_sw[r0:r0 + 512, :],
                                           in_=wqkv_s[r0:r0 + 512, :]), 16)
                        g.wait_ge(sC.h, sC.n)
                    cast[("wqkvb", cb)] = sC.n
                for h in range(2):
                    g.wait_ge(sWa.h, 92 + 2 * h)
                    sC.inc(g.dma_start(out=wproj_bf[h * 512:(h + 1) * 512, :],
                                       in_=w_proj[h * 512:(h + 1) * 512, :]), 16)
                    g.wait_ge(sC.h, sC.n)
                cast["wproj"] = sC.n
        block.gpsimd(_)


        def _(t):
            t.wait_ge(sIn.h, M["consts"])
            sPE.inc(t.matmul(pu[:, 0:NE], or_f[:, :], bias_sb[:, :], start=True, stop=True))
        block.tensor(_)
        M["biasmm"] = sPE.n

        def _(v):
            v.wait_ge(sPE.h, M["biasmm"])
            sDV.inc(v.tensor_copy(biasb[:, :], pu[:, 0:NE]))
        block.vector(_)

        M["pu_free"] = [(sDV, sDV.n)]
        M["po_free"] = []
        M["pb_free"] = [[] for _ in range(4)]
        M["bufK_free"] = []
        M["bufV_free"] = []
        M["xlnT_free"] = []
        M["xres_w"] = []
        M["wA_extra"] = []
        M["wB_extra"] = []
        M["wC_extra"] = []
        wa_idx = [0, 0]
        wb_idx = [0, 0]
        wc_idx = [0, 0]

        def wt(eng, mks):
            for s, n in mks:
                if n > 0:
                    eng.wait_ge(s.h, n)

        def ln_phase():
            ag = []

            def _(v):
                v.wait_ge(sIn.h, M["consts"])
                wt(v, M["bufV_free"])
                wt(v, M["xres_w"])
                for j in range(TC):
                    for gq in range(4):
                        sDV.inc(v.bn_stats(stats[:, 6 * gq:6 * gq + 6],
                                           xres[:, j * H + 512 * gq:j * H + 512 * (gq + 1)]))
                    v.wait_ge(sDV.h, sDV.n)
                    sDV.inc(v.bn_aggr(mvall[:, 2 * j:2 * j + 2], stats[:, :]))
                    v.wait_ge(sDV.h, sDV.n)
                    ag.append(sDV.n)
                    sDV.inc(v.tensor_scalar_mul(negmean[:, j:j + 1],
                                                mvall[:, 2 * j:2 * j + 1], -1.0))
            block.vector(_)
            neg_mark = sDV.n

            sq = []

            def _(s):
                s.wait_ge(sIn.h, M["consts"])
                for j in range(TC):
                    s.wait_ge(sDV.h, ag[j])
                    sAC.inc(s.activation(sdall[:, j:j + 1], mvall[:, 2 * j + 1:2 * j + 2],
                                         AF.Sqrt, bias=epsln[:, :]))
                    sq.append(sAC.n)
            block.scalar(_)

            xm = []

            def _(v):
                v.wait_ge(sDV.h, neg_mark)
                for j in range(TC):
                    v.wait_ge(sAC.h, sq[j])
                    sDV.inc(v.reciprocal(rstd[:, j:j + 1], sdall[:, j:j + 1]))
                    v.wait_ge(sDV.h, sDV.n)
                for j in range(TC):
                    sDV.inc(v.tensor_scalar(xln_tm[j][:, :], xres[:, j * H:(j + 1) * H],
                                            negmean[:, j:j + 1], rstd[:, j:j + 1],
                                            ALU.add, ALU.mult))
                    xm.append(sDV.n)
            block.vector(_)

            mark = pe_transpose_to_xlnT(lambda j, hc: xln_tm[j][:, hc * 128:(hc + 1) * 128],
                                        xm)
            M["bufV_free"] = [(sPE, sPE.n)]
            return mark

        def pe_transpose_to_xlnT(src, jm):
            tr_m = {}
            cp_pred = [sDV.n + s + 1 for s in range(32)]

            def _(t):
                wt(t, M["pu_free"])
                for j in range(TC):
                    t.wait_ge(sDV.h, jm[j])
                    for g in range(4):
                        step = j * 4 + g
                        if step >= 2:
                            t.wait_ge(sDV.h, cp_pred[step - 2])
                        off = (step % 2) * 512
                        for i in range(4):
                            hc = g * 4 + i
                            ins = t.matmul(pu[:, off + i * 128:off + (i + 1) * 128],
                                           src(j, hc), id_bf[:, :],
                                           start=True, stop=True)
                        sPE.inc(ins)
                        tr_m[step] = sPE.n
            block.tensor(_)

            cp = []

            def _(v):
                wt(v, M["xlnT_free"])
                for step in range(32):
                    v.wait_ge(sPE.h, tr_m[step])
                    j, g = divmod(step, 4)
                    off = (step % 2) * 512
                    dst = xlnT[:, :].rearrange("p (hc t) -> p hc t", t=T)[
                        :, g * 4:(g + 1) * 4, j * 128:(j + 1) * 128]
                    sDV.inc(v.tensor_copy(
                        dst, pu[:, off:off + 512].rearrange("p (i t) -> p i t", t=128)))
                    cp.append(sDV.n)
            block.vector(_)
            assert cp == cp_pred
            M["pu_free"] = [(sDV, sDV.n)]
            return sDV.n

        def ffn_phase(xlnT_mark):
            wA_marks = {}
            wB_marks = {}
            wC_marks = {}

            def _(sy):
                wA_ex, M["wA_extra"] = M["wA_extra"], []
                wB_ex, M["wB_extra"] = M["wB_extra"], []
                wC_ex, M["wC_extra"] = M["wC_extra"], []
                wa_idx[1] = wa_idx[0]
                wb_idx[1] = wb_idx[0]
                wc_idx[1] = wc_idx[0]
                for ip, path in enumerate(PATHS):
                    for f in range(FCN):
                        if path == "sh":
                            src1 = ws1_sw[f * 128:(f + 1) * 128, :]
                            src3 = ws3_sw[f * 128:(f + 1) * 128, :]
                            cm = max(cast["ws1"], cast["ws3"])
                        else:
                            base = path * FF
                            src1 = we1_sw[base + f * 128:base + (f + 1) * 128, :]
                            src3 = we3_sw[base + f * 128:base + (f + 1) * 128, :]
                            cm = max(cast[("we1", path)], cast[("we3", path)])
                        sy.wait_ge(sC.h, cm)
                        ia = wa_idx[0]
                        if ia >= 2:
                            sy.wait_ge(sWa.h, ia - 1)
                        if wa_idx[0] - wa_idx[1] < 2:
                            wt(sy, wA_ex)
                        sTa[ia % 2].inc(sy.dma_start(
                            out=wbufA[:, (ia % 2) * H:(ia % 2 + 1) * H],
                            in_=src1), 16)
                        wA_marks[(ip, f)] = (sTa[ia % 2], sTa[ia % 2].n)
                        wa_idx[0] += 1
                        ib = wb_idx[0]
                        if ib >= 2:
                            sy.wait_ge(sWb.h, ib - 1)
                        if wb_idx[0] - wb_idx[1] < 2:
                            wt(sy, wB_ex)
                        sTb[ib % 2].inc(sy.dma_start(
                            out=wbufB[:, (ib % 2) * H:(ib % 2 + 1) * H],
                            in_=src3), 16)
                        wB_marks[(ip, f)] = (sTb[ib % 2], sTb[ib % 2].n)
                        wb_idx[0] += 1
                    w2src = ws2_bf if path == "sh" else we2_bf[path * 512:(path + 1) * 512, :]
                    cm = cast["ws2"] if path == "sh" else cast[("we2", path)]
                    sy.wait_ge(sC.h, cm)
                    for Q in range(4):
                        ic = wc_idx[0]
                        if ic >= 2:
                            sy.wait_ge(sWc.h, ic - 1)
                        if wc_idx[0] - wc_idx[1] < 2:
                            wt(sy, wC_ex)
                        reg = (ic % 2) * 5632
                        sTc[ic % 2].inc(sy.dma_start(
                            out=wbufC[:, reg:reg + 5632],
                            in_=w2src[Q * 128:(Q + 1) * 128, :]), 16)
                        wC_marks[(ip, Q)] = (sTc[ic % 2], sTc[ic % 2].n)
                        wc_idx[0] += 1
            block.sync(_)

            gate_pe = []
            sig_pred = [sAC.n + j + 1 for j in range(TC)]

            def _(t):
                t.wait_ge(sDV.h, xlnT_mark)
                t.wait_ge(sC.h, cast["gate"])
                wt(t, M["pu_free"])
                for j in range(TC):
                    if j >= 2:
                        t.wait_ge(sAC.h, sig_pred[j - 2])
                    dst = pu[:, 0:NE] if j % 2 == 0 else pu[:, 512:512 + NE]
                    for hc in range(HCN):
                        ins = t.matmul(dst, xlnT[:, hc * T + j * 128:hc * T + (j + 1) * 128],
                                       gateT[:, hc * NE:(hc + 1) * NE],
                                       start=(hc == 0), stop=(hc == HCN - 1))
                    sPE.inc(ins)
                    gate_pe.append(sPE.n)
            block.tensor(_)

            sig = []
            gv_pred = [sDV.n + 20 * (j + 1) for j in range(TC)]

            def _(s):
                for j in range(TC):
                    s.wait_ge(sPE.h, gate_pe[j])
                    if j >= 1:
                        s.wait_ge(sDV.h, gv_pred[j - 1])
                    src = pu[:, 0:NE] if j % 2 == 0 else pu[:, 512:512 + NE]
                    sAC.inc(s.activation(scores[:, :], src, AF.Sigmoid))
                    sig.append(sAC.n)
            block.scalar(_)
            assert sig == sig_pred

            gv = []

            def _(v):
                for j in range(TC):
                    v.wait_ge(sAC.h, sig[j])

                    def stp(ins):
                        sDV.inc(ins)
                        v.wait_ge(sDV.h, sDV.n)
                    stp(v.tensor_add(routing[:, :], scores[:, :], biasb[:, :]))
                    stp(v.reduce_max(idx1[:, :], routing[:, :], axis=AX))
                    stp(v.tensor_single_scalar(eqt[:, :], routing[:, :], idx1[:, :], ALU.is_equal))
                    stp(v.scalar_tensor_tensor(cand[:, :], eqt[:, :], -BIG, iopb_sb[:, :],
                                               ALU.mult, ALU.add))
                    stp(v.tensor_reduce(idx1[:, :], cand[:, :], axis=AX, op=ALU.min))
                    stp(v.tensor_single_scalar(oh1[:, :], iota_sb[:, :], idx1[:, :], ALU.is_equal))
                    stp(v.scalar_tensor_tensor(r2t[:, :], oh1[:, :], -BIG, routing[:, :],
                                               ALU.mult, ALU.add))
                    stp(v.reduce_max(idx1[:, :], r2t[:, :], axis=AX))
                    stp(v.tensor_single_scalar(eqt[:, :], r2t[:, :], idx1[:, :], ALU.is_equal))
                    stp(v.scalar_tensor_tensor(cand[:, :], eqt[:, :], -BIG, iopb_sb[:, :],
                                               ALU.mult, ALU.add))
                    stp(v.tensor_reduce(idx1[:, :], cand[:, :], axis=AX, op=ALU.min))
                    stp(v.tensor_single_scalar(oh2[:, :], iota_sb[:, :], idx1[:, :], ALU.is_equal))
                    stp(v.scalar_tensor_tensor(junk8[:, :], scores[:, :], 1.0, oh1[:, :],
                                               ALU.mult, ALU.mult, accum_out=sel1[:, :]))
                    stp(v.scalar_tensor_tensor(junk8[:, :], scores[:, :], 1.0, oh2[:, :],
                                               ALU.mult, ALU.mult, accum_out=sel2[:, :]))
                    stp(v.tensor_add(gsum[:, :], sel1[:, :], sel2[:, :]))
                    stp(v.reciprocal(ginv[:, :], gsum[:, :]))
                    stp(v.tensor_mul(g1[:, :], sel1[:, :], ginv[:, :]))
                    stp(v.tensor_mul(g2[:, :], sel2[:, :], ginv[:, :]))
                    stp(v.tensor_scalar(junk8[:, :], oh1[:, :], g1[:, :], None, ALU.mult))
                    stp(v.scalar_tensor_tensor(c_all[:, j * NE:(j + 1) * NE], oh2[:, :],
                                               g2[:, :], junk8[:, :], ALU.mult, ALU.add))
                    gv.append(sDV.n)
            block.vector(_)
            assert gv == gv_pred

            ctp = []
            ctc_pred = [sDV.n + j + 1 for j in range(TC)]

            def _(t):
                t.wait_ge(sAC.h, sig[-1])
                for j in range(TC):
                    t.wait_ge(sDV.h, gv[j])
                    if j >= 1:
                        t.wait_ge(sDV.h, ctc_pred[j - 1])
                    sPE.inc(t.transpose(pu[0:NE, 0:128], c_all[:, j * NE:(j + 1) * NE],
                                        id_sb[:, :]))
                    ctp.append(sPE.n)
            block.tensor(_)

            ctc = []

            def _(v):
                for j in range(TC):
                    v.wait_ge(sPE.h, ctp[j])
                    sDV.inc(v.tensor_copy(cT[:, j * 128:(j + 1) * 128], pu[0:NE, 0:128]))
                    ctc.append(sDV.n)
            block.vector(_)
            assert ctc == ctc_pred
            cT_mark = sDV.n

            pu_reader = cT_mark
            out_done = 0
            bufK_war = list(M["bufK_free"])
            prev_tail = [None] * 4
            prev_evac_base = [0]
            for ip, path in enumerate(PATHS):
                is_exp = path != "sh"
                if is_exp:
                    def _(t, e=path, pr=pu_reader):
                        t.wait_ge(sDV.h, max(cT_mark, pr))
                        for half in range(2):
                            sPE.inc(t.matmul(pu[:, half * 512:(half + 1) * 512],
                                             oh_sb[:, e * 128:(e + 1) * 128],
                                             cT[:, half * 512:(half + 1) * 512],
                                             start=True, stop=True))
                    block.tensor(_)
                    cbc_mark = sPE.n

                silu_m = {}
                mul2_m = {}
                h1_m = {}
                h3_m = {}
                for f in range(FCN):
                    def _(t, f=f, ip=ip):
                        wt(t, [wA_marks[(ip, f)], wB_marks[(ip, f)]])
                        slotA = wbufA[:, ((wa_seq[0] + f) % 2) * H:((wa_seq[0] + f) % 2 + 1) * H]
                        slotB = wbufB[:, ((wb_seq[0] + f) % 2) * H:((wb_seq[0] + f) % 2 + 1) * H]
                        for half in range(2):
                            mk = silu_m.get((f - 1, half))
                            if mk is not None:
                                t.wait_ge(sAC.h, mk)
                            else:
                                wt(t, M["pb_free"][half])
                            for hc in range(HCN):
                                ins = t.matmul(ph1[:, half * 512:(half + 1) * 512],
                                               slotA[:, hc * 128:(hc + 1) * 128],
                                               xlnT[:, hc * T + half * 512:hc * T + (half + 1) * 512],
                                               start=(hc == 0), stop=(hc == HCN - 1))
                            sPE.inc(ins)
                            h1_m[(f, half)] = sPE.n
                        t.wait_ge(sPE.h, h1_m[(f, 1)])
                        sWa.inc(t.nop())
                        for half in range(2):
                            mk = mul2_m.get((f - 1, half))
                            if mk is not None:
                                t.wait_ge(sDV.h, mk)
                            else:
                                wt(t, M["pb_free"][2 + half])
                            for hc in range(HCN):
                                ins = t.matmul(ph3[:, half * 512:(half + 1) * 512],
                                               slotB[:, hc * 128:(hc + 1) * 128],
                                               xlnT[:, hc * T + half * 512:hc * T + (half + 1) * 512],
                                               start=(hc == 0), stop=(hc == HCN - 1))
                            sPE.inc(ins)
                            h3_m[(f, half)] = sPE.n
                        t.wait_ge(sPE.h, h3_m[(f, 1)])
                        sWb.inc(t.nop())
                    block.tensor(_)

                    def _(s, f=f):
                        for half in range(2):
                            s.wait_ge(sPE.h, h1_m[(f, half)])
                            mk = mul2_m.get((f - 1, half))
                            if mk is not None:
                                s.wait_ge(sDV.h, mk)
                            sAC.inc(s.activation(tmp_silu[:, half * 512:(half + 1) * 512],
                                                 ph1[:, half * 512:(half + 1) * 512], AF.Silu))
                            silu_m[(f, half)] = sAC.n
                    block.scalar(_)

                    def _(v, f=f, is_exp=is_exp):
                        for half in range(2):
                            v.wait_ge(sAC.h, silu_m[(f, half)])
                            v.wait_ge(sPE.h, h3_m[(f, half)])
                            if f == 0 and half == 0:
                                if out_done:
                                    v.wait_ge(sPE.h, out_done)
                                else:
                                    wt(v, bufK_war)
                                if is_exp:
                                    v.wait_ge(sPE.h, cbc_mark)
                            ts = tmp_silu[:, half * 512:(half + 1) * 512]
                            if is_exp:
                                sDV.inc(v.tensor_mul(ts, ts, pu[:, half * 512:(half + 1) * 512]))
                                v.wait_ge(sDV.h, sDV.n)
                            sDV.inc(v.tensor_mul(
                                bufK[:, f * T + half * 512:f * T + (half + 1) * 512],
                                ts, ph3[:, half * 512:(half + 1) * 512]))
                            mul2_m[(f, half)] = sDV.n
                    block.vector(_)
                if DEBUG_FLUSH_ACT:
                    def _(v):
                        v.wait_ge(sDV.h, sDV.n)
                        for rep in range(3):
                            sDV.inc(v.tensor_copy(junk8[:, :], bufK[0:128, 0:2 * NE].bitcast(F32)))
                            v.wait_ge(sDV.h, sDV.n)
                    block.vector(_)
                wa_seq[0] += FCN
                wb_seq[0] += FCN
                pu_reader = sDV.n
                act_ready = sDV.n
                M["pb_free"] = [[(sAC, silu_m[(FCN - 1, 0)])], [(sAC, silu_m[(FCN - 1, 1)])],
                                [(sDV, mul2_m[(FCN - 1, 0)])], [(sDV, mul2_m[(FCN - 1, 1)])]]

                tile_m = {}
                evac_m = {}
                for Q in range(4):
                    for jg in range(2):
                        def _(t, Q=Q, jg=jg, ip=ip, act_ready=act_ready):
                            wt(t, [wC_marks[(ip, Q)]])
                            if Q == 0 and jg == 0:
                                t.wait_ge(sDV.h, act_ready)
                                if ip == 0:
                                    wt(t, M["po_free"] + M["pu_free"])
                            reg = ((wc_seq[0] + Q) % 2) * 5632
                            for j in range(jg * 4, jg * 4 + 4):
                                tl = Q * 8 + j
                                bank = tl % 4
                                if DEBUG_SERIAL_OUT and (tl > 0 or prev_evac_base[0]):
                                    t.wait_ge(sDV.h, (act_ready + tl) if tl > 0 else prev_evac_base[0])
                                if tl >= 4:
                                    t.wait_ge(sDV.h, evac_m[tl - 4])
                                elif prev_tail[bank] is not None:
                                    t.wait_ge(sDV.h, prev_tail[bank])
                                dst = (po if bank < 2 else pu)[:, (bank % 2) * 512:(bank % 2) * 512 + 512]
                                for f in range(FCN):
                                    ins = t.matmul(dst,
                                                   bufK[:, f * T + j * 128:f * T + (j + 1) * 128],
                                                   wbufC[:, reg + f * 512:reg + (f + 1) * 512],
                                                   start=(f == 0), stop=(f == FCN - 1))
                                sPE.inc(ins)
                                tile_m[tl] = sPE.n
                            if jg == 1:
                                t.wait_ge(sPE.h, tile_m[Q * 8 + 7])
                                sWc.inc(t.nop())
                        block.tensor(_)

                        def _(v, Q=Q, jg=jg):
                            for j in range(jg * 4, jg * 4 + 4):
                                tl = Q * 8 + j
                                bank = tl % 4
                                v.wait_ge(sPE.h, tile_m[tl])
                                src = (po if bank < 2 else pu)[:, (bank % 2) * 512:(bank % 2) * 512 + 512]
                                dst = xres[:, j * H + Q * 512:j * H + (Q + 1) * 512]
                                sDV.inc(v.scalar_tensor_tensor(dst, src, 0.5, dst,
                                                               ALU.mult, ALU.add))
                                evac_m[tl] = sDV.n
                        block.vector(_)
                wc_seq[0] += 4
                out_done = sPE.n
                prev_evac_base[0] = evac_m[31]
                prev_tail = [evac_m[28], evac_m[29], evac_m[30], evac_m[31]]
                pu_reader = sDV.n
                M["po_free"] = [(sDV, sDV.n)]
                M["pu_free"] = [(sDV, sDV.n)]

            M["bufK_free"] = [(sPE, sPE.n)]
            M["xlnT_free"] = [(sPE, sPE.n)]
            M["xres_w"] = [(sDV, sDV.n)]
            M["bufV_free"] = [(sDV, sDV.n), (sPE, sPE.n)]
            return sDV.n

        wa_seq = [0]
        wb_seq = [0]
        wc_seq = [0]

        def attn_phase(xlnT_mark):
            qkv_marks = {}

            def emit_slabs(fb_list):
                def _(sy):
                    wA_ex, M["wA_extra"] = M["wA_extra"], []
                    for Fb in fb_list:
                        sy.wait_ge(sC.h, cast[("wqkvb", Fb // 8)])
                        ia = wa_idx[0]
                        if ia >= 2:
                            sy.wait_ge(sWa.h, ia - 1)
                        if wa_idx[0] - wa_idx[1] < 2:
                            wt(sy, wA_ex)
                        sTa[ia % 2].inc(sy.dma_start(
                            out=wbufA[:, (ia % 2) * H:(ia % 2 + 1) * H],
                            in_=wqkv_sw[Fb * 128:(Fb + 1) * 128, :]), 16)
                        qkv_marks[Fb] = (sTa[ia % 2], sTa[ia % 2].n)
                        wa_idx[0] += 1
                block.sync(_)
            wa_idx[1] = wa_idx[0]
            emit_slabs(list(range(16, 48)))

            def _(g):
                g.wait_ge(sWb.h, wb_seq[0])
                sC.inc(g.dma_start(out=bprojT, in_=b_proj.rearrange("h -> () h")), 16)
                cast["bproj"] = sC.n
            block.gpsimd(_)

            evac_kv = {}
            tile_kv = {}
            kv_tr = {}
            kv_cp = {}
            kv_stage_init = [0]

            for k, Fb in enumerate(range(16, 48)):
                is_k = Fb < 32

                def _(t, k=k, Fb=Fb):
                    if k == 0:
                        t.wait_ge(sDV.h, xlnT_mark)
                    wt(t, [qkv_marks[Fb]])
                    slot = wbufA[:, ((wa_seq[0] + k) % 2) * H:((wa_seq[0] + k) % 2 + 1) * H]
                    for th in range(2):
                        gt = 2 * k + th
                        if gt >= 4:
                            s_, n_ = evac_kv[gt - 4]
                            t.wait_ge(s_.h, n_)
                        else:
                            wt(t, M["pb_free"][gt])
                        for hc in range(HCN):
                            ins = t.matmul(pbank[gt % 4],
                                           slot[:, hc * 128:(hc + 1) * 128],
                                           xlnT[:, hc * T + th * 512:hc * T + (th + 1) * 512],
                                           start=(hc == 0), stop=(hc == HCN - 1))
                        sPE.inc(ins)
                        tile_kv[gt] = sPE.n
                    t.wait_ge(sPE.h, tile_kv[2 * k + 1])
                    sWa.inc(t.nop())
                block.tensor(_)

                def _(e, k=k, Fb=Fb, is_k=is_k):
                    for th in range(2):
                        gt = 2 * k + th
                        e.wait_ge(sPE.h, tile_kv[gt])
                        if k >= 3:
                            e.wait_ge(sPE.h, kv_tr[2 * (k - 3) + 1])
                        elif kv_stage_init[0] == 0:
                            e.wait_ge(sWc.h, wc_seq[0])
                        dst = kv_stage[k % 3][:, th * 512:(th + 1) * 512]
                        if is_k:
                            ins = e.activation(dst, pbank[gt % 4], AF.Relu)
                            evac_kv[gt] = (sAC, sAC.inc(ins))
                        else:
                            ins = e.tensor_copy(dst, pbank[gt % 4])
                            evac_kv[gt] = (sDV, sDV.inc(ins))
                    kv_stage_init[0] = 1
                (block.scalar if is_k else block.vector)(_)

                def _(t, k=k, Fb=Fb):
                    for th in range(2):
                        step = 2 * k + th
                        s_, n_ = evac_kv[step]
                        t.wait_ge(s_.h, n_)
                        if step >= 2:
                            t.wait_ge(sDV.h, kv_cp[step - 2])
                        else:
                            wt(t, M["po_free"])
                        off = (step % 2) * 512
                        for i in range(4):
                            ins = t.matmul(
                                po[:, off + i * 128:off + (i + 1) * 128],
                                kv_stage[k % 3][:, th * 512 + i * 128:th * 512 + (i + 1) * 128],
                                id_bf[:, :], start=True, stop=True)
                        sPE.inc(ins)
                        kv_tr[step] = sPE.n
                block.tensor(_)

                def _(v, k=k, Fb=Fb, is_k=is_k):
                    dstbuf = bufK if is_k else bufV
                    fc = (Fb - 16) if is_k else (Fb - 32)
                    for th in range(2):
                        step = 2 * k + th
                        v.wait_ge(sPE.h, kv_tr[step])
                        if is_k and step == 0:
                            wt(v, M["bufK_free"])
                        if (not is_k) and k == 16 and th == 0:
                            wt(v, M["bufV_free"])
                        off = (step % 2) * 512
                        dst = dstbuf[:, :].rearrange("p (j h) -> p j h", h=H)[
                            :, th * 4:(th + 1) * 4, fc * 128:(fc + 1) * 128]
                        sDV.inc(v.tensor_copy(
                            dst, po[:, off:off + 512].rearrange("p (i t) -> p i t", t=128)))
                        kv_cp[step] = sDV.n
                block.vector(_)
            wa_seq[0] += 32
            M["pb_free"] = [[evac_kv[60]], [evac_kv[61]], [evac_kv[62]], [evac_kv[63]]]
            M["po_free"] = [(sDV, kv_cp[63])]

            if ATTN_UPTO < 2:
                return sDV.n
            if ATTN_UPTO >= 4:
                emit_slabs(list(range(16)))

            def _(t):
                t.wait_ge(sDV.h, kv_cp[63])
                wt(t, M["pu_free"])
                for bank in range(2):
                    first = True
                    for g in range(8):
                        hgrp = bank * 8 + g
                        for q in range(4):
                            hd = 4 * hgrp + q
                            col0 = bank * 512 + g * 33
                            for j in range(TC):
                                t.matmul(pu[32 * q:32 * q + 32, col0:col0 + 32],
                                         bufK[:, j * H + hd * 32:j * H + (hd + 1) * 32],
                                         bufV[:, j * H + hd * 32:j * H + (hd + 1) * 32],
                                         start=first, stop=False,
                                         tile_position=(0, 32 * q))
                                first = False
                                last = (g == 7 and q == 3 and j == TC - 1)
                                ins = t.matmul(pu[32 * q:32 * q + 32, col0 + 32:col0 + 33],
                                               bufK[:, j * H + hd * 32:j * H + (hd + 1) * 32],
                                               oc_bf[:, :],
                                               start=False, stop=last,
                                               tile_position=(0, 32 * q))
                    sPE.inc(ins)
                for col in (500, 1020):
                    sPE.inc(t.matmul(pu[:, col:col + 1], id_bf[:, :], oc_bf[:, :],
                                     start=True, stop=True))
            block.tensor(_)
            vkT_mm_mark = sPE.n

            if ATTN_UPTO < 3:
                return sDV.n
            def _(v):
                v.wait_ge(sPE.h, vkT_mm_mark)
                sDV.inc(v.tensor_copy(vkT_sb[:, 0:264], pu[:, 0:264]))
                sDV.inc(v.tensor_copy(vkT_sb[:, 264:528], pu[:, 512:776]))
            block.vector(_)
            vkT_ev_mark = sDV.n
            M["pu_free"] = [(sDV, sDV.n)]

            def _(g):
                g.wait_ge(sDV.h, vkT_ev_mark)
                sB.inc(g.dma_start(out=cc_in.ap(), in_=vkT_sb[:, :]), 16)
                g.wait_ge(sB.h, sB.n)
                sCC.inc(g.collective_compute(
                    "AllReduce", ALU.add,
                    replica_groups=[[0, 1, 2, 3], [4, 5, 6, 7]],
                    ins=[cc_in.ap().opt()], outs=[cc_out.ap().opt()]))
                g.wait_ge(sCC.h, sCC.n)
                sB.inc(g.dma_start(out=vkT_sb[:, :], in_=cc_out.ap()), 16)
            block.gpsimd(_)
            cc_mark = sB.n

            def _(v):
                v.wait_ge(sB.h, cc_mark)
                sDV.inc(v.tensor_copy(vkT_bf[:, :], vkT_sb[:, :]))
            block.vector(_)
            vkT_bf_mark = sDV.n

            if ATTN_UPTO < 4:
                return sDV.n
            tile_q = {}
            qev = []

            for fb in range(16):
                def _(t, fb=fb):
                    wt(t, [qkv_marks[fb]])
                    slot = wbufA[:, ((wa_seq[0] + fb) % 2) * H:((wa_seq[0] + fb) % 2 + 1) * H]
                    for th in range(2):
                        gt = 2 * fb + th
                        if gt >= 4:
                            s_, n_ = qev[gt - 4]
                            t.wait_ge(s_.h, n_)
                        else:
                            wt(t, M["pb_free"][gt])
                        for hc in range(HCN):
                            ins = t.matmul(pbank[gt % 4],
                                           slot[:, hc * 128:(hc + 1) * 128],
                                           xlnT[:, hc * T + th * 512:hc * T + (th + 1) * 512],
                                           start=(hc == 0), stop=(hc == HCN - 1))
                        sPE.inc(ins)
                        tile_q[gt] = sPE.n
                    t.wait_ge(sPE.h, tile_q[2 * fb + 1])
                    sWa.inc(t.nop())
                block.tensor(_)

                def _(s, fb=fb):
                    for th in range(2):
                        gt = 2 * fb + th
                        s.wait_ge(sPE.h, tile_q[gt])
                        if gt == 0:
                            s.wait_ge(sPE.h, vkT_mm_mark)
                        ins = s.activation(bufK[:, fb * T + th * 512:fb * T + (th + 1) * 512],
                                           pbank[gt % 4], AF.Relu)
                        qev.append((sAC, sAC.inc(ins)))
                block.scalar(_)
            wa_seq[0] += 16
            q_done = sAC.n
            M["pb_free"] = [[qev[28]], [qev[29]], [qev[30]], [qev[31]]]

            if ATTN_UPTO < 5:
                return sDV.n
            epi_m = {}
            otile_m = {}

            for j in range(TC):
                for tp in range(4):
                    def _(t, j=j, tp=tp):
                        if j == 0 and tp == 0:
                            t.wait_ge(sDV.h, vkT_bf_mark)
                            t.wait_ge(sAC.h, q_done)
                            wt(t, M["po_free"])
                        for tt in range(2 * tp, 2 * tp + 2):
                            gi = j * 8 + tt
                            q4, sh = tt % 4, tt // 4
                            if gi >= 2:
                                mk = epi_m.get(gi - 2)
                                if mk is not None:
                                    t.wait_ge(sDV.h, mk)
                            off = (gi % 2) * 512
                            for i in range(8):
                                hgrp = 8 * sh + i
                                ins = t.matmul(
                                    po[:, off + i * 33:off + (i + 1) * 33],
                                    bufK[32 * q4:32 * q4 + 32,
                                         hgrp * T + j * 128:hgrp * T + (j + 1) * 128],
                                    vkT_bf[32 * q4:32 * q4 + 32, hgrp * 33:(hgrp + 1) * 33],
                                    start=(i == 0), stop=(i == 7),
                                    tile_position=(32 * q4, 0))
                            ins = t.matmul(po[:, off + 490:off + 491],
                                           id_bf[:, :], oc_bf[:, :],
                                           start=True, stop=True)
                            sPE.inc(ins)
                            otile_m[gi] = sPE.n
                    block.tensor(_)

                    def _(v, j=j, tp=tp):
                        if ATTN_UPTO == 41:
                            for tt in range(2 * tp, 2 * tp + 2):
                                gi = j * 8 + tt
                                v.wait_ge(sPE.h, otile_m[gi])
                                sDV.inc(v.tensor_copy(dnm[:, :], po[0:128, 0:NE]))
                                epi_m[gi] = sDV.n
                            return
                        for tt in range(2 * tp, 2 * tp + 2):
                            gi = j * 8 + tt
                            q4, sh = tt % 4, tt // 4
                            v.wait_ge(sPE.h, otile_m[gi])
                            off = (gi % 2) * 512
                            p3 = po[:, off:off + 264].rearrange("p (h e) -> p h e", e=33)
                            sDV.inc(v.tensor_single_scalar(dnm[:, :], p3[:, :, 32:33].rearrange("p h e -> p (h e)"),
                                                           EPS_AT, ALU.add))
                            v.wait_ge(sDV.h, sDV.n)
                            sDV.inc(v.reciprocal(rcp[:, :], dnm[:, :]))
                            v.wait_ge(sDV.h, sDV.n)
                            base = j * H + 1024 * sh
                            ov = bufV[:, base:base + 1024].rearrange("p (i r) -> p i r", r=128)
                            sDV.inc(v.tensor_tensor(
                                ov[:, :, 32 * q4:32 * q4 + 32],
                                p3[:, :, 0:32],
                                rcp[:, :].rearrange("p h -> p h ()").broadcast_to([128, NE, 32]),
                                ALU.mult))
                            epi_m[gi] = sDV.n
                    block.vector(_)

            M["po_free"] = [(sDV, sDV.n)]
            attn_out_done = sPE.n

            if ATTN_UPTO < 6:
                return sDV.n
            epi_j = [epi_m[j * 8 + 7] for j in range(TC)]
            afm_mark = pe_transpose_to_xlnT(
                lambda j, hc: bufV[:, j * H + hc * 128:j * H + (hc + 1) * 128], epi_j)

            pset = {}

            def _(sy):
                sy.wait_ge(sC.h, cast["wproj"])
                wc_idx[1] = wc_idx[0]
                for Q in range(4):
                    for jh in range(2):
                        for ch in range(2):
                            ic = wc_idx[0]
                            if ic >= 2:
                                sy.wait_ge(sWc.h, ic - 1)
                            reg = (ic % 2) * 5632
                            sTc[ic % 2].inc(sy.dma_start(
                                out=wbufC[:, reg:reg + 4096],
                                in_=wproj_bf[(Q * 2 + ch) * 128:(Q * 2 + ch + 1) * 128, :]), 16)
                            pset[(Q, jh, ch)] = (sTc[ic % 2], sTc[ic % 2].n)
                            wc_idx[0] += 1
            block.sync(_)

            pj_evac = {}
            for Q in range(4):
                for jh in range(2):
                    ptile = {}

                    def _(t, Q=Q, jh=jh):
                        wt(t, [pset[(Q, jh, 0)]])
                        if Q == 0 and jh == 0:
                            t.wait_ge(sDV.h, afm_mark)
                            t.wait_ge(sC.h, cast["bproj"])
                            wt(t, M["pb_free"][0] + M["pb_free"][1]
                               + M["pb_free"][2] + M["pb_free"][3])
                        else:
                            prev = (Q, jh - 1) if jh else (Q - 1, 1)
                            for jj in range(4):
                                t.wait_ge(sDV.h, pj_evac[(prev[0], prev[1], jj)])
                        for ch in range(2):
                            if ch == 1:
                                wt(t, [pset[(Q, jh, 1)]])
                            reg = ((wc_seq[0] + 2 * (2 * Q + jh) + ch) % 2) * 5632
                            for jj in range(4):
                                j = jh * 4 + jj
                                for cc in range(8):
                                    c0 = 8 * ch + cc
                                    ins = t.matmul(
                                        pbank[jj],
                                        xlnT[:, c0 * T + j * 128:c0 * T + (j + 1) * 128],
                                        wbufC[:, reg + cc * 512:reg + (cc + 1) * 512],
                                        start=(ch == 0 and cc == 0), stop=False)
                                if ch == 1:
                                    ins = t.matmul(pbank[jj], or_bf[:, :],
                                                   bprojT[0:1, Q * 512:(Q + 1) * 512],
                                                   start=False, stop=(cc == 7))
                                sPE.inc(ins)
                                ptile[jj] = sPE.n
                            if ch == 1:
                                t.wait_ge(sPE.h, ptile[3])
                                sWc.inc(t.nop(), 2)
                    block.tensor(_)

                    def _(v, Q=Q, jh=jh):
                        for jj in range(4):
                            j = jh * 4 + jj
                            v.wait_ge(sPE.h, ptile[jj])
                            dst = xres[:, j * H + Q * 512:j * H + (Q + 1) * 512]
                            sDV.inc(v.tensor_add(dst, dst, pbank[jj]))
                            pj_evac[(Q, jh, jj)] = sDV.n
                    block.vector(_)
            wc_seq[0] += 16
            proj_done_pe = sPE.n
            M["pb_free"] = [[(sDV, pj_evac[(3, 1, jj)])] for jj in range(4)]
            M["bufK_free"] = [(sPE, attn_out_done)]
            M["xlnT_free"] = [(sPE, proj_done_pe)]
            M["xres_w"] = [(sDV, sDV.n)]
            M["bufV_free"] = [(sPE, proj_done_pe)]
            M["wB_extra"] = [(sPE, proj_done_pe)]
            M["wC_extra"] = [(sPE, kv_tr[63])]
            return sDV.n

        m1 = ln_phase()
        ffn_phase(m1)
        if DEBUG_DUMP:
            def _(sy):
                wt(sy, M["xres_w"])
                wt(sy, M["bufK_free"])
                sT.inc(sy.dma_start(out=dbg_actK, in_=bufK[:, :]), 16)
                sT.inc(sy.dma_start(out=dbg_bufV, in_=bufV[:, :]), 16)
                sT.inc(sy.dma_start(out=dbg_call, in_=c_all[:, :]), 16)
                sT.inc(sy.dma_start(out=dbg_xlnT, in_=xlnT[:, :]), 16)
                sy.wait_ge(sT.h, sT.n)
            block.sync(_)
        if STAGES >= 2:
            m2 = ln_phase()
            attn_phase(m2)
        if STAGES >= 3:
            m3 = ln_phase()
            ffn_phase(m3)

        def _(sy):
            wt(sy, M["xres_w"])
            sT.inc(sy.dma_start(out=out.rearrange("(j p) h -> p j h", p=128),
                                in_=xres[:, :].rearrange("p (j h) -> p j h", h=H)), 16)
            sy.wait_ge(sT.h, sT.n)
        block.sync(_)

    return nc


_NC_CACHE = {}


def host_swz(w):
    R = w.shape[0]
    return np.ascontiguousarray(
        w.reshape(R // 128, 128, HCN, 128).transpose(0, 3, 2, 1).reshape(R, HCN * 128))


def host_w2(w):
    return np.ascontiguousarray(
        w.reshape(4, 512, FCN, 128).transpose(0, 3, 2, 1).reshape(512, FCN * 512))


def host_proj(w):
    return np.ascontiguousarray(
        w.reshape(4, 512, 2, 8, 128).transpose(0, 2, 4, 3, 1).reshape(1024, 4096))


def kernel(**inputs):
    x = np.asarray(inputs["L0_attn_x"])
    if "prog" not in _NC_CACHE:
        _NC_CACHE["prog"] = build_program()
    nc = _NC_CACHE["prog"]
    we2_np = np.asarray(inputs["L0_ffn_we2"])
    common = {
        "wqkv_s": host_swz(np.asarray(inputs["L0_attn_w_qkv"])),
        "w_proj": host_proj(np.asarray(inputs["L0_attn_w_proj"])),
        "b_proj": inputs["L0_attn_b_proj"],
        "g_bias": inputs["L0_ffn_bias"],
        "gate_wt": np.asarray(inputs["L0_ffn_gate_w"]).T,
        "ws1_s": host_swz(np.asarray(inputs["L0_ffn_ws1"])),
        "ws2": host_w2(np.asarray(inputs["L0_ffn_ws2"])),
        "ws3_s": host_swz(np.asarray(inputs["L0_ffn_ws3"])),
        "we1_s": host_swz(np.asarray(inputs["L0_ffn_we1"]).reshape(-1, H)),
        "we2": np.concatenate([host_w2(we2_np[e]) for e in range(NE)], axis=0),
        "we3_s": host_swz(np.asarray(inputs["L0_ffn_we3"]).reshape(-1, H)),
    }
    common = {k: np.ascontiguousarray(v) for k, v in common.items()}
    in_maps = []
    for c in range(8):
        m = dict(common)
        m["x"] = np.ascontiguousarray(x[c // 4, (c % 4) * T:(c % 4 + 1) * T, :])
        in_maps.append(m)
    res = run_bass_kernel_spmd(nc, in_maps, core_ids=list(range(8)))
    y = np.empty((2, 4096, H), np.float32)
    for c in range(8):
        y[c // 4, (c % 4) * T:(c % 4 + 1) * T, :] = res.results[c]["out"]
    return y
